# version 15
# baseline (speedup 1.0000x reference)
import numpy as np
import ml_dtypes
from contextlib import ExitStack

import concourse.bass as bass
import concourse.mybir as mybir
from concourse.bass_utils import run_bass_kernel_spmd

F32 = mybir.dt.float32
BF16 = mybir.dt.bfloat16
AF = mybir.ActivationFunctionType
ALU = mybir.AluOpType
AX = mybir.AxisListType

NB = 8
DBG_PHASE = 99
DBG_DUMP = False
USE_TNOP = True
USE_DRAIN = True
EPS = 1e-5
ESHIFT = -8.0
MASKVAL = -30000.0


class Ctr:
    def __init__(self, sem):
        self.sem = sem
        self.n = 0


class Sched:

    ENGINES = ("sync", "gpsimd", "vector", "scalar", "tensor")

    def __init__(self):
        self.q = {e: [] for e in self.ENGINES}
        self.relay = None

    def op(self, eng, builder, waits=(), inc=None, inck=1, vis=True):
        w = [(c, v) for (c, v) in waits if v > 0]
        if (eng == "tensor" and inc is not None and self.relay is not None
                and inc is self.relay[1]):
            priv, pub = self.relay
            priv.n += inck
            pv = priv.n
            self.q[eng].append((builder, w, priv, inck))
            pub.n += inck

            def relay_fn(e, pv=pv, inck=inck, pub=pub):
                e.wait_ge(priv.sem, pv)
                e.sem_inc(pub.sem, inck)
            self.q["gpsimd"].append((relay_fn, [], None, 0))
            return pub.n
        if inc is not None and vis and eng in ("vector", "scalar"):
            self.q[eng].append((builder, w, None, 0))
            inc.n += inck
            self.q[eng].append(((lambda e: e.drain()), [], inc, inck))
            return inc.n
        if inc is not None:
            inc.n += inck
        self.q[eng].append((builder, w, inc, inck))
        return None if inc is None else inc.n

    def emit(self, eng_name, eng):
        for builder, waits, inc, inck in self.q[eng_name]:
            for c, v in waits:
                eng.wait_ge(c.sem, v)
            inst = builder(eng)
            if inc is not None and inst is not None:
                inst.then_inc(inc.sem, inck)


def build_program(mk=None):
    factory = mk if mk is not None else bass.Bass
    nc = factory("TRN2", target_bir_lowering=False,
                 detect_race_conditions=False)

    x_h = nc.dram_tensor("x", [2304, 1024], F32, kind="ExternalInput")
    w_h = {n: nc.dram_tensor(n, [1024, 1024], F32, kind="ExternalInput")
           for n in ["wqt", "wkt", "wvt", "wot", "cbt"]}
    scb_h = nc.dram_tensor("scb", [128, 1024], F32, kind="ExternalInput")
    bias_h = nc.dram_tensor("bias", [128, 1024], F32, kind="ExternalInput")
    id_h = nc.dram_tensor("ident", [128, 128], BF16, kind="ExternalInput")
    mask_h = {n: nc.dram_tensor(n, [128, 512], BF16, kind="ExternalInput")
              for n in ["m_c0", "m_c2", "m_q0_c0", "m_q0_c1"]}
    out_h = nc.dram_tensor("out", [2048, 1024], F32, kind="ExternalOutput")
    dbg_h = {}
    if DBG_DUMP:
        for n, shp, dt in [("d_hln", [128, 2048], F32), ("d_hts", [128, 2048], F32),
                           ("d_qt", [128, 2048], F32), ("d_kt0", [128, 2048], F32),
                           ("d_vt0", [128, 2048], F32), ("d_kt2", [128, 2048], F32),
                           ("d_vt2", [128, 2048], F32),
                           ("d_pb0", [128, 2048], F32), ("d_pb1", [128, 2048], F32),
                           ("d_pb2", [128, 2048], F32), ("d_a0", [128, 1024], F32),
                           ("d_a1", [128, 1024], F32), ("d_at", [128, 2048], F32),
                           ("d_y1", [128, 2048], F32), ("d_g", [128, 2048], F32),
                           ("d_x2", [128, 2048], F32), ("d_sums", [128, 16], F32)]:
            dbg_h[n] = nc.dram_tensor(n, shp, dt, kind="ExternalOutput")

    ctx = ExitStack()
    with ctx:
        S = lambda *a: ctx.enter_context(nc.sbuf_tensor(*a))
        P = lambda *a: ctx.enter_context(nc.psum_tensor(*a))
        SEM = lambda n: Ctr(ctx.enter_context(nc.semaphore(n)))

        wsb = {n: S(n + "s", [128, 8, 1024], BF16) for n in w_h}
        xs = [S(f"xr{i}", [128, 2, 1024], F32) for i in range(2)]
        x2 = S("x2", [128, 2, 1024], F32)
        hln = S("hln", [128, 2, 1024], BF16)
        hln2 = S("hln2", [128, 2, 1024], BF16)
        hts = S("hts", [128, 8, 256], BF16)
        hts2 = S("hts2", [128, 8, 256], BF16)
        qt_t = S("qt", [128, 8, 256], BF16)
        kts = [S(f"kt{i}", [128, 8, 256], BF16) for i in range(3)]
        vts = [S(f"vt{i}", [128, 2, 1024], BF16) for i in range(3)]
        probs = [S(f"pb{i}", [128, 16, 128], BF16) for i in range(3)]
        a_t = [S(f"a{i}", [128, 1024], BF16) for i in range(2)]
        at_t = S("at", [128, 8, 256], BF16)
        y1sb = S("y1sb", [128, 2, 1024], F32)
        gsb = S("gsb", [128, 2, 1024], F32)
        outt = gsb
        sqf = S("sqf", [128, 1024], BF16)
        scbrep = S("screp", [128, 1024], F32)
        biasrep = S("birep", [128, 1024], F32)
        idt = S("idt", [128, 128], BF16)
        msk = {n: S("s" + n, [128, 512], BF16) for n in mask_h}
        ones = S("ones", [128, 1], BF16)
        csb = S("csb", [128, 4], F32)
        recip = S("recip", [128, 16], F32)
        st_sum = [S(f"tsum{i}", [128, 2], F32) for i in range(2)]
        st_sq = [S(f"tsq{i}", [128, 2], F32) for i in range(2)]
        st_mean = [S(f"tmean{i}", [128, 2], F32) for i in range(2)]
        st_m2 = [S(f"tm2{i}", [128, 2], F32) for i in range(2)]
        st_ve = [S(f"tve{i}", [128, 2], F32) for i in range(2)]
        st_rstd = [S(f"trstd{i}", [128, 2], F32) for i in range(2)]
        st_negm = [S(f"tnegm{i}", [128, 2], F32) for i in range(2)]
        st_mb = [S(f"tmb{i}", [128, 2], F32) for i in range(2)]

        nc.const_aps.register(F32, EPS, csb[:, 0:1])
        nc.const_aps.register(F32, -0.5, csb[:, 1:2])
        nc.const_aps.register(F32, 0.125, csb[:, 2:3])
        nc.const_aps.register(F32, ESHIFT, csb[:, 3:4])

        sc = [P(f"sc{i}", [128, 512], F32) for i in range(3)]
        pv = [P("pv0", [128, 512], F32), P("pv1", [128, 512], F32)]
        pj = [P("pj0", [128, 512], F32), P("pj1", [128, 512], F32),
              pv[0], pv[1]]
        pvl = P("pvl", [128, 16], F32)

        dx = [SEM(f"dx{i}") for i in range(2)]
        dw = {n: SEM("d" + n) for n in w_h}
        dsm = SEM("dsm")
        dO = SEM("dO")
        tS = SEM("tS")
        tP = SEM("tP")
        vS = SEM("vS")
        sS = SEM("sS")

        PH = 5 if DBG_PHASE in (51, 52) else DBG_PHASE
        sch = Sched()
        if USE_TNOP:
            sch.relay = (tP, tS)
        last_ve = {0: 0, 1: 0}
        sqf_free = [0]
        pj_gate = [[], [], [], []]
        sc_gate = [[], [], []]
        pvl_gate = []
        pj_i = 0
        first_tensor = [True]

        def _smalls(eng):
            dsm_sem = dsm.sem
            eng.dma_start(out=scbrep[:], in_=scb_h[:]).then_inc(dsm_sem, 16)
            eng.dma_start(out=biasrep[:], in_=bias_h[:]).then_inc(dsm_sem, 16)
            eng.dma_start(out=idt[:], in_=id_h[:]).then_inc(dsm_sem, 16)
            for n in mask_h:
                eng.dma_start(out=msk[n][:], in_=mask_h[n][:]).then_inc(dsm_sem, 16)
        sch.op("sync", _smalls)
        dsm.n = 16 * (3 + len(mask_h))
        SMALLS = dsm.n

        def _weights(eng):
            for n in ["wqt", "wkt", "wvt", "wot", "cbt"]:
                src = w_h[n][:].rearrange("(kc p) n -> p kc n", p=128)
                eng.dma_start(out=wsb[n][:], in_=src).then_inc(dw[n].sem, 16)
        sch.op("gpsimd", _weights)
        for n in dw:
            dw[n].n = 16


        def _vconsts(eng):
            eng.memset(csb[:, 0:1], EPS)
            eng.memset(csb[:, 1:2], -0.5)
            eng.memset(csb[:, 2:3], 0.125)
            eng.memset(csb[:, 3:4], ESHIFT)
            return eng.memset(ones[:], 1.0)
        sch.op("vector", _vconsts, inc=vS)
        V_CONST = vS.n
        sch.op("vector", lambda e: e.tensor_scalar(
            scbrep[:], scbrep[:], 1.0 / 127.0, None, ALU.mult),
            waits=[(dsm, SMALLS)], inc=vS)
        V_SCB = vS.n

        def ln_stats_apply(par, src, waits_v, waits_s, out_tile, out_is_bf,
                           out_guard=(), apply_vis=True):
            sch.op("scalar", lambda e: e.activation(
                sqf[:], src[:, 0, :], AF.Square),
                waits=list(waits_s) + [(vS, sqf_free[0])])
            sch.op("scalar", lambda e: e.drain(), inc=sS, vis=False)
            sq0 = sS.n
            for t in range(2):
                sch.op("vector", lambda e, t=t: e.reduce_sum(
                    st_sum[par][:, t:t + 1], src[:, t, :], AX.X),
                    waits=waits_v if t == 0 else ())
            sch.op("vector", lambda e, par=par: e.reduce_sum(
                st_sq[par][:, 0:1], sqf[:], AX.X),
                waits=[(sS, sq0), (vS, last_ve[par])], inc=vS, vis=False)
            rd0 = vS.n
            sch.op("scalar", lambda e: e.activation(
                sqf[:], src[:, 1, :], AF.Square), waits=[(vS, rd0)])
            sch.op("scalar", lambda e: e.drain(), inc=sS, vis=False)
            sq1 = sS.n
            sch.op("vector", lambda e, par=par: e.reduce_sum(
                st_sq[par][:, 1:2], sqf[:], AX.X), waits=[(sS, sq1)],
                inc=vS, vis=False)
            sqf_free[0] = vS.n
            sch.op("vector", lambda e: e.drain())
            sch.op("vector", lambda e: e.tensor_scalar(
                st_mean[par][:], st_sum[par][:], 1.0 / 1024.0, None, ALU.mult))
            sch.op("vector", lambda e: e.drain())
            sch.op("vector", lambda e: e.tensor_mul(
                st_m2[par][:], st_mean[par][:], st_mean[par][:]))
            sch.op("vector", lambda e: e.drain())
            sch.op("vector", lambda e: e.tensor_scalar(
                st_ve[par][:], st_sq[par][:], 1.0 / 1024.0, None, ALU.mult))
            sch.op("vector", lambda e: e.drain())
            sch.op("vector", lambda e: e.tensor_sub(
                st_ve[par][:], st_ve[par][:], st_m2[par][:]), inc=vS)
            ve_done = vS.n
            last_ve[par] = ve_done
            sch.op("scalar", lambda e: e.activation(
                st_m2[par][:], st_ve[par][:], AF.Ln, bias=EPS),
                waits=[(vS, ve_done), (vS, V_CONST)])
            sch.op("scalar", lambda e: e.activation(
                st_rstd[par][:], st_m2[par][:], AF.Exp, scale=-0.5))
            sch.op("scalar", lambda e: e.drain(), inc=sS, vis=False)
            rstd_done = sS.n
            if out_tile is None:
                return rstd_done, sq1
            av = []
            ot = out_tile if callable(out_tile) else (
                lambda t, o=out_tile: o[:, t, :])
            for t in range(2):
                sch.op("vector", lambda e, t=t: e.tensor_scalar(
                    ot(t), src[:, t, :], st_mean[par][:, t:t + 1],
                    None, ALU.subtract),
                    waits=([(sS, rstd_done)] + list(out_guard))
                    if t == 0 else ())
                sch.op("vector", lambda e, t=t: e.tensor_scalar(
                    ot(t), ot(t),
                    st_rstd[par][:, t:t + 1], None, ALU.mult), inc=vS,
                    vis=apply_vis)
                av.append(vS.n)
            return av, sq1

        def transposes(src_fn, dst, dst_col, groups, waits0):
            nonlocal pj_i
            for (s, fcs) in groups:
                b = pj_i % 4
                pj_i += 1
                gate = list(pj_gate[b])

                def fill(e, s=s, fcs=fcs, b=b):
                    last = None
                    for i, fc in enumerate(fcs):
                        last = e.matmul(pj[b][:, i * 128:(i + 1) * 128],
                                        src_fn(s, fc), idt[:],
                                        start=(i == 0), stop=(i == len(fcs) - 1))
                    return last
                w = gate + (waits0(s) if callable(waits0) else list(waits0))
                if first_tensor[0]:
                    w = w + [(dsm, SMALLS)]
                    first_tensor[0] = False
                sch.op("tensor", fill, waits=w, inc=tS)
                fill_done = tS.n
                c0 = dst_col(s)
                sch.op("scalar", lambda e, fcs=fcs, b=b, c0=c0: e.copy(
                    dst[:, fcs[0]:fcs[0] + len(fcs), c0:c0 + 128],
                    pj[b][:].rearrange("p (a n) -> p a n", a=len(fcs))),
                    waits=[(tS, fill_done)], inc=sS,
                    vis=((s, fcs) == groups[-1]))
                pj_gate[b] = [(sS, sS.n)]
            return sS.n

        def proj_fill(wname, nc0, rhs_ap, extra_waits, out_dst, vis=False):
            nonlocal pj_i
            b = pj_i % 4
            pj_i += 1
            gate = list(pj_gate[b])

            def fill(e, b=b):
                last = None
                for ni in range(2):
                    for kc in range(8):
                        lhs = wsb[wname][:, kc, (nc0 + ni) * 128:
                                         (nc0 + ni + 1) * 128]
                        rhs = rhs_ap(kc)
                        o = pj[b][:, ni * 256:(ni + 1) * 256]
                        last = e.matmul(o, lhs, rhs,
                                        start=(ni == 0 and kc == 0),
                                        stop=(ni == 1 and kc == 7))
                return last
            sch.op("tensor", fill, waits=gate + extra_waits, inc=tS)
            fd = tS.n
            sch.op("vector", lambda e, b=b: out_dst(e, pj[b]),
                   waits=[(tS, fd)], inc=vS, vis=vis)
            pj_gate[b] = [(vS, vS.n)]
            return vS.n

        def mm_fill(fill_builder, evac_builder, extra_waits, evac_eng="vector",
                    evac_ctr=None, vis=False):
            nonlocal pj_i
            b = pj_i % 4
            pj_i += 1
            gate = list(pj_gate[b])
            sch.op("tensor", lambda e, b=b: fill_builder(e, pj[b]),
                   waits=gate + extra_waits, inc=tS)
            fd = tS.n
            ec = evac_ctr or (vS if evac_eng == "vector" else sS)
            sch.op(evac_eng, lambda e, b=b: evac_builder(e, pj[b]),
                   waits=[(tS, fd)], inc=ec, vis=vis)
            pj_gate[b] = [(ec, ec.n)]
            return ec.n

        x_sync_extra = {}
        resadd_v = {}
        outadd_v = {}
        sq_reads_x = {}

        def sched_x_dma(j):
            slot = (j + 1) % 2
            waits = []
            if j - 2 >= -1 and (j - 2) in resadd_v:
                waits = [(vS, resadd_v[j - 2]), (sS, sq_reads_x[j - 2])]
            row0 = (j + 1) * 256
            src = x_h[row0:row0 + 256, :].rearrange("(a p) n -> p a n", p=128)
            sch.op("sync", lambda e, slot=slot: e.dma_start(
                out=xs[slot][:], in_=src), waits=waits, inc=dx[slot], inck=16)
            return dx[slot].n

        def sched_out_dma(j):
            dst = out_h[j * 256:(j + 1) * 256, :].rearrange(
                "(a p) n -> p a n", p=128)
            sch.op("sync", lambda e: e.dma_start(out=dst, in_=outt[:]),
                   waits=[(vS, outadd_v[j])], inc=dO, inck=16)

        probs_gate = [[], [], []]
        kvslot_read = {0: [], 1: [], 2: []}
        a_read = [[], []]
        at_read = []
        hln_read = []
        y1_read = []
        exp_count = {}
        pv_done = {}
        applies1 = {}
        ht1_done = {}
        proj_done_v = {}
        a_v = {}
        y1_evac = {}
        applies3 = {}
        x2_sq = {}
        hln2_read = []
        ln4_sq = {}

        xval = sched_x_dma(-1)
        sq_reads_x[-1] = None
        applies_h, _ = ln_stats_apply(1, xs[0], [(dx[0], xval)],
                                      [(dx[0], xval)], hln, True)
        sq_reads_x[-1] = sS.n
        resadd_v[-1] = applies_h[1]
        TGROUPS = [(s, [g * 4 + i for i in range(4)]) for s in range(2)
                   for g in range(2)]
        ht_halo = transposes(
            lambda s, fc: hln[:, s, fc * 128:(fc + 1) * 128], hts,
            lambda s: s * 128, TGROUPS,
            lambda s: [(vS, applies_h[s])])
        hln_read.append(tS.n)
        for nc0 in (0, 2, 4, 6):
            proj_fill("wkt", nc0, lambda kc: hts[:, kc, :],
                      [(sS, ht_halo), (dw["wkt"], 16)],
                      lambda e, p, nc0=nc0: e.tensor_copy(
                          kts[2][:, nc0:nc0 + 2, :],
                          p[:].rearrange("p (a n) -> p a n", a=2)))
        for s in range(2):
            for nh in range(2):
                def fill(e, p, s=s, nh=nh):
                    last = None
                    for kc in range(8):
                        last = e.matmul(
                            p[:], hts[:, kc, s * 128:(s + 1) * 128],
                            wsb["wvt"][:, kc, nh * 512:(nh + 1) * 512],
                            start=(kc == 0), stop=(kc == 7))
                    return last

                def evac(e, p, s=s, nh=nh):
                    return e.tensor_copy(
                        vts[2][:, s, nh * 512:(nh + 1) * 512], p[:])
                mm_fill(fill, evac, [(sS, ht_halo), (dw["wvt"], 16)])

        xv = {-1: xval}
        for j in range(min(2, NB)):
            xv[j] = sched_x_dma(j)

        def hln3_out(t):
            return at_t[:, 4 * t:4 * t + 4, :].rearrange("p a n -> p (a n)")

        def hln3_src(s, fc):
            o = s * 1024 + fc * 128
            return at_t[:, o // 256, o % 256:o % 256 + 128]

        def seg_A(j):
            xslot = (j + 1) % 2
            w = [(dx[xslot], xv[j])]
            hg = [(tS, hln_read[-1])] if hln_read else []
            av, _ = ln_stats_apply(j % 2, xs[xslot], list(w), list(w), hln,
                                   True, out_guard=hg)
            applies1[j] = av
            sq_reads_x[j] = sS.n

        def seg_B(j):
            ht1_done[j] = transposes(
                lambda s, fc: hln[:, s, fc * 128:(fc + 1) * 128], hts,
                lambda s: s * 128, TGROUPS,
                lambda s, j=j: [(vS, applies1[j][s])])
            hln_read.append(tS.n)

        def seg_C(j):
            kslot = j % 3
            kv_gate = list(kvslot_read[kslot])
            ht = ht1_done[j]
            for nc0 in (0, 2, 4, 6):
                proj_fill("wqt", nc0, lambda kc: hts[:, kc, :],
                          [(sS, ht)] + ([(dw["wqt"], 16)] if j == 0 else []),
                          lambda e, p, nc0=nc0: e.tensor_copy(
                              qt_t[:, nc0:nc0 + 2, :],
                              p[:].rearrange("p (a n) -> p a n", a=2)))
            for nc0 in (0, 2, 4, 6):
                proj_fill("wkt", nc0, lambda kc: hts[:, kc, :],
                          [(sS, ht)] + kv_gate +
                          ([(dw["wkt"], 16)] if j == 0 else []),
                          lambda e, p, nc0=nc0, kslot=kslot: e.tensor_copy(
                              kts[kslot][:, nc0:nc0 + 2, :],
                              p[:].rearrange("p (a n) -> p a n", a=2)))
            for s in range(2):
                for nh in range(2):
                    def fillv(e, p, s=s, nh=nh):
                        last = None
                        for kc in range(8):
                            last = e.matmul(
                                p[:], hts[:, kc, s * 128:(s + 1) * 128],
                                wsb["wvt"][:, kc, nh * 512:(nh + 1) * 512],
                                start=(kc == 0), stop=(kc == 7))
                        return last

                    def evacv(e, p, s=s, nh=nh, kslot=kslot):
                        return e.tensor_copy(
                            vts[kslot][:, s, nh * 512:(nh + 1) * 512], p[:])
                    mm_fill(fillv, evacv, [(sS, ht)] + kv_gate +
                            ([(dw["wvt"], 16)] if j == 0 else []),
                            vis=(s == 1 and nh == 1))
            kvslot_read[kslot] = []
            proj_done_v[j] = vS.n

        def kt_of(k):
            ob = (k >> 1) - 1
            return ob % 3, k % 2

        sc_i = [0]

        def seg_D(j):
            for qrel in range(2):
                qq = 2 * j + qrel
                for c in range(3):
                    k = qq + c
                    kslot_c, half = kt_of(k)
                    pslot = c
                    pgate = list(probs_gate[pslot])
                    for u4 in range(4):
                        g, par_u = u4 >> 1, u4 & 1
                        h0 = 8 * g + par_u
                        bi = sc_i[0] % 3
                        sc_i[0] += 1
                        gate = list(sc_gate[bi])
                        mname = None
                        if c == 0:
                            mname = ("m_q0_c0" if qq <= 1 else "m_c0")
                        elif c == 1 and qq == 0:
                            mname = "m_q0_c1"
                        elif c == 2:
                            mname = "m_c2"

                        def fill(e, h0=h0, par_u=par_u, kslot_c=kslot_c,
                                 half=half, qrel=qrel, bi=bi, mname=mname):
                            last = None
                            po = 64 * par_u
                            for i in range(4):
                                h = h0 + 2 * i
                                lhs = kts[kslot_c][po:po + 64, h // 2,
                                                   half * 128:half * 128 + 128]
                                rhs = qt_t[po:po + 64, h // 2,
                                           qrel * 128:qrel * 128 + 128]
                                last = e.matmul(
                                    sc[bi][:, i * 128:(i + 1) * 128],
                                    lhs, rhs, start=(i == 0),
                                    stop=(i == 3 and mname is None))
                            if mname is not None:
                                last = e.matmul(sc[bi][:], idt[:],
                                                msk[mname][:],
                                                start=False, stop=True)
                            return last
                        w_extra = ([(vS, proj_done_v[j])]
                                   if (qrel == 0 and c == 0 and u4 == 0)
                                   else [])
                        sch.op("tensor", fill, waits=gate + w_extra, inc=tS)
                        f_done = tS.n

                        def expf(e, bi=bi, pslot=pslot, h0=h0, g=g):
                            return e.activation(
                                probs[pslot][:, h0:8 * g + 8:2, :],
                                sc[bi][:].rearrange("p (a n) -> p a n", a=4),
                                AF.Exp, bias=ESHIFT, scale=0.125)
                        sch.op("scalar", expf,
                               waits=[(tS, f_done), (vS, V_CONST)] + pgate,
                               inc=sS, vis=(c == 2 and u4 == 3))
                        exp_count[(qq, c, u4)] = sS.n
                        sc_gate[bi] = [(sS, sS.n)]
                        pgate = []

                pv_w = (list(pj_gate[2]) + list(pj_gate[3]) +
                        list(pvl_gate) +
                        [(sS, exp_count[(qq, 2, 3)]), (vS, V_CONST)])

                def pvfill(e, qq=qq):
                    last = None
                    for c in range(3):
                        k = qq + c
                        kslot_c, half = kt_of(k)
                        for h in range(16):
                            lhs = probs[c][:, h, :]
                            rhs = vts[kslot_c][:, half, h * 64:(h + 1) * 64]
                            last = e.matmul(
                                pv[h // 8][:, (h % 8) * 64:(h % 8 + 1) * 64],
                                lhs, rhs, start=(c == 0 and h % 8 == 0),
                                stop=(c == 2 and (h % 8) == 7))
                            last = e.matmul(
                                pvl[:, h:h + 1], lhs, ones[:],
                                start=(c == 0 and h == 0),
                                stop=(c == 2 and h == 15))
                    return last
                sch.op("tensor", pvfill, waits=pv_w, inc=tS)
                pv_done[qq] = tS.n
                for c in range(3):
                    probs_gate[c] = [(tS, pv_done[qq])]
                for b in range(3):
                    kvslot_read[kt_of(qq + b)[0]] = [(tS, pv_done[qq])]

                agate = list(a_read[qrel])

                def norm(e, qrel=qrel):
                    e.reciprocal(recip[:], pvl[:])
                    e.drain()
                    last = None
                    for h in range(16):
                        last = e.tensor_scalar(
                            a_t[qrel][:, h * 64:(h + 1) * 64],
                            pv[h // 8][:, (h % 8) * 64:(h % 8 + 1) * 64],
                            recip[:, h:h + 1], None, ALU.mult)
                    return last
                sch.op("vector", norm, waits=[(tS, pv_done[qq])] + agate,
                       inc=vS)
                pj_gate[2] = [(vS, vS.n)]
                pj_gate[3] = [(vS, vS.n)]
                pvl_gate.clear()
                pvl_gate.append((vS, vS.n))
                a_read[qrel] = []
                a_v.setdefault(j, [None, None])[qrel] = vS.n

        def seg_E(j):
            atgate = [(tS, at_read[-1])] if at_read else []
            at_done = transposes(
                lambda s, fc: a_t[s][:, fc * 128:(fc + 1) * 128], at_t,
                lambda s: s * 128, TGROUPS,
                lambda s, j=j: [(vS, a_v[j][s])] + atgate)
            for s in range(2):
                a_read[s] = [(tS, tS.n)]
            y1gate = list(y1_read)
            del y1_read[:]
            for s in range(2):
                for nh in range(2):
                    def fillo(e, p, s=s, nh=nh):
                        last = None
                        for kc in range(8):
                            last = e.matmul(
                                p[:], at_t[:, kc, s * 128:(s + 1) * 128],
                                wsb["wot"][:, kc, nh * 512:(nh + 1) * 512],
                                start=(kc == 0), stop=(kc == 7))
                        return last

                    def evaco(e, p, s=s, nh=nh):
                        return e.copy(
                            y1sb[:, s, nh * 512:(nh + 1) * 512], p[:])
                    mm_fill(fillo, evaco,
                            [(sS, at_done)] + y1gate +
                            ([(dw["wot"], 16)] if j == 0 else []),
                            evac_eng="scalar", vis=(s == 1 and nh == 1))
                    y1gate = []
            at_read.append(tS.n)
            y1_evac[j] = sS.n

        def seg_F(j):
            par = j % 2
            g_guard = ([(sS, ln4_sq[j - 1]), (dO, 16 * j)]
                       if j > 0 else [])
            ln_stats_apply(par, y1sb, [(sS, y1_evac[j])], [], gsb, False,
                           out_guard=g_guard, apply_vis=False)
            xslot = (j + 1) % 2
            x_guard = [(sS, x2_sq[j - 1])] if j > 0 else []
            for t in range(2):
                sch.op("vector", lambda e, t=t, xslot=xslot: e.tensor_add(
                    x2[:, t, :], xs[xslot][:, t, :], gsb[:, t, :]),
                    waits=x_guard if t == 0 else (), inc=vS, vis=(t == 1))
            resadd_v[j] = vS.n
            if j + 2 < NB:
                xv[j + 2] = sched_x_dma(j + 2)
            av3, sq3 = ln_stats_apply(par, x2, [], [(vS, resadd_v[j])],
                                      hln2, True,
                                      out_guard=list(hln2_read))
            applies3[j] = av3
            x2_sq[j] = sq3

        def seg_G(j):
            h2t_done = transposes(
                lambda s, fc: hln2[:, s, fc * 128:(fc + 1) * 128], hts2,
                lambda s: s * 128, TGROUPS,
                lambda s, j=j: [(vS, applies3[j][s])])
            hln2_read[:] = [(tS, tS.n)]
            for s in range(2):
                for nh in range(2):
                    def fillf(e, p, s=s, nh=nh):
                        last = None
                        for kc in range(8):
                            last = e.matmul(
                                p[:], hts2[:, kc, s * 128:(s + 1) * 128],
                                wsb["cbt"][:, kc, nh * 512:(nh + 1) * 512],
                                start=(kc == 0), stop=(kc == 7))
                        return last

                    def evacf(e, p, s=s, nh=nh):
                        return e.tensor_mul(
                            gsb[:, s, nh * 512:(nh + 1) * 512], p[:],
                            scbrep[:, nh * 512:(nh + 1) * 512])
                    mm_fill(fillf, evacf,
                            [(sS, h2t_done), (dsm, SMALLS)] +
                            ([(dw["cbt"], 16)] if j == 0 else []))

        def seg_H(j):
            par = j % 2
            for t in range(2):
                sch.op("vector", lambda e, t=t: e.tensor_add(
                    gsb[:, t, :], gsb[:, t, :], biasrep[:]),
                    waits=[(dsm, SMALLS)] if (j == 0 and t == 0) else (),
                    inc=vS, vis=(t == 1))
            g_ready_v = vS.n
            rstd4, sq4 = ln_stats_apply(par, gsb, [], [(vS, g_ready_v)],
                                        None, False)
            ln4_sq[j] = sq4
            sch.op("vector", lambda e: e.drain())
            sch.op("vector", lambda e, par=par: e.tensor_scalar(
                st_negm[par][:], st_mean[par][:], -1.0, None, ALU.mult))
            sch.op("vector", lambda e: e.drain())
            sch.op("vector", lambda e, par=par: e.tensor_mul(
                st_mb[par][:], st_negm[par][:], st_rstd[par][:]),
                waits=[(sS, rstd4)])
            sch.op("vector", lambda e: e.drain())
            outw = [(dO, 16 * j)] if j > 0 else []
            for t in range(2):
                sch.op("vector", lambda e, t=t, par=par: e.tensor_scalar(
                    gsb[:, t, :], gsb[:, t, :], st_rstd[par][:, t:t + 1],
                    None, ALU.mult), waits=outw if t == 0 else ())
                sch.op("vector", lambda e, t=t, par=par: e.tensor_scalar(
                    gsb[:, t, :], gsb[:, t, :], st_mb[par][:, t:t + 1],
                    None, ALU.add))
                sch.op("vector", lambda e, t=t: e.tensor_add(
                    gsb[:, t, :], gsb[:, t, :], x2[:, t, :]),
                    inc=vS, vis=(t == 1))
            outadd_v[j] = vS.n
            sched_out_dma(j)

        seg_A(0)
        seg_B(0)
        seg_C(0)
        for j in range(NB):
            seg_D(j)
            if j + 1 < NB:
                seg_A(j + 1)
            seg_E(j)
            if j > 0:
                seg_G(j - 1)
                seg_H(j - 1)
            if j + 1 < NB:
                seg_B(j + 1)
                seg_C(j + 1)
            seg_F(j)
        seg_G(NB - 1)
        seg_H(NB - 1)

        if DBG_PHASE < 99:
            tv, vv, sv2 = tS.n, vS.n, sS.n

            def _cleanup(e):
                for n in dw:
                    e.wait_ge(dw[n].sem, 16)
                e.wait_ge(dsm.sem, SMALLS)
                for i in range(3):
                    if dx[i].n:
                        e.wait_ge(dx[i].sem, dx[i].n)
                if tv:
                    e.wait_ge(tS.sem, tv)
                if vv:
                    e.wait_ge(vS.sem, vv)
                if sv2:
                    e.wait_ge(sS.sem, sv2)
                for jj in range(NB):
                    dst = out_h[jj * 256:(jj + 1) * 256, :].rearrange(
                        "(a p) n -> p a n", p=128)
                    e.dma_start(out=dst, in_=outt[:]).then_inc(dO.sem, 16)
            sch.op("sync", _cleanup)
            dO.n = 16 * NB

        if DBG_DUMP:
            dbg_src = {"d_hln": hln, "d_hts": hts, "d_qt": qt_t,
                       "d_kt0": kts[0], "d_vt0": vts[0], "d_kt2": kts[2],
                       "d_vt2": vts[2], "d_pb0": probs[0], "d_pb1": probs[1],
                       "d_pb2": probs[2], "d_a0": a_t[0], "d_a1": a_t[1],
                       "d_at": at_t, "d_y1": y1sb, "d_g": gsb, "d_x2": x2}
            tv, vv, sv2 = tS.n, vS.n, sS.n
            dov = dO.n
            dcvt = outt[:].rearrange("p a n -> p (a n)")

            def _dump(e):
                e.wait_ge(tS.sem, tv)
                e.wait_ge(vS.sem, vv)
                e.wait_ge(sS.sem, sv2)
                e.wait_ge(dO.sem, dov)
                e.tensor_copy(dcvt[:, 0:2], st_sum[0][:])
                e.tensor_copy(dcvt[:, 2:4], st_sq[0][:])
                e.tensor_copy(dcvt[:, 4:6], st_mean[0][:])
                e.tensor_copy(dcvt[:, 6:8], st_ve[0][:])
                e.tensor_copy(dcvt[:, 8:10], st_rstd[0][:])
                return e.engine_nop()
            sch.op("vector", _dump, inc=vS)
            sch.op("sync", lambda e, v=vS.n: e.wait_ge(vS.sem, v))
            sch.op("sync", lambda e: e.dma_start(
                out=dbg_h["d_sums"][:], in_=dcvt[:, 0:16]).then_inc(dO.sem, 16))
            dO.n += 16
            dcur0 = dO.n
            sch.op("vector", lambda e, d=dcur0: e.wait_ge(dO.sem, d))
            for n in ["d_hln", "d_hts", "d_qt", "d_kt0", "d_vt0", "d_kt2",
                      "d_vt2", "d_pb0", "d_pb1", "d_pb2", "d_a0", "d_a1",
                      "d_at", "d_y1", "d_g", "d_x2"]:
                srcap = dbg_src[n]
                nel = 1024 if n in ("d_a0", "d_a1") else 2048

                def _cvt(e, srcap=srcap, nel=nel):
                    flat = srcap[:].rearrange("p ... -> p (...)") \
                        if srcap[:].ndim > 2 else srcap[:]
                    return e.tensor_copy(dcvt[:, 0:nel], flat)
                sch.op("vector", _cvt, inc=vS)
                vcur = vS.n

                def _dma(e, n=n, nel=nel, vcur=vcur):
                    e.wait_ge(vS.sem, vcur)
                    return e.dma_start(out=dbg_h[n][:, 0:nel],
                                       in_=dcvt[:, 0:nel]).then_inc(dO.sem, 16)
                sch.op("sync", _dma)
                dO.n += 16
                dcur = dO.n
                sch.op("vector", lambda e, dcur=dcur: e.wait_ge(dO.sem, dcur))

        sch.op("sync", lambda e: e.wait_ge(dO.sem, dO.n if DBG_DUMP else 16 * NB))

        block = ctx.enter_context(nc.Block())

        @block.sync
        def _(eng):
            sch.emit("sync", eng)

        @block.gpsimd
        def _(eng):
            sch.emit("gpsimd", eng)

        @block.vector
        def _(eng):
            sch.emit("vector", eng)

        @block.scalar
        def _(eng):
            sch.emit("scalar", eng)

        @block.tensor
        def _(eng):
            sch.emit("tensor", eng)

    return nc


_NC_CACHE = {}


def _get_program():
    if "nc" not in _NC_CACHE:
        _NC_CACHE["nc"] = build_program()
    return _NC_CACHE["nc"]


def _masks_for_core(core):
    i = np.arange(128)
    m = np.arange(128)[:, None]
    c0 = np.where(m > i[None, :], 0.0, MASKVAL).astype(np.float32)
    c2 = np.where(m <= i[None, :], 0.0, MASKVAL).astype(np.float32)
    full = np.full((128, 128), MASKVAL, np.float32)
    zero = np.zeros((128, 128), np.float32)
    if core == 0:
        q0c0, q0c1, q1c0 = full, full, full
    else:
        q0c0, q0c1, q1c0 = c0, zero, c0
    tile = lambda t: np.tile(t, (1, 4)).astype(ml_dtypes.bfloat16)
    return {"m_c0": tile(c0), "m_c2": tile(c2), "m_q0_c0": tile(q0c0),
            "m_q0_c1": tile(q0c1)}


def kernel(L0_attn_x, L0_attn_wq, L0_attn_wk, L0_attn_wv, L0_attn_wo,
           L0_ffn_CB, L0_ffn_SCB, L0_ffn_bias):
    x = np.asarray(L0_attn_x, np.float32).reshape(16384, 1024)
    wts = {"wqt": L0_attn_wq, "wkt": L0_attn_wk, "wvt": L0_attn_wv,
           "wot": L0_attn_wo, "cbt": L0_ffn_CB}
    wts = {n: np.ascontiguousarray(np.asarray(w, np.float32).T)
           for n, w in wts.items()}
    scb = np.tile(np.asarray(L0_ffn_SCB, np.float32).reshape(1, 1024),
                  (128, 1))
    bias = np.tile(np.asarray(L0_ffn_bias, np.float32).reshape(1, 1024),
                   (128, 1))
    ident = np.eye(128, dtype=ml_dtypes.bfloat16)

    nc = _get_program()
    in_maps = []
    for c in range(8):
        lo = c * 2048
        xs = np.empty((2304, 1024), np.float32)
        if c == 0:
            xs[0:256] = 0.0
        else:
            xs[0:256] = x[lo - 256:lo]
        xs[256:] = x[lo:lo + 2048]
        m = {"x": xs, "scb": scb, "bias": bias, "ident": ident}
        m.update(wts)
        m.update(_masks_for_core(c))
        in_maps.append(m)

    res = run_bass_kernel_spmd(nc, in_maps, core_ids=list(range(8)))
    out = np.concatenate([res.results[c]["out"] for c in range(8)], axis=0)
    return out.reshape(1, 16384, 1024)


# version 16
# speedup vs baseline: 1.0080x; 1.0080x over previous
import numpy as np
import ml_dtypes
from contextlib import ExitStack

import concourse.bass as bass
import concourse.mybir as mybir
from concourse.bass_utils import run_bass_kernel_spmd

F32 = mybir.dt.float32
BF16 = mybir.dt.bfloat16
AF = mybir.ActivationFunctionType
ALU = mybir.AluOpType
AX = mybir.AxisListType

NB = 8
DBG_PHASE = 99
DBG_DUMP = False
USE_TNOP = True
USE_DRAIN = True
EPS = 1e-5
ESHIFT = -8.0
MASKVAL = -30000.0


class Ctr:
    def __init__(self, sem):
        self.sem = sem
        self.n = 0


class Sched:

    ENGINES = ("sync", "gpsimd", "vector", "scalar", "tensor")

    def __init__(self):
        self.q = {e: [] for e in self.ENGINES}
        self.relay = None

    def op(self, eng, builder, waits=(), inc=None, inck=1, vis=True):
        w = [(c, v) for (c, v) in waits if v > 0]
        if (eng == "tensor" and inc is not None and self.relay is not None
                and inc is self.relay[1]):
            priv, pub = self.relay
            priv.n += inck
            pv = priv.n
            self.q[eng].append((builder, w, priv, inck))
            pub.n += inck

            def relay_fn(e, pv=pv, inck=inck, pub=pub):
                e.wait_ge(priv.sem, pv)
                e.sem_inc(pub.sem, inck)
            self.q["gpsimd"].append((relay_fn, [], None, 0))
            return pub.n
        if inc is not None and vis and eng in ("vector", "scalar"):
            self.q[eng].append((builder, w, None, 0))
            inc.n += inck
            self.q[eng].append(((lambda e: e.drain()), [], inc, inck))
            return inc.n
        if inc is not None:
            inc.n += inck
        self.q[eng].append((builder, w, inc, inck))
        return None if inc is None else inc.n

    def emit(self, eng_name, eng):
        for builder, waits, inc, inck in self.q[eng_name]:
            for c, v in waits:
                eng.wait_ge(c.sem, v)
            inst = builder(eng)
            if inc is not None and inst is not None:
                inst.then_inc(inc.sem, inck)


def build_program(mk=None):
    factory = mk if mk is not None else bass.Bass
    nc = factory("TRN2", target_bir_lowering=False,
                 detect_race_conditions=False)

    x_h = nc.dram_tensor("x", [2304, 1024], F32, kind="ExternalInput")
    w_h = {n: nc.dram_tensor(n, [1024, 1024], F32, kind="ExternalInput")
           for n in ["wqt", "wkt", "wvt", "wot", "cbt"]}
    scb_h = nc.dram_tensor("scb", [128, 1024], F32, kind="ExternalInput")
    bias_h = nc.dram_tensor("bias", [128, 1024], F32, kind="ExternalInput")
    id_h = nc.dram_tensor("ident", [128, 128], BF16, kind="ExternalInput")
    mask_h = {n: nc.dram_tensor(n, [128, 512], BF16, kind="ExternalInput")
              for n in ["m_c0", "m_c2", "m_q0_c0", "m_q0_c1"]}
    out_h = nc.dram_tensor("out", [2048, 1024], F32, kind="ExternalOutput")
    dbg_h = {}
    if DBG_DUMP:
        for n, shp, dt in [("d_hln", [128, 2048], F32), ("d_hts", [128, 2048], F32),
                           ("d_qt", [128, 2048], F32), ("d_kt0", [128, 2048], F32),
                           ("d_vt0", [128, 2048], F32), ("d_kt2", [128, 2048], F32),
                           ("d_vt2", [128, 2048], F32),
                           ("d_pb0", [128, 2048], F32), ("d_pb1", [128, 2048], F32),
                           ("d_pb2", [128, 2048], F32), ("d_a0", [128, 1024], F32),
                           ("d_a1", [128, 1024], F32), ("d_at", [128, 2048], F32),
                           ("d_y1", [128, 2048], F32), ("d_g", [128, 2048], F32),
                           ("d_x2", [128, 2048], F32), ("d_sums", [128, 16], F32)]:
            dbg_h[n] = nc.dram_tensor(n, shp, dt, kind="ExternalOutput")

    ctx = ExitStack()
    with ctx:
        S = lambda *a: ctx.enter_context(nc.sbuf_tensor(*a))
        P = lambda *a: ctx.enter_context(nc.psum_tensor(*a))
        SEM = lambda n: Ctr(ctx.enter_context(nc.semaphore(n)))

        wsb = {n: S(n + "s", [128, 8, 1024], BF16) for n in w_h}
        xs = [S(f"xr{i}", [128, 2, 1024], F32) for i in range(3)]
        x2 = S("x2", [128, 2, 1024], F32)
        hln = S("hln", [128, 2, 1024], BF16)
        hts = S("hts", [128, 8, 256], BF16)
        hts2 = S("hts2", [128, 8, 256], BF16)
        qt_t = S("qt", [128, 8, 256], BF16)
        kts = [S(f"kt{i}", [128, 8, 256], BF16) for i in range(3)]
        vts = [S(f"vt{i}", [128, 2, 1024], BF16) for i in range(3)]
        probs = [S(f"pb{i}", [128, 16, 128], BF16) for i in range(3)]
        a_t = [S(f"a{i}", [128, 1024], BF16) for i in range(2)]
        at_t = S("at", [128, 8, 256], BF16)
        y1sb = S("y1sb", [128, 2, 1024], F32)
        gsb = S("gsb", [128, 2, 1024], F32)
        outt = gsb
        sqf = S("sqf", [128, 1024], BF16)
        scbrep = S("screp", [128, 1024], F32)
        biasrep = S("birep", [128, 1024], F32)
        idt = S("idt", [128, 128], BF16)
        msk = {n: S("s" + n, [128, 512], BF16) for n in mask_h}
        ones = S("ones", [128, 1], BF16)
        csb = S("csb", [128, 4], F32)
        recip = S("recip", [128, 16], F32)
        st_sum = [S(f"tsum{i}", [128, 2], F32) for i in range(2)]
        st_sq = [S(f"tsq{i}", [128, 2], F32) for i in range(2)]
        st_mean = [S(f"tmean{i}", [128, 2], F32) for i in range(2)]
        st_m2 = [S(f"tm2{i}", [128, 2], F32) for i in range(2)]
        st_ve = [S(f"tve{i}", [128, 2], F32) for i in range(2)]
        st_rstd = [S(f"trstd{i}", [128, 2], F32) for i in range(2)]

        nc.const_aps.register(F32, EPS, csb[:, 0:1])
        nc.const_aps.register(F32, -0.5, csb[:, 1:2])
        nc.const_aps.register(F32, 0.125, csb[:, 2:3])
        nc.const_aps.register(F32, ESHIFT, csb[:, 3:4])

        sc = [P(f"sc{i}", [128, 512], F32) for i in range(3)]
        pv = [P("pv0", [128, 512], F32), P("pv1", [128, 512], F32)]
        pj = [P("pj0", [128, 512], F32), P("pj1", [128, 512], F32),
              pv[0], pv[1]]
        pvl = P("pvl", [128, 16], F32)

        dx = [SEM(f"dx{i}") for i in range(3)]
        dw = {n: SEM("d" + n) for n in w_h}
        dsm = SEM("dsm")
        dO = SEM("dO")
        tS = SEM("tS")
        tP = SEM("tP")
        vS = SEM("vS")
        sS = SEM("sS")

        PH = 5 if DBG_PHASE in (51, 52) else DBG_PHASE
        sch = Sched()
        if USE_TNOP:
            sch.relay = (tP, tS)
        last_ve = {0: 0, 1: 0}
        sqf_free = [0]
        pj_gate = [[], [], [], []]
        sc_gate = [[], [], []]
        pvl_gate = []
        pj_i = 0
        first_tensor = [True]

        def _smalls(eng):
            dsm_sem = dsm.sem
            eng.dma_start(out=scbrep[:], in_=scb_h[:]).then_inc(dsm_sem, 16)
            eng.dma_start(out=biasrep[:], in_=bias_h[:]).then_inc(dsm_sem, 16)
            eng.dma_start(out=idt[:], in_=id_h[:]).then_inc(dsm_sem, 16)
            for n in mask_h:
                eng.dma_start(out=msk[n][:], in_=mask_h[n][:]).then_inc(dsm_sem, 16)
        sch.op("sync", _smalls)
        dsm.n = 16 * (3 + len(mask_h))
        SMALLS = dsm.n

        def _weights(eng):
            for n in ["wqt", "wkt", "wvt", "wot", "cbt"]:
                src = w_h[n][:].rearrange("(kc p) n -> p kc n", p=128)
                eng.dma_start(out=wsb[n][:], in_=src).then_inc(dw[n].sem, 16)
        sch.op("gpsimd", _weights)
        for n in dw:
            dw[n].n = 16


        def _vconsts(eng):
            eng.memset(csb[:, 0:1], EPS)
            eng.memset(csb[:, 1:2], -0.5)
            eng.memset(csb[:, 2:3], 0.125)
            eng.memset(csb[:, 3:4], ESHIFT)
            return eng.memset(ones[:], 1.0)
        sch.op("vector", _vconsts, inc=vS)
        V_CONST = vS.n
        sch.op("vector", lambda e: e.tensor_scalar(
            scbrep[:], scbrep[:], 1.0 / 127.0, None, ALU.mult),
            waits=[(dsm, SMALLS)], inc=vS)
        V_SCB = vS.n

        def ln_stats_apply(par, src, waits_v, waits_s, out_tile, out_is_bf,
                           out_guard=(), apply_vis=True):
            sch.op("scalar", lambda e: e.activation(
                sqf[:], src[:, 0, :], AF.Square),
                waits=list(waits_s) + [(vS, sqf_free[0])])
            sch.op("scalar", lambda e: e.drain(), inc=sS, vis=False)
            sq0 = sS.n
            for t in range(2):
                sch.op("vector", lambda e, t=t: e.reduce_sum(
                    st_sum[par][:, t:t + 1], src[:, t, :], AX.X),
                    waits=waits_v if t == 0 else ())
            sch.op("vector", lambda e, par=par: e.reduce_sum(
                st_sq[par][:, 0:1], sqf[:], AX.X),
                waits=[(sS, sq0), (vS, last_ve[par])], inc=vS, vis=False)
            rd0 = vS.n
            sch.op("scalar", lambda e: e.activation(
                sqf[:], src[:, 1, :], AF.Square), waits=[(vS, rd0)])
            sch.op("scalar", lambda e: e.drain(), inc=sS, vis=False)
            sq1 = sS.n
            sch.op("vector", lambda e, par=par: e.reduce_sum(
                st_sq[par][:, 1:2], sqf[:], AX.X), waits=[(sS, sq1)],
                inc=vS, vis=False)
            sqf_free[0] = vS.n
            sch.op("vector", lambda e: e.drain())
            sch.op("vector", lambda e: e.tensor_scalar(
                st_mean[par][:], st_sum[par][:], 1.0 / 1024.0, None, ALU.mult))
            sch.op("vector", lambda e: e.drain())
            sch.op("vector", lambda e: e.tensor_mul(
                st_m2[par][:], st_mean[par][:], st_mean[par][:]))
            sch.op("vector", lambda e: e.drain())
            sch.op("vector", lambda e: e.tensor_scalar(
                st_ve[par][:], st_sq[par][:], 1.0 / 1024.0, None, ALU.mult))
            sch.op("vector", lambda e: e.drain())
            sch.op("vector", lambda e: e.tensor_sub(
                st_ve[par][:], st_ve[par][:], st_m2[par][:]), inc=vS)
            ve_done = vS.n
            last_ve[par] = ve_done
            sch.op("scalar", lambda e: e.activation(
                st_m2[par][:], st_ve[par][:], AF.Ln, bias=EPS),
                waits=[(vS, ve_done), (vS, V_CONST)])
            sch.op("scalar", lambda e: e.activation(
                st_rstd[par][:], st_m2[par][:], AF.Exp, scale=-0.5))
            sch.op("scalar", lambda e: e.drain(), inc=sS, vis=False)
            rstd_done = sS.n
            av = []
            ot = out_tile if callable(out_tile) else (
                lambda t, o=out_tile: o[:, t, :])
            for t in range(2):
                sch.op("vector", lambda e, t=t: e.tensor_scalar(
                    ot(t), src[:, t, :], st_mean[par][:, t:t + 1],
                    None, ALU.subtract),
                    waits=([(sS, rstd_done)] + list(out_guard))
                    if t == 0 else ())
                sch.op("vector", lambda e, t=t: e.tensor_scalar(
                    ot(t), ot(t),
                    st_rstd[par][:, t:t + 1], None, ALU.mult), inc=vS,
                    vis=apply_vis)
                av.append(vS.n)
            return av, sq1

        def transposes(src_fn, dst, dst_col, groups, waits0):
            nonlocal pj_i
            for (s, fcs) in groups:
                b = pj_i % 4
                pj_i += 1
                gate = list(pj_gate[b])

                def fill(e, s=s, fcs=fcs, b=b):
                    last = None
                    for i, fc in enumerate(fcs):
                        last = e.matmul(pj[b][:, i * 128:(i + 1) * 128],
                                        src_fn(s, fc), idt[:],
                                        start=(i == 0), stop=(i == len(fcs) - 1))
                    return last
                w = gate + (waits0(s) if callable(waits0) else list(waits0))
                if first_tensor[0]:
                    w = w + [(dsm, SMALLS)]
                    first_tensor[0] = False
                sch.op("tensor", fill, waits=w, inc=tS)
                fill_done = tS.n
                c0 = dst_col(s)
                sch.op("scalar", lambda e, fcs=fcs, b=b, c0=c0: e.copy(
                    dst[:, fcs[0]:fcs[0] + len(fcs), c0:c0 + 128],
                    pj[b][:].rearrange("p (a n) -> p a n", a=len(fcs))),
                    waits=[(tS, fill_done)], inc=sS,
                    vis=((s, fcs) == groups[-1]))
                pj_gate[b] = [(sS, sS.n)]
            return sS.n

        def proj_fill(wname, nc0, rhs_ap, extra_waits, out_dst, vis=False,
                      eng="vector"):
            nonlocal pj_i
            b = pj_i % 4
            pj_i += 1
            gate = list(pj_gate[b])

            def fill(e, b=b):
                last = None
                for ni in range(2):
                    for kc in range(8):
                        lhs = wsb[wname][:, kc, (nc0 + ni) * 128:
                                         (nc0 + ni + 1) * 128]
                        rhs = rhs_ap(kc)
                        o = pj[b][:, ni * 256:(ni + 1) * 256]
                        last = e.matmul(o, lhs, rhs,
                                        start=(ni == 0 and kc == 0),
                                        stop=(ni == 1 and kc == 7))
                return last
            sch.op("tensor", fill, waits=gate + extra_waits, inc=tS)
            fd = tS.n
            ec = vS if eng == "vector" else sS
            sch.op(eng, lambda e, b=b: out_dst(e, pj[b]),
                   waits=[(tS, fd)], inc=ec, vis=vis)
            pj_gate[b] = [(ec, ec.n)]
            return ec.n

        def mm_fill(fill_builder, evac_builder, extra_waits, evac_eng="vector",
                    evac_ctr=None, vis=False):
            nonlocal pj_i
            b = pj_i % 4
            pj_i += 1
            gate = list(pj_gate[b])
            sch.op("tensor", lambda e, b=b: fill_builder(e, pj[b]),
                   waits=gate + extra_waits, inc=tS)
            fd = tS.n
            ec = evac_ctr or (vS if evac_eng == "vector" else sS)
            sch.op(evac_eng, lambda e, b=b: evac_builder(e, pj[b]),
                   waits=[(tS, fd)], inc=ec, vis=vis)
            pj_gate[b] = [(ec, ec.n)]
            return ec.n

        x_sync_extra = {}
        resadd_v = {}
        outadd_v = {}
        sq_reads_x = {}

        def sched_x_dma(j):
            slot = (j + 1) % 3
            waits = []
            if j - 3 >= -1 and (j - 3) in resadd_v:
                waits = [(vS, resadd_v[j - 3]), (sS, sq_reads_x[j - 3])]
            row0 = (j + 1) * 256
            src = x_h[row0:row0 + 256, :].rearrange("(a p) n -> p a n", p=128)
            sch.op("sync", lambda e, slot=slot: e.dma_start(
                out=xs[slot][:], in_=src), waits=waits, inc=dx[slot], inck=16)
            return dx[slot].n

        def sched_out_dma(j):
            dst = out_h[j * 256:(j + 1) * 256, :].rearrange(
                "(a p) n -> p a n", p=128)
            sch.op("sync", lambda e: e.dma_start(out=dst, in_=outt[:]),
                   waits=[(vS, outadd_v[j])], inc=dO, inck=16)

        probs_gate = [[], [], []]
        kvslot_read = {0: [], 1: [], 2: []}
        a_read = [[], []]
        at_read = []
        hln_read = []
        y1_read = []
        exp_count = {}
        pv_done = {}
        applies1 = {}
        ht1_done = {}
        proj_done_v = {}
        a_v = {}
        y1_evac = {}
        applies3 = {}
        x2_sq = {}
        ln4_sq = {}

        xval = sched_x_dma(-1)
        sq_reads_x[-1] = None
        applies_h, _ = ln_stats_apply(1, xs[0], [(dx[0], xval)],
                                      [(dx[0], xval)], hln, True)
        sq_reads_x[-1] = sS.n
        resadd_v[-1] = applies_h[1]
        TGROUPS = [(s, [g * 4 + i for i in range(4)]) for s in range(2)
                   for g in range(2)]
        ht_halo = transposes(
            lambda s, fc: hln[:, s, fc * 128:(fc + 1) * 128], hts,
            lambda s: s * 128, TGROUPS,
            lambda s: [(vS, applies_h[s])])
        hln_read.append(tS.n)
        for nc0 in (0, 2, 4, 6):
            proj_fill("wkt", nc0, lambda kc: hts[:, kc, :],
                      [(sS, ht_halo), (dw["wkt"], 16)],
                      lambda e, p, nc0=nc0: e.tensor_copy(
                          kts[2][:, nc0:nc0 + 2, :],
                          p[:].rearrange("p (a n) -> p a n", a=2)))
        for s in range(2):
            for nh in range(2):
                def fill(e, p, s=s, nh=nh):
                    last = None
                    for kc in range(8):
                        last = e.matmul(
                            p[:], hts[:, kc, s * 128:(s + 1) * 128],
                            wsb["wvt"][:, kc, nh * 512:(nh + 1) * 512],
                            start=(kc == 0), stop=(kc == 7))
                    return last

                def evac(e, p, s=s, nh=nh):
                    return e.tensor_copy(
                        vts[2][:, s, nh * 512:(nh + 1) * 512], p[:])
                mm_fill(fill, evac, [(sS, ht_halo), (dw["wvt"], 16)])

        xv = {-1: xval}
        for j in range(min(3, NB)):
            xv[j] = sched_x_dma(j)

        def hln3_out(t):
            return at_t[:, 4 * t:4 * t + 4, :].rearrange("p a n -> p (a n)")

        def hln3_src(s, fc):
            o = s * 1024 + fc * 128
            return at_t[:, o // 256, o % 256:o % 256 + 128]

        def seg_A(j):
            xslot = (j + 1) % 3
            w = [(dx[xslot], xv[j])]
            hg = [(tS, hln_read[-1])] if hln_read else []
            av, _ = ln_stats_apply(j % 2, xs[xslot], list(w), list(w), hln,
                                   True, out_guard=hg)
            applies1[j] = av
            sq_reads_x[j] = sS.n

        def seg_B(j):
            ht1_done[j] = transposes(
                lambda s, fc: hln[:, s, fc * 128:(fc + 1) * 128], hts,
                lambda s: s * 128, TGROUPS,
                lambda s, j=j: [(vS, applies1[j][s])])
            hln_read.append(tS.n)

        def seg_C(j):
            kslot = j % 3
            kv_gate = list(kvslot_read[kslot])
            ht = ht1_done[j]
            ei = [0]

            def nxt():
                eng = "vector" if ei[0] % 2 == 0 else "scalar"
                v = ei[0] >= 10
                ei[0] += 1
                return eng, v

            def cp(e, eng, dst, sap):
                return (e.tensor_copy(dst, sap) if eng == "vector"
                        else e.copy(dst, sap))
            for nc0 in (0, 2, 4, 6):
                eng, v = nxt()
                proj_fill("wqt", nc0, lambda kc: hts[:, kc, :],
                          [(sS, ht)] + ([(dw["wqt"], 16)] if j == 0 else []),
                          lambda e, p, nc0=nc0, eng=eng: cp(
                              e, eng, qt_t[:, nc0:nc0 + 2, :],
                              p[:].rearrange("p (a n) -> p a n", a=2)),
                          vis=v, eng=eng)
            for nc0 in (0, 2, 4, 6):
                eng, v = nxt()
                proj_fill("wkt", nc0, lambda kc: hts[:, kc, :],
                          [(sS, ht)] + kv_gate +
                          ([(dw["wkt"], 16)] if j == 0 else []),
                          lambda e, p, nc0=nc0, kslot=kslot, eng=eng: cp(
                              e, eng, kts[kslot][:, nc0:nc0 + 2, :],
                              p[:].rearrange("p (a n) -> p a n", a=2)),
                          vis=v, eng=eng)
            for s in range(2):
                for nh in range(2):
                    def fillv(e, p, s=s, nh=nh):
                        last = None
                        for kc in range(8):
                            last = e.matmul(
                                p[:], hts[:, kc, s * 128:(s + 1) * 128],
                                wsb["wvt"][:, kc, nh * 512:(nh + 1) * 512],
                                start=(kc == 0), stop=(kc == 7))
                        return last
                    eng, v = nxt()

                    def evacv(e, p, s=s, nh=nh, kslot=kslot, eng=eng):
                        return cp(e, eng,
                                  vts[kslot][:, s, nh * 512:(nh + 1) * 512],
                                  p[:])
                    mm_fill(fillv, evacv, [(sS, ht)] + kv_gate +
                            ([(dw["wvt"], 16)] if j == 0 else []),
                            evac_eng=eng, vis=v)
            kvslot_read[kslot] = []
            proj_done_v[j] = [(vS, vS.n), (sS, sS.n)]

        def kt_of(k):
            ob = (k >> 1) - 1
            return ob % 3, k % 2

        sc_i = [0]

        def seg_D(j):
            for qrel in range(2):
                qq = 2 * j + qrel
                for c in range(3):
                    k = qq + c
                    kslot_c, half = kt_of(k)
                    pslot = c
                    pgate = list(probs_gate[pslot])
                    for u4 in range(4):
                        g, par_u = u4 >> 1, u4 & 1
                        h0 = 8 * g + par_u
                        bi = sc_i[0] % 3
                        sc_i[0] += 1
                        gate = list(sc_gate[bi])
                        mname = None
                        if c == 0:
                            mname = ("m_q0_c0" if qq <= 1 else "m_c0")
                        elif c == 1 and qq == 0:
                            mname = "m_q0_c1"
                        elif c == 2:
                            mname = "m_c2"

                        def fill(e, h0=h0, par_u=par_u, kslot_c=kslot_c,
                                 half=half, qrel=qrel, bi=bi, mname=mname):
                            last = None
                            po = 64 * par_u
                            for i in range(4):
                                h = h0 + 2 * i
                                lhs = kts[kslot_c][po:po + 64, h // 2,
                                                   half * 128:half * 128 + 128]
                                rhs = qt_t[po:po + 64, h // 2,
                                           qrel * 128:qrel * 128 + 128]
                                last = e.matmul(
                                    sc[bi][:, i * 128:(i + 1) * 128],
                                    lhs, rhs, start=(i == 0),
                                    stop=(i == 3 and mname is None))
                            if mname is not None:
                                last = e.matmul(sc[bi][:], idt[:],
                                                msk[mname][:],
                                                start=False, stop=True)
                            return last
                        w_extra = (list(proj_done_v[j])
                                   if (qrel == 0 and c == 0 and u4 == 0)
                                   else [])
                        sch.op("tensor", fill, waits=gate + w_extra, inc=tS)
                        f_done = tS.n

                        def expf(e, bi=bi, pslot=pslot, h0=h0, g=g):
                            return e.activation(
                                probs[pslot][:, h0:8 * g + 8:2, :],
                                sc[bi][:].rearrange("p (a n) -> p a n", a=4),
                                AF.Exp, bias=ESHIFT, scale=0.125)
                        sch.op("scalar", expf,
                               waits=[(tS, f_done), (vS, V_CONST)] + pgate,
                               inc=sS, vis=(c == 2 and u4 == 3))
                        exp_count[(qq, c, u4)] = sS.n
                        sc_gate[bi] = [(sS, sS.n)]
                        pgate = []

                pv_w = (list(pj_gate[2]) + list(pj_gate[3]) +
                        list(pvl_gate) +
                        [(sS, exp_count[(qq, 2, 3)]), (vS, V_CONST)])

                def pvfill(e, qq=qq):
                    last = None
                    for c in range(3):
                        k = qq + c
                        kslot_c, half = kt_of(k)
                        for h in range(16):
                            lhs = probs[c][:, h, :]
                            rhs = vts[kslot_c][:, half, h * 64:(h + 1) * 64]
                            last = e.matmul(
                                pv[h // 8][:, (h % 8) * 64:(h % 8 + 1) * 64],
                                lhs, rhs, start=(c == 0 and h % 8 == 0),
                                stop=(c == 2 and (h % 8) == 7))
                            last = e.matmul(
                                pvl[:, h:h + 1], lhs, ones[:],
                                start=(c == 0 and h == 0),
                                stop=(c == 2 and h == 15))
                    return last
                sch.op("tensor", pvfill, waits=pv_w, inc=tS)
                pv_done[qq] = tS.n
                for c in range(3):
                    probs_gate[c] = [(tS, pv_done[qq])]
                for b in range(3):
                    kvslot_read[kt_of(qq + b)[0]] = [(tS, pv_done[qq])]

                agate = list(a_read[qrel])

                def norm(e, qrel=qrel):
                    e.reciprocal(recip[:], pvl[:])
                    e.drain()
                    last = None
                    for h in range(16):
                        last = e.tensor_scalar(
                            a_t[qrel][:, h * 64:(h + 1) * 64],
                            pv[h // 8][:, (h % 8) * 64:(h % 8 + 1) * 64],
                            recip[:, h:h + 1], None, ALU.mult)
                    return last
                sch.op("vector", norm, waits=[(tS, pv_done[qq])] + agate,
                       inc=vS)
                pj_gate[2] = [(vS, vS.n)]
                pj_gate[3] = [(vS, vS.n)]
                pvl_gate.clear()
                pvl_gate.append((vS, vS.n))
                a_read[qrel] = []
                a_v.setdefault(j, [None, None])[qrel] = vS.n

        def seg_E(j):
            atgate = [(tS, at_read[-1])] if at_read else []
            at_done = transposes(
                lambda s, fc: a_t[s][:, fc * 128:(fc + 1) * 128], at_t,
                lambda s: s * 128, TGROUPS,
                lambda s, j=j: [(vS, a_v[j][s])] + atgate)
            for s in range(2):
                a_read[s] = [(tS, tS.n)]
            y1gate = list(y1_read)
            del y1_read[:]
            for s in range(2):
                for nh in range(2):
                    def fillo(e, p, s=s, nh=nh):
                        last = None
                        for kc in range(8):
                            last = e.matmul(
                                p[:], at_t[:, kc, s * 128:(s + 1) * 128],
                                wsb["wot"][:, kc, nh * 512:(nh + 1) * 512],
                                start=(kc == 0), stop=(kc == 7))
                        return last

                    def evaco(e, p, s=s, nh=nh):
                        return e.copy(
                            y1sb[:, s, nh * 512:(nh + 1) * 512], p[:])
                    mm_fill(fillo, evaco,
                            [(sS, at_done)] + y1gate +
                            ([(dw["wot"], 16)] if j == 0 else []),
                            evac_eng="scalar", vis=(s == 1 and nh == 1))
                    y1gate = []
            at_read.append(tS.n)
            y1_evac[j] = sS.n

        def seg_F(j):
            par = j % 2
            g_guard = ([(sS, ln4_sq[j - 1]), (dO, 16 * j)]
                       if j > 0 else [])
            ln_stats_apply(par, y1sb, [(sS, y1_evac[j])], [], gsb, False,
                           out_guard=g_guard, apply_vis=False)
            xslot = (j + 1) % 3
            x_guard = [(sS, x2_sq[j - 1])] if j > 0 else []
            for t in range(2):
                sch.op("vector", lambda e, t=t, xslot=xslot: e.tensor_add(
                    x2[:, t, :], xs[xslot][:, t, :], gsb[:, t, :]),
                    waits=x_guard if t == 0 else (), inc=vS, vis=(t == 1))
            resadd_v[j] = vS.n
            if j + 3 < NB:
                xv[j + 3] = sched_x_dma(j + 3)
            av3, sq3 = ln_stats_apply(par, x2, [], [(vS, resadd_v[j])],
                                      hln3_out, True,
                                      out_guard=[(tS, at_read[-1])])
            applies3[j] = av3
            x2_sq[j] = sq3

        def seg_G(j):
            h2t_done = transposes(
                hln3_src, hts2, lambda s: s * 128, TGROUPS,
                lambda s, j=j: [(vS, applies3[j][s])])
            for s in range(2):
                for nh in range(2):
                    def fillf(e, p, s=s, nh=nh):
                        last = None
                        for kc in range(8):
                            last = e.matmul(
                                p[:], hts2[:, kc, s * 128:(s + 1) * 128],
                                wsb["cbt"][:, kc, nh * 512:(nh + 1) * 512],
                                start=(kc == 0), stop=(kc == 7))
                        return last

                    def evacf(e, p, s=s, nh=nh):
                        return e.tensor_mul(
                            gsb[:, s, nh * 512:(nh + 1) * 512], p[:],
                            scbrep[:, nh * 512:(nh + 1) * 512])
                    mm_fill(fillf, evacf,
                            [(sS, h2t_done), (dsm, SMALLS)] +
                            ([(dw["cbt"], 16)] if j == 0 else []))

        def seg_H(j):
            par = j % 2
            for t in range(2):
                sch.op("vector", lambda e, t=t: e.tensor_add(
                    gsb[:, t, :], gsb[:, t, :], biasrep[:]),
                    waits=[(dsm, SMALLS)] if (j == 0 and t == 0) else (),
                    inc=vS, vis=(t == 1))
            g_ready_v = vS.n
            _, sq4 = ln_stats_apply(par, gsb, [], [(vS, g_ready_v)], y1sb,
                                    False, apply_vis=False)
            ln4_sq[j] = sq4
            outw = ([(dO, 16 * j)] if j > 0 else []) + [(sS, ln4_sq[j])]
            for t in range(2):
                sch.op("vector", lambda e, t=t: e.tensor_add(
                    outt[:, t, :], x2[:, t, :], y1sb[:, t, :]),
                    waits=outw if t == 0 else (), inc=vS, vis=(t == 1))
            outadd_v[j] = vS.n
            y1_read.append((vS, vS.n))
            sched_out_dma(j)

        seg_A(0)
        seg_B(0)
        seg_C(0)
        for j in range(NB):
            seg_D(j)
            if j + 1 < NB:
                seg_A(j + 1)
            seg_E(j)
            seg_F(j)
            if j + 1 < NB:
                seg_B(j + 1)
                seg_C(j + 1)
            seg_G(j)
            seg_H(j)

        if DBG_PHASE < 99:
            tv, vv, sv2 = tS.n, vS.n, sS.n

            def _cleanup(e):
                for n in dw:
                    e.wait_ge(dw[n].sem, 16)
                e.wait_ge(dsm.sem, SMALLS)
                for i in range(3):
                    if dx[i].n:
                        e.wait_ge(dx[i].sem, dx[i].n)
                if tv:
                    e.wait_ge(tS.sem, tv)
                if vv:
                    e.wait_ge(vS.sem, vv)
                if sv2:
                    e.wait_ge(sS.sem, sv2)
                for jj in range(NB):
                    dst = out_h[jj * 256:(jj + 1) * 256, :].rearrange(
                        "(a p) n -> p a n", p=128)
                    e.dma_start(out=dst, in_=outt[:]).then_inc(dO.sem, 16)
            sch.op("sync", _cleanup)
            dO.n = 16 * NB

        if DBG_DUMP:
            dbg_src = {"d_hln": hln, "d_hts": hts, "d_qt": qt_t,
                       "d_kt0": kts[0], "d_vt0": vts[0], "d_kt2": kts[2],
                       "d_vt2": vts[2], "d_pb0": probs[0], "d_pb1": probs[1],
                       "d_pb2": probs[2], "d_a0": a_t[0], "d_a1": a_t[1],
                       "d_at": at_t, "d_y1": y1sb, "d_g": gsb, "d_x2": x2}
            tv, vv, sv2 = tS.n, vS.n, sS.n
            dov = dO.n
            dcvt = outt[:].rearrange("p a n -> p (a n)")

            def _dump(e):
                e.wait_ge(tS.sem, tv)
                e.wait_ge(vS.sem, vv)
                e.wait_ge(sS.sem, sv2)
                e.wait_ge(dO.sem, dov)
                e.tensor_copy(dcvt[:, 0:2], st_sum[0][:])
                e.tensor_copy(dcvt[:, 2:4], st_sq[0][:])
                e.tensor_copy(dcvt[:, 4:6], st_mean[0][:])
                e.tensor_copy(dcvt[:, 6:8], st_ve[0][:])
                e.tensor_copy(dcvt[:, 8:10], st_rstd[0][:])
                return e.engine_nop()
            sch.op("vector", _dump, inc=vS)
            sch.op("sync", lambda e, v=vS.n: e.wait_ge(vS.sem, v))
            sch.op("sync", lambda e: e.dma_start(
                out=dbg_h["d_sums"][:], in_=dcvt[:, 0:16]).then_inc(dO.sem, 16))
            dO.n += 16
            dcur0 = dO.n
            sch.op("vector", lambda e, d=dcur0: e.wait_ge(dO.sem, d))
            for n in ["d_hln", "d_hts", "d_qt", "d_kt0", "d_vt0", "d_kt2",
                      "d_vt2", "d_pb0", "d_pb1", "d_pb2", "d_a0", "d_a1",
                      "d_at", "d_y1", "d_g", "d_x2"]:
                srcap = dbg_src[n]
                nel = 1024 if n in ("d_a0", "d_a1") else 2048

                def _cvt(e, srcap=srcap, nel=nel):
                    flat = srcap[:].rearrange("p ... -> p (...)") \
                        if srcap[:].ndim > 2 else srcap[:]
                    return e.tensor_copy(dcvt[:, 0:nel], flat)
                sch.op("vector", _cvt, inc=vS)
                vcur = vS.n

                def _dma(e, n=n, nel=nel, vcur=vcur):
                    e.wait_ge(vS.sem, vcur)
                    return e.dma_start(out=dbg_h[n][:, 0:nel],
                                       in_=dcvt[:, 0:nel]).then_inc(dO.sem, 16)
                sch.op("sync", _dma)
                dO.n += 16
                dcur = dO.n
                sch.op("vector", lambda e, dcur=dcur: e.wait_ge(dO.sem, dcur))

        sch.op("sync", lambda e: e.wait_ge(dO.sem, dO.n if DBG_DUMP else 16 * NB))

        block = ctx.enter_context(nc.Block())

        @block.sync
        def _(eng):
            sch.emit("sync", eng)

        @block.gpsimd
        def _(eng):
            sch.emit("gpsimd", eng)

        @block.vector
        def _(eng):
            sch.emit("vector", eng)

        @block.scalar
        def _(eng):
            sch.emit("scalar", eng)

        @block.tensor
        def _(eng):
            sch.emit("tensor", eng)

    return nc


_NC_CACHE = {}


def _get_program():
    if "nc" not in _NC_CACHE:
        _NC_CACHE["nc"] = build_program()
    return _NC_CACHE["nc"]


def _masks_for_core(core):
    i = np.arange(128)
    m = np.arange(128)[:, None]
    c0 = np.where(m > i[None, :], 0.0, MASKVAL).astype(np.float32)
    c2 = np.where(m <= i[None, :], 0.0, MASKVAL).astype(np.float32)
    full = np.full((128, 128), MASKVAL, np.float32)
    zero = np.zeros((128, 128), np.float32)
    if core == 0:
        q0c0, q0c1, q1c0 = full, full, full
    else:
        q0c0, q0c1, q1c0 = c0, zero, c0
    tile = lambda t: np.tile(t, (1, 4)).astype(ml_dtypes.bfloat16)
    return {"m_c0": tile(c0), "m_c2": tile(c2), "m_q0_c0": tile(q0c0),
            "m_q0_c1": tile(q0c1)}


def kernel(L0_attn_x, L0_attn_wq, L0_attn_wk, L0_attn_wv, L0_attn_wo,
           L0_ffn_CB, L0_ffn_SCB, L0_ffn_bias):
    x = np.asarray(L0_attn_x, np.float32).reshape(16384, 1024)
    wts = {"wqt": L0_attn_wq, "wkt": L0_attn_wk, "wvt": L0_attn_wv,
           "wot": L0_attn_wo, "cbt": L0_ffn_CB}
    wts = {n: np.ascontiguousarray(np.asarray(w, np.float32).T)
           for n, w in wts.items()}
    scb = np.tile(np.asarray(L0_ffn_SCB, np.float32).reshape(1, 1024),
                  (128, 1))
    bias = np.tile(np.asarray(L0_ffn_bias, np.float32).reshape(1, 1024),
                   (128, 1))
    ident = np.eye(128, dtype=ml_dtypes.bfloat16)

    nc = _get_program()
    in_maps = []
    for c in range(8):
        lo = c * 2048
        xs = np.empty((2304, 1024), np.float32)
        if c == 0:
            xs[0:256] = 0.0
        else:
            xs[0:256] = x[lo - 256:lo]
        xs[256:] = x[lo:lo + 2048]
        m = {"x": xs, "scb": scb, "bias": bias, "ident": ident}
        m.update(wts)
        m.update(_masks_for_core(c))
        in_maps.append(m)

    res = run_bass_kernel_spmd(nc, in_maps, core_ids=list(range(8)))
    out = np.concatenate([res.results[c]["out"] for c in range(8)], axis=0)
    return out.reshape(1, 16384, 1024)


# version 17
# speedup vs baseline: 1.0337x; 1.0255x over previous
import numpy as np
import ml_dtypes
from contextlib import ExitStack

import concourse.bass as bass
import concourse.mybir as mybir
from concourse.bass_utils import run_bass_kernel_spmd

F32 = mybir.dt.float32
BF16 = mybir.dt.bfloat16
AF = mybir.ActivationFunctionType
ALU = mybir.AluOpType
AX = mybir.AxisListType

NB = 8
DBG_PHASE = 99
DBG_DUMP = False
USE_TNOP = True
USE_DRAIN = True
EPS = 1e-5
ESHIFT = -8.0
MASKVAL = -30000.0


class Ctr:
    def __init__(self, sem):
        self.sem = sem
        self.n = 0


class Sched:

    ENGINES = ("sync", "gpsimd", "vector", "scalar", "tensor")

    def __init__(self):
        self.q = {e: [] for e in self.ENGINES}
        self.relay = None

    def op(self, eng, builder, waits=(), inc=None, inck=1, vis=True):
        w = [(c, v) for (c, v) in waits if v > 0]
        if (eng == "tensor" and inc is not None and self.relay is not None
                and inc is self.relay[1]):
            priv, pub = self.relay
            priv.n += inck
            pv = priv.n
            self.q[eng].append((builder, w, priv, inck))
            pub.n += inck

            def relay_fn(e, pv=pv, inck=inck, pub=pub):
                e.wait_ge(priv.sem, pv)
                e.sem_inc(pub.sem, inck)
            self.q["gpsimd"].append((relay_fn, [], None, 0))
            return pub.n
        if inc is not None and vis and eng in ("vector", "scalar"):
            self.q[eng].append((builder, w, None, 0))
            inc.n += inck
            self.q[eng].append(((lambda e: e.drain()), [], inc, inck))
            return inc.n
        if inc is not None:
            inc.n += inck
        self.q[eng].append((builder, w, inc, inck))
        return None if inc is None else inc.n

    def emit(self, eng_name, eng):
        for builder, waits, inc, inck in self.q[eng_name]:
            for c, v in waits:
                eng.wait_ge(c.sem, v)
            inst = builder(eng)
            if inc is not None and inst is not None:
                inst.then_inc(inc.sem, inck)


def build_program(mk=None):
    factory = mk if mk is not None else bass.Bass
    nc = factory("TRN2", target_bir_lowering=False,
                 detect_race_conditions=False)

    x_h = nc.dram_tensor("x", [2304, 1024], F32, kind="ExternalInput")
    w_h = {n: nc.dram_tensor(n, [1024, 1024], F32, kind="ExternalInput")
           for n in ["wqt", "wkt", "wvt", "wot", "cbt"]}
    scb_h = nc.dram_tensor("scb", [128, 1024], F32, kind="ExternalInput")
    bias_h = nc.dram_tensor("bias", [128, 1024], F32, kind="ExternalInput")
    id_h = nc.dram_tensor("ident", [128, 128], BF16, kind="ExternalInput")
    mask_h = {n: nc.dram_tensor(n, [128, 512], BF16, kind="ExternalInput")
              for n in ["m_c0", "m_c2", "m_q0_c0", "m_q0_c1"]}
    out_h = nc.dram_tensor("out", [2048, 1024], F32, kind="ExternalOutput")
    dbg_h = {}
    if DBG_DUMP:
        for n, shp, dt in [("d_hln", [128, 2048], F32), ("d_hts", [128, 2048], F32),
                           ("d_qt", [128, 2048], F32), ("d_kt0", [128, 2048], F32),
                           ("d_vt0", [128, 2048], F32), ("d_kt2", [128, 2048], F32),
                           ("d_vt2", [128, 2048], F32),
                           ("d_pb0", [128, 2048], F32), ("d_pb1", [128, 2048], F32),
                           ("d_pb2", [128, 2048], F32), ("d_a0", [128, 1024], F32),
                           ("d_a1", [128, 1024], F32), ("d_at", [128, 2048], F32),
                           ("d_y1", [128, 2048], F32), ("d_g", [128, 2048], F32),
                           ("d_x2", [128, 2048], F32), ("d_sums", [128, 16], F32)]:
            dbg_h[n] = nc.dram_tensor(n, shp, dt, kind="ExternalOutput")

    ctx = ExitStack()
    with ctx:
        S = lambda *a: ctx.enter_context(nc.sbuf_tensor(*a))
        P = lambda *a: ctx.enter_context(nc.psum_tensor(*a))
        SEM = lambda n: Ctr(ctx.enter_context(nc.semaphore(n)))

        wsb = {n: S(n + "s", [128, 8, 1024], BF16) for n in w_h}
        xs = [S(f"xr{i}", [128, 2, 1024], F32) for i in range(3)]
        x2 = S("x2", [128, 2, 1024], F32)
        hln = S("hln", [128, 2, 1024], BF16)
        hts = S("hts", [128, 8, 256], BF16)
        hts2 = S("hts2", [128, 8, 256], BF16)
        qt_t = S("qt", [128, 8, 256], BF16)
        kts = [S(f"kt{i}", [128, 8, 256], BF16) for i in range(3)]
        vts = [S(f"vt{i}", [128, 2, 1024], BF16) for i in range(3)]
        probs = [S(f"pb{i}", [128, 16, 128], BF16) for i in range(3)]
        a_t = [S(f"a{i}", [128, 1024], BF16) for i in range(2)]
        at_t = S("at", [128, 8, 256], BF16)
        y1sb = S("y1sb", [128, 2, 1024], F32)
        gsb = S("gsb", [128, 2, 1024], F32)
        outt = gsb
        sqf = S("sqf", [128, 1024], BF16)
        scbrep = S("screp", [128, 1024], F32)
        biasrep = S("birep", [128, 1024], F32)
        idt = S("idt", [128, 128], BF16)
        msk = {n: S("s" + n, [128, 512], BF16) for n in mask_h}
        ones = S("ones", [128, 1], BF16)
        csb = S("csb", [128, 4], F32)
        recip = S("recip", [128, 16], F32)
        st_sum = [S(f"tsum{i}", [128, 2], F32) for i in range(2)]
        st_sq = [S(f"tsq{i}", [128, 2], F32) for i in range(2)]
        st_mean = [S(f"tmean{i}", [128, 2], F32) for i in range(2)]
        st_m2 = [S(f"tm2{i}", [128, 2], F32) for i in range(2)]
        st_ve = [S(f"tve{i}", [128, 2], F32) for i in range(2)]
        st_rstd = [S(f"trstd{i}", [128, 2], F32) for i in range(2)]

        nc.const_aps.register(F32, EPS, csb[:, 0:1])
        nc.const_aps.register(F32, -0.5, csb[:, 1:2])
        nc.const_aps.register(F32, 0.125, csb[:, 2:3])
        nc.const_aps.register(F32, ESHIFT, csb[:, 3:4])

        sc = [P(f"sc{i}", [128, 512], F32) for i in range(3)]
        pv = [P("pv0", [128, 512], F32), P("pv1", [128, 512], F32)]
        pj = [P("pj0", [128, 512], F32), P("pj1", [128, 512], F32),
              pv[0], pv[1]]
        pvl = P("pvl", [128, 16], F32)

        dx = [SEM(f"dx{i}") for i in range(3)]
        dw = {n: SEM("d" + n) for n in w_h}
        dsm = SEM("dsm")
        dO = SEM("dO")
        tS = SEM("tS")
        tP = SEM("tP")
        vS = SEM("vS")
        sS = SEM("sS")

        PH = 5 if DBG_PHASE in (51, 52) else DBG_PHASE
        sch = Sched()
        if USE_TNOP:
            sch.relay = (tP, tS)
        last_ve = {0: 0, 1: 0}
        sqf_free = [0]
        pj_gate = [[], [], [], []]
        sc_gate = [[], [], []]
        pvl_gate = []
        pj_i = 0
        first_tensor = [True]

        def _smalls(eng):
            dsm_sem = dsm.sem
            eng.dma_start(out=scbrep[:], in_=scb_h[:]).then_inc(dsm_sem, 16)
            eng.dma_start(out=biasrep[:], in_=bias_h[:]).then_inc(dsm_sem, 16)
            eng.dma_start(out=idt[:], in_=id_h[:]).then_inc(dsm_sem, 16)
            for n in mask_h:
                eng.dma_start(out=msk[n][:], in_=mask_h[n][:]).then_inc(dsm_sem, 16)
        sch.op("sync", _smalls)
        dsm.n = 16 * (3 + len(mask_h))
        SMALLS = dsm.n

        def _weights(eng):
            for n in ["wkt", "wvt", "wqt", "wot", "cbt"]:
                src = w_h[n][:].rearrange("(kc p) n -> p kc n", p=128)
                eng.dma_start(out=wsb[n][:], in_=src).then_inc(dw[n].sem, 16)
        sch.op("gpsimd", _weights)
        for n in dw:
            dw[n].n = 16


        def _vconsts(eng):
            eng.memset(csb[:, 0:1], EPS)
            eng.memset(csb[:, 1:2], -0.5)
            eng.memset(csb[:, 2:3], 0.125)
            eng.memset(csb[:, 3:4], ESHIFT)
            return eng.memset(ones[:], 1.0)
        sch.op("vector", _vconsts, inc=vS)
        V_CONST = vS.n
        sch.op("vector", lambda e: e.tensor_scalar(
            scbrep[:], scbrep[:], 1.0 / 127.0, None, ALU.mult),
            waits=[(dsm, SMALLS)], inc=vS)
        V_SCB = vS.n

        def ln_stats_apply(par, src, waits_v, waits_s, out_tile, out_is_bf,
                           out_guard=(), apply_vis=True):
            sch.op("scalar", lambda e: e.activation(
                sqf[:], src[:, 0, :], AF.Square),
                waits=list(waits_s) + [(vS, sqf_free[0])])
            sch.op("scalar", lambda e: e.drain(), inc=sS, vis=False)
            sq0 = sS.n
            for t in range(2):
                sch.op("vector", lambda e, t=t: e.reduce_sum(
                    st_sum[par][:, t:t + 1], src[:, t, :], AX.X),
                    waits=waits_v if t == 0 else ())
            sch.op("vector", lambda e, par=par: e.reduce_sum(
                st_sq[par][:, 0:1], sqf[:], AX.X),
                waits=[(sS, sq0), (vS, last_ve[par])], inc=vS, vis=False)
            rd0 = vS.n
            sch.op("scalar", lambda e: e.activation(
                sqf[:], src[:, 1, :], AF.Square), waits=[(vS, rd0)])
            sch.op("scalar", lambda e: e.drain(), inc=sS, vis=False)
            sq1 = sS.n
            sch.op("vector", lambda e, par=par: e.reduce_sum(
                st_sq[par][:, 1:2], sqf[:], AX.X), waits=[(sS, sq1)],
                inc=vS, vis=False)
            sqf_free[0] = vS.n
            sch.op("vector", lambda e: e.drain())
            sch.op("vector", lambda e: e.tensor_scalar(
                st_mean[par][:], st_sum[par][:], 1.0 / 1024.0, None, ALU.mult))
            sch.op("vector", lambda e: e.drain())
            sch.op("vector", lambda e: e.tensor_mul(
                st_m2[par][:], st_mean[par][:], st_mean[par][:]))
            sch.op("vector", lambda e: e.drain())
            sch.op("vector", lambda e: e.tensor_scalar(
                st_ve[par][:], st_sq[par][:], 1.0 / 1024.0, None, ALU.mult))
            sch.op("vector", lambda e: e.drain())
            sch.op("vector", lambda e: e.tensor_sub(
                st_ve[par][:], st_ve[par][:], st_m2[par][:]), inc=vS)
            ve_done = vS.n
            last_ve[par] = ve_done
            sch.op("scalar", lambda e: e.activation(
                st_m2[par][:], st_ve[par][:], AF.Ln, bias=EPS),
                waits=[(vS, ve_done), (vS, V_CONST)])
            sch.op("scalar", lambda e: e.activation(
                st_rstd[par][:], st_m2[par][:], AF.Exp, scale=-0.5))
            sch.op("scalar", lambda e: e.drain(), inc=sS, vis=False)
            rstd_done = sS.n
            av = []
            ot = out_tile if callable(out_tile) else (
                lambda t, o=out_tile: o[:, t, :])
            for t in range(2):
                sch.op("vector", lambda e, t=t: e.tensor_scalar(
                    ot(t), src[:, t, :], st_mean[par][:, t:t + 1],
                    None, ALU.subtract),
                    waits=([(sS, rstd_done)] + list(out_guard))
                    if t == 0 else ())
                sch.op("vector", lambda e, t=t: e.tensor_scalar(
                    ot(t), ot(t),
                    st_rstd[par][:, t:t + 1], None, ALU.mult), inc=vS,
                    vis=apply_vis)
                av.append(vS.n)
            return av, sq1

        def transposes(src_fn, dst, dst_col, groups, waits0):
            nonlocal pj_i
            for (s, fcs) in groups:
                b = pj_i % 4
                pj_i += 1
                gate = list(pj_gate[b])

                def fill(e, s=s, fcs=fcs, b=b):
                    last = None
                    for i, fc in enumerate(fcs):
                        last = e.matmul(pj[b][:, i * 128:(i + 1) * 128],
                                        src_fn(s, fc), idt[:],
                                        start=(i == 0), stop=(i == len(fcs) - 1))
                    return last
                w = gate + (waits0(s) if callable(waits0) else list(waits0))
                if first_tensor[0]:
                    w = w + [(dsm, SMALLS)]
                    first_tensor[0] = False
                sch.op("tensor", fill, waits=w, inc=tS)
                fill_done = tS.n
                c0 = dst_col(s)
                sch.op("scalar", lambda e, fcs=fcs, b=b, c0=c0: e.copy(
                    dst[:, fcs[0]:fcs[0] + len(fcs), c0:c0 + 128],
                    pj[b][:].rearrange("p (a n) -> p a n", a=len(fcs))),
                    waits=[(tS, fill_done)], inc=sS,
                    vis=((s, fcs) == groups[-1]))
                pj_gate[b] = [(sS, sS.n)]
            return sS.n

        def proj_fill(wname, nc0, rhs_ap, extra_waits, out_dst, vis=False):
            nonlocal pj_i
            b = pj_i % 4
            pj_i += 1
            gate = list(pj_gate[b])

            def fill(e, b=b):
                last = None
                for ni in range(2):
                    for kc in range(8):
                        lhs = wsb[wname][:, kc, (nc0 + ni) * 128:
                                         (nc0 + ni + 1) * 128]
                        rhs = rhs_ap(kc)
                        o = pj[b][:, ni * 256:(ni + 1) * 256]
                        last = e.matmul(o, lhs, rhs,
                                        start=(ni == 0 and kc == 0),
                                        stop=(ni == 1 and kc == 7))
                return last
            sch.op("tensor", fill, waits=gate + extra_waits, inc=tS)
            fd = tS.n
            sch.op("vector", lambda e, b=b: out_dst(e, pj[b]),
                   waits=[(tS, fd)], inc=vS, vis=vis)
            pj_gate[b] = [(vS, vS.n)]
            return vS.n

        def mm_fill(fill_builder, evac_builder, extra_waits, evac_eng="vector",
                    evac_ctr=None, vis=False):
            nonlocal pj_i
            b = pj_i % 4
            pj_i += 1
            gate = list(pj_gate[b])
            sch.op("tensor", lambda e, b=b: fill_builder(e, pj[b]),
                   waits=gate + extra_waits, inc=tS)
            fd = tS.n
            ec = evac_ctr or (vS if evac_eng == "vector" else sS)
            sch.op(evac_eng, lambda e, b=b: evac_builder(e, pj[b]),
                   waits=[(tS, fd)], inc=ec, vis=vis)
            pj_gate[b] = [(ec, ec.n)]
            return ec.n

        x_sync_extra = {}
        resadd_v = {}
        outadd_v = {}
        sq_reads_x = {}

        def sched_x_dma(j):
            slot = (j + 1) % 3
            waits = []
            if j - 3 >= -1 and (j - 3) in resadd_v:
                waits = [(vS, resadd_v[j - 3]), (sS, sq_reads_x[j - 3])]
            row0 = (j + 1) * 256
            src = x_h[row0:row0 + 256, :].rearrange("(a p) n -> p a n", p=128)
            sch.op("sync", lambda e, slot=slot: e.dma_start(
                out=xs[slot][:], in_=src), waits=waits, inc=dx[slot], inck=16)
            return dx[slot].n

        def sched_out_dma(j):
            dst = out_h[j * 256:(j + 1) * 256, :].rearrange(
                "(a p) n -> p a n", p=128)
            sch.op("sync", lambda e: e.dma_start(out=dst, in_=outt[:]),
                   waits=[(vS, outadd_v[j])], inc=dO, inck=16)

        probs_gate = [[], [], []]
        kvslot_read = {0: [], 1: [], 2: []}
        a_read = [[], []]
        at_read = []
        hln_read = []
        y1_read = []
        exp_count = {}
        pv_done = {}
        applies1 = {}
        ht1_done = {}
        proj_done_v = {}
        a_v = {}
        y1_evac = {}
        applies3 = {}
        x2_sq = {}
        ln4_sq = {}

        xval = sched_x_dma(-1)
        sq_reads_x[-1] = None
        applies_h, _ = ln_stats_apply(1, xs[0], [(dx[0], xval)],
                                      [(dx[0], xval)], hln, True)
        sq_reads_x[-1] = sS.n
        resadd_v[-1] = applies_h[1]
        TGROUPS = [(s, [g * 4 + i for i in range(4)]) for s in range(2)
                   for g in range(2)]
        ht_halo = transposes(
            lambda s, fc: hln[:, s, fc * 128:(fc + 1) * 128], hts,
            lambda s: s * 128, TGROUPS,
            lambda s: [(vS, applies_h[s])])
        hln_read.append(tS.n)
        for nc0 in (0, 2, 4, 6):
            proj_fill("wkt", nc0, lambda kc: hts[:, kc, :],
                      [(sS, ht_halo), (dw["wkt"], 16)],
                      lambda e, p, nc0=nc0: e.tensor_copy(
                          kts[2][:, nc0:nc0 + 2, :],
                          p[:].rearrange("p (a n) -> p a n", a=2)))
        for s in range(2):
            for nh in range(2):
                def fill(e, p, s=s, nh=nh):
                    last = None
                    for kc in range(8):
                        last = e.matmul(
                            p[:], hts[:, kc, s * 128:(s + 1) * 128],
                            wsb["wvt"][:, kc, nh * 512:(nh + 1) * 512],
                            start=(kc == 0), stop=(kc == 7))
                    return last

                def evac(e, p, s=s, nh=nh):
                    return e.tensor_copy(
                        vts[2][:, s, nh * 512:(nh + 1) * 512], p[:])
                mm_fill(fill, evac, [(sS, ht_halo), (dw["wvt"], 16)])

        xv = {-1: xval}
        for j in range(min(3, NB)):
            xv[j] = sched_x_dma(j)

        def hln3_out(t):
            return at_t[:, 4 * t:4 * t + 4, :].rearrange("p a n -> p (a n)")

        def hln3_src(s, fc):
            o = s * 1024 + fc * 128
            return at_t[:, o // 256, o % 256:o % 256 + 128]

        def seg_A(j):
            xslot = (j + 1) % 3
            w = [(dx[xslot], xv[j])]
            hg = [(tS, hln_read[-1])] if hln_read else []
            av, _ = ln_stats_apply(j % 2, xs[xslot], list(w), list(w), hln,
                                   True, out_guard=hg)
            applies1[j] = av
            sq_reads_x[j] = sS.n

        def seg_B(j):
            ht1_done[j] = transposes(
                lambda s, fc: hln[:, s, fc * 128:(fc + 1) * 128], hts,
                lambda s: s * 128, TGROUPS,
                lambda s, j=j: [(vS, applies1[j][s])])
            hln_read.append(tS.n)

        def seg_C(j):
            kslot = j % 3
            kv_gate = list(kvslot_read[kslot])
            ht = ht1_done[j]
            for nc0 in (0, 2, 4, 6):
                proj_fill("wqt", nc0, lambda kc: hts[:, kc, :],
                          [(sS, ht)] + ([(dw["wqt"], 16)] if j == 0 else []),
                          lambda e, p, nc0=nc0: e.tensor_copy(
                              qt_t[:, nc0:nc0 + 2, :],
                              p[:].rearrange("p (a n) -> p a n", a=2)))
            for nc0 in (0, 2, 4, 6):
                proj_fill("wkt", nc0, lambda kc: hts[:, kc, :],
                          [(sS, ht)] + kv_gate +
                          ([(dw["wkt"], 16)] if j == 0 else []),
                          lambda e, p, nc0=nc0, kslot=kslot: e.tensor_copy(
                              kts[kslot][:, nc0:nc0 + 2, :],
                              p[:].rearrange("p (a n) -> p a n", a=2)))
            for s in range(2):
                for nh in range(2):
                    def fillv(e, p, s=s, nh=nh):
                        last = None
                        for kc in range(8):
                            last = e.matmul(
                                p[:], hts[:, kc, s * 128:(s + 1) * 128],
                                wsb["wvt"][:, kc, nh * 512:(nh + 1) * 512],
                                start=(kc == 0), stop=(kc == 7))
                        return last

                    def evacv(e, p, s=s, nh=nh, kslot=kslot):
                        return e.tensor_copy(
                            vts[kslot][:, s, nh * 512:(nh + 1) * 512], p[:])
                    mm_fill(fillv, evacv, [(sS, ht)] + kv_gate +
                            ([(dw["wvt"], 16)] if j == 0 else []),
                            vis=(s == 1 and nh == 1))
            kvslot_read[kslot] = []
            proj_done_v[j] = vS.n

        def kt_of(k):
            ob = (k >> 1) - 1
            return ob % 3, k % 2

        sc_i = [0]

        def seg_D(j):
            for qrel in range(2):
                qq = 2 * j + qrel
                for c in range(3):
                    k = qq + c
                    kslot_c, half = kt_of(k)
                    pslot = c
                    pgate = list(probs_gate[pslot])
                    for u4 in range(4):
                        g, par_u = u4 >> 1, u4 & 1
                        h0 = 8 * g + par_u
                        bi = sc_i[0] % 3
                        sc_i[0] += 1
                        gate = list(sc_gate[bi])
                        mname = None
                        if c == 0:
                            mname = ("m_q0_c0" if qq <= 1 else "m_c0")
                        elif c == 1 and qq == 0:
                            mname = "m_q0_c1"
                        elif c == 2:
                            mname = "m_c2"

                        def fill(e, h0=h0, par_u=par_u, kslot_c=kslot_c,
                                 half=half, qrel=qrel, bi=bi, mname=mname):
                            last = None
                            po = 64 * par_u
                            for i in range(4):
                                h = h0 + 2 * i
                                lhs = kts[kslot_c][po:po + 64, h // 2,
                                                   half * 128:half * 128 + 128]
                                rhs = qt_t[po:po + 64, h // 2,
                                           qrel * 128:qrel * 128 + 128]
                                last = e.matmul(
                                    sc[bi][:, i * 128:(i + 1) * 128],
                                    lhs, rhs, start=(i == 0),
                                    stop=(i == 3 and mname is None))
                            if mname is not None:
                                last = e.matmul(sc[bi][:], idt[:],
                                                msk[mname][:],
                                                start=False, stop=True)
                            return last
                        w_extra = ([(vS, proj_done_v[j])]
                                   if (qrel == 0 and c == 0 and u4 == 0)
                                   else [])
                        sch.op("tensor", fill, waits=gate + w_extra, inc=tS)
                        f_done = tS.n

                        def expf(e, bi=bi, pslot=pslot, h0=h0, g=g):
                            return e.activation(
                                probs[pslot][:, h0:8 * g + 8:2, :],
                                sc[bi][:].rearrange("p (a n) -> p a n", a=4),
                                AF.Exp, bias=ESHIFT, scale=0.125)
                        sch.op("scalar", expf,
                               waits=[(tS, f_done), (vS, V_CONST)] + pgate,
                               inc=sS, vis=(c == 2 and u4 == 3))
                        exp_count[(qq, c, u4)] = sS.n
                        sc_gate[bi] = [(sS, sS.n)]
                        pgate = []

                pv_w = (list(pj_gate[2]) + list(pj_gate[3]) +
                        list(pvl_gate) +
                        [(sS, exp_count[(qq, 2, 3)]), (vS, V_CONST)])

                def pvfill(e, qq=qq):
                    last = None
                    for c in range(3):
                        k = qq + c
                        kslot_c, half = kt_of(k)
                        for h in range(16):
                            lhs = probs[c][:, h, :]
                            rhs = vts[kslot_c][:, half, h * 64:(h + 1) * 64]
                            last = e.matmul(
                                pv[h // 8][:, (h % 8) * 64:(h % 8 + 1) * 64],
                                lhs, rhs, start=(c == 0 and h % 8 == 0),
                                stop=(c == 2 and (h % 8) == 7))
                            last = e.matmul(
                                pvl[:, h:h + 1], lhs, ones[:],
                                start=(c == 0 and h == 0),
                                stop=(c == 2 and h == 15))
                    return last
                sch.op("tensor", pvfill, waits=pv_w, inc=tS)
                pv_done[qq] = tS.n
                for c in range(3):
                    probs_gate[c] = [(tS, pv_done[qq])]
                for b in range(3):
                    kvslot_read[kt_of(qq + b)[0]] = [(tS, pv_done[qq])]

                agate = list(a_read[qrel])

                def norm(e, qrel=qrel):
                    e.reciprocal(recip[:], pvl[:])
                    e.drain()
                    last = None
                    for h in range(16):
                        last = e.tensor_scalar(
                            a_t[qrel][:, h * 64:(h + 1) * 64],
                            pv[h // 8][:, (h % 8) * 64:(h % 8 + 1) * 64],
                            recip[:, h:h + 1], None, ALU.mult)
                    return last
                sch.op("vector", norm, waits=[(tS, pv_done[qq])] + agate,
                       inc=vS)
                pj_gate[2] = [(vS, vS.n)]
                pj_gate[3] = [(vS, vS.n)]
                pvl_gate.clear()
                pvl_gate.append((vS, vS.n))
                a_read[qrel] = []
                a_v.setdefault(j, [None, None])[qrel] = vS.n

        def seg_E(j):
            atgate = [(tS, at_read[-1])] if at_read else []
            at_done = transposes(
                lambda s, fc: a_t[s][:, fc * 128:(fc + 1) * 128], at_t,
                lambda s: s * 128, TGROUPS,
                lambda s, j=j: [(vS, a_v[j][s])] + atgate)
            for s in range(2):
                a_read[s] = [(tS, tS.n)]
            y1gate = list(y1_read)
            del y1_read[:]
            for s in range(2):
                for nh in range(2):
                    def fillo(e, p, s=s, nh=nh):
                        last = None
                        for kc in range(8):
                            last = e.matmul(
                                p[:], at_t[:, kc, s * 128:(s + 1) * 128],
                                wsb["wot"][:, kc, nh * 512:(nh + 1) * 512],
                                start=(kc == 0), stop=(kc == 7))
                        return last

                    def evaco(e, p, s=s, nh=nh):
                        return e.copy(
                            y1sb[:, s, nh * 512:(nh + 1) * 512], p[:])
                    mm_fill(fillo, evaco,
                            [(sS, at_done)] + y1gate +
                            ([(dw["wot"], 16)] if j == 0 else []),
                            evac_eng="scalar", vis=(s == 1 and nh == 1))
                    y1gate = []
            at_read.append(tS.n)
            y1_evac[j] = sS.n

        def seg_F(j):
            par = j % 2
            g_guard = ([(sS, ln4_sq[j - 1]), (dO, 16 * j)]
                       if j > 0 else [])
            ln_stats_apply(par, y1sb, [(sS, y1_evac[j])], [], gsb, False,
                           out_guard=g_guard, apply_vis=False)
            xslot = (j + 1) % 3
            x_guard = [(sS, x2_sq[j - 1])] if j > 0 else []
            for t in range(2):
                sch.op("vector", lambda e, t=t, xslot=xslot: e.tensor_add(
                    x2[:, t, :], xs[xslot][:, t, :], gsb[:, t, :]),
                    waits=x_guard if t == 0 else (), inc=vS, vis=(t == 1))
            resadd_v[j] = vS.n
            if j + 3 < NB:
                xv[j + 3] = sched_x_dma(j + 3)
            av3, sq3 = ln_stats_apply(par, x2, [], [(vS, resadd_v[j])],
                                      hln3_out, True,
                                      out_guard=[(tS, at_read[-1])])
            applies3[j] = av3
            x2_sq[j] = sq3

        def seg_G(j):
            h2t_done = transposes(
                hln3_src, hts2, lambda s: s * 128, TGROUPS,
                lambda s, j=j: [(vS, applies3[j][s])])
            for s in range(2):
                for nh in range(2):
                    def fillf(e, p, s=s, nh=nh):
                        last = None
                        for kc in range(8):
                            last = e.matmul(
                                p[:], hts2[:, kc, s * 128:(s + 1) * 128],
                                wsb["cbt"][:, kc, nh * 512:(nh + 1) * 512],
                                start=(kc == 0), stop=(kc == 7))
                        return last

                    def evacf(e, p, s=s, nh=nh):
                        return e.tensor_mul(
                            gsb[:, s, nh * 512:(nh + 1) * 512], p[:],
                            scbrep[:, nh * 512:(nh + 1) * 512])
                    mm_fill(fillf, evacf,
                            [(sS, h2t_done), (dsm, SMALLS)] +
                            ([(dw["cbt"], 16)] if j == 0 else []))

        def seg_H(j):
            par = j % 2
            for t in range(2):
                sch.op("vector", lambda e, t=t: e.tensor_add(
                    gsb[:, t, :], gsb[:, t, :], biasrep[:]),
                    waits=[(dsm, SMALLS)] if (j == 0 and t == 0) else (),
                    inc=vS, vis=(t == 1))
            g_ready_v = vS.n
            _, sq4 = ln_stats_apply(par, gsb, [], [(vS, g_ready_v)], y1sb,
                                    False, apply_vis=False)
            ln4_sq[j] = sq4
            outw = ([(dO, 16 * j)] if j > 0 else []) + [(sS, ln4_sq[j])]
            for t in range(2):
                sch.op("vector", lambda e, t=t: e.tensor_add(
                    outt[:, t, :], x2[:, t, :], y1sb[:, t, :]),
                    waits=outw if t == 0 else (), inc=vS, vis=(t == 1))
            outadd_v[j] = vS.n
            y1_read.append((vS, vS.n))
            sched_out_dma(j)

        seg_A(0)
        seg_B(0)
        seg_C(0)
        for j in range(NB):
            seg_D(j)
            if j + 1 < NB:
                seg_A(j + 1)
            seg_E(j)
            seg_F(j)
            if j + 1 < NB:
                seg_B(j + 1)
                seg_C(j + 1)
            seg_G(j)
            seg_H(j)

        if DBG_PHASE < 99:
            tv, vv, sv2 = tS.n, vS.n, sS.n

            def _cleanup(e):
                for n in dw:
                    e.wait_ge(dw[n].sem, 16)
                e.wait_ge(dsm.sem, SMALLS)
                for i in range(3):
                    if dx[i].n:
                        e.wait_ge(dx[i].sem, dx[i].n)
                if tv:
                    e.wait_ge(tS.sem, tv)
                if vv:
                    e.wait_ge(vS.sem, vv)
                if sv2:
                    e.wait_ge(sS.sem, sv2)
                for jj in range(NB):
                    dst = out_h[jj * 256:(jj + 1) * 256, :].rearrange(
                        "(a p) n -> p a n", p=128)
                    e.dma_start(out=dst, in_=outt[:]).then_inc(dO.sem, 16)
            sch.op("sync", _cleanup)
            dO.n = 16 * NB

        if DBG_DUMP:
            dbg_src = {"d_hln": hln, "d_hts": hts, "d_qt": qt_t,
                       "d_kt0": kts[0], "d_vt0": vts[0], "d_kt2": kts[2],
                       "d_vt2": vts[2], "d_pb0": probs[0], "d_pb1": probs[1],
                       "d_pb2": probs[2], "d_a0": a_t[0], "d_a1": a_t[1],
                       "d_at": at_t, "d_y1": y1sb, "d_g": gsb, "d_x2": x2}
            tv, vv, sv2 = tS.n, vS.n, sS.n
            dov = dO.n
            dcvt = outt[:].rearrange("p a n -> p (a n)")

            def _dump(e):
                e.wait_ge(tS.sem, tv)
                e.wait_ge(vS.sem, vv)
                e.wait_ge(sS.sem, sv2)
                e.wait_ge(dO.sem, dov)
                e.tensor_copy(dcvt[:, 0:2], st_sum[0][:])
                e.tensor_copy(dcvt[:, 2:4], st_sq[0][:])
                e.tensor_copy(dcvt[:, 4:6], st_mean[0][:])
                e.tensor_copy(dcvt[:, 6:8], st_ve[0][:])
                e.tensor_copy(dcvt[:, 8:10], st_rstd[0][:])
                return e.engine_nop()
            sch.op("vector", _dump, inc=vS)
            sch.op("sync", lambda e, v=vS.n: e.wait_ge(vS.sem, v))
            sch.op("sync", lambda e: e.dma_start(
                out=dbg_h["d_sums"][:], in_=dcvt[:, 0:16]).then_inc(dO.sem, 16))
            dO.n += 16
            dcur0 = dO.n
            sch.op("vector", lambda e, d=dcur0: e.wait_ge(dO.sem, d))
            for n in ["d_hln", "d_hts", "d_qt", "d_kt0", "d_vt0", "d_kt2",
                      "d_vt2", "d_pb0", "d_pb1", "d_pb2", "d_a0", "d_a1",
                      "d_at", "d_y1", "d_g", "d_x2"]:
                srcap = dbg_src[n]
                nel = 1024 if n in ("d_a0", "d_a1") else 2048

                def _cvt(e, srcap=srcap, nel=nel):
                    flat = srcap[:].rearrange("p ... -> p (...)") \
                        if srcap[:].ndim > 2 else srcap[:]
                    return e.tensor_copy(dcvt[:, 0:nel], flat)
                sch.op("vector", _cvt, inc=vS)
                vcur = vS.n

                def _dma(e, n=n, nel=nel, vcur=vcur):
                    e.wait_ge(vS.sem, vcur)
                    return e.dma_start(out=dbg_h[n][:, 0:nel],
                                       in_=dcvt[:, 0:nel]).then_inc(dO.sem, 16)
                sch.op("sync", _dma)
                dO.n += 16
                dcur = dO.n
                sch.op("vector", lambda e, dcur=dcur: e.wait_ge(dO.sem, dcur))

        sch.op("sync", lambda e: e.wait_ge(dO.sem, dO.n if DBG_DUMP else 16 * NB))

        block = ctx.enter_context(nc.Block())

        @block.sync
        def _(eng):
            sch.emit("sync", eng)

        @block.gpsimd
        def _(eng):
            sch.emit("gpsimd", eng)

        @block.vector
        def _(eng):
            sch.emit("vector", eng)

        @block.scalar
        def _(eng):
            sch.emit("scalar", eng)

        @block.tensor
        def _(eng):
            sch.emit("tensor", eng)

    return nc


_NC_CACHE = {}


def _get_program():
    if "nc" not in _NC_CACHE:
        _NC_CACHE["nc"] = build_program()
    return _NC_CACHE["nc"]


def _masks_for_core(core):
    i = np.arange(128)
    m = np.arange(128)[:, None]
    c0 = np.where(m > i[None, :], 0.0, MASKVAL).astype(np.float32)
    c2 = np.where(m <= i[None, :], 0.0, MASKVAL).astype(np.float32)
    full = np.full((128, 128), MASKVAL, np.float32)
    zero = np.zeros((128, 128), np.float32)
    if core == 0:
        q0c0, q0c1, q1c0 = full, full, full
    else:
        q0c0, q0c1, q1c0 = c0, zero, c0
    tile = lambda t: np.tile(t, (1, 4)).astype(ml_dtypes.bfloat16)
    return {"m_c0": tile(c0), "m_c2": tile(c2), "m_q0_c0": tile(q0c0),
            "m_q0_c1": tile(q0c1)}


def kernel(L0_attn_x, L0_attn_wq, L0_attn_wk, L0_attn_wv, L0_attn_wo,
           L0_ffn_CB, L0_ffn_SCB, L0_ffn_bias):
    x = np.asarray(L0_attn_x, np.float32).reshape(16384, 1024)
    wts = {"wqt": L0_attn_wq, "wkt": L0_attn_wk, "wvt": L0_attn_wv,
           "wot": L0_attn_wo, "cbt": L0_ffn_CB}
    wts = {n: np.ascontiguousarray(np.asarray(w, np.float32).T)
           for n, w in wts.items()}
    scb = np.tile(np.asarray(L0_ffn_SCB, np.float32).reshape(1, 1024),
                  (128, 1))
    bias = np.tile(np.asarray(L0_ffn_bias, np.float32).reshape(1, 1024),
                   (128, 1))
    ident = np.eye(128, dtype=ml_dtypes.bfloat16)

    nc = _get_program()
    in_maps = []
    for c in range(8):
        lo = c * 2048
        xs = np.empty((2304, 1024), np.float32)
        if c == 0:
            xs[0:256] = 0.0
        else:
            xs[0:256] = x[lo - 256:lo]
        xs[256:] = x[lo:lo + 2048]
        m = {"x": xs, "scb": scb, "bias": bias, "ident": ident}
        m.update(wts)
        m.update(_masks_for_core(c))
        in_maps.append(m)

    res = run_bass_kernel_spmd(nc, in_maps, core_ids=list(range(8)))
    out = np.concatenate([res.results[c]["out"] for c in range(8)], axis=0)
    return out.reshape(1, 16384, 1024)


# version 18
# speedup vs baseline: 1.0640x; 1.0293x over previous
import numpy as np
import ml_dtypes
from contextlib import ExitStack

import concourse.bass as bass
import concourse.mybir as mybir
from concourse.bass_utils import run_bass_kernel_spmd

F32 = mybir.dt.float32
BF16 = mybir.dt.bfloat16
AF = mybir.ActivationFunctionType
ALU = mybir.AluOpType
AX = mybir.AxisListType

NB = 8
DBG_PHASE = 99
DBG_DUMP = False
USE_TNOP = True
USE_DRAIN = True
EPS = 1e-5
ESHIFT = -8.0
MASKVAL = -30000.0


class Ctr:
    def __init__(self, sem):
        self.sem = sem
        self.n = 0


class Sched:

    ENGINES = ("sync", "gpsimd", "vector", "scalar", "tensor")

    def __init__(self):
        self.q = {e: [] for e in self.ENGINES}
        self.relay = None

    def op(self, eng, builder, waits=(), inc=None, inck=1, vis=True):
        w = [(c, v) for (c, v) in waits if v > 0]
        if (eng == "tensor" and inc is not None and self.relay is not None
                and inc is self.relay[1]):
            priv, pub = self.relay
            priv.n += inck
            pv = priv.n
            self.q[eng].append((builder, w, priv, inck))
            pub.n += inck

            def relay_fn(e, pv=pv, inck=inck, pub=pub):
                e.wait_ge(priv.sem, pv)
                e.sem_inc(pub.sem, inck)
            self.q["gpsimd"].append((relay_fn, [], None, 0))
            return pub.n
        if inc is not None and vis and eng in ("vector", "scalar"):
            self.q[eng].append((builder, w, None, 0))
            inc.n += inck
            self.q[eng].append(((lambda e: e.drain()), [], inc, inck))
            return inc.n
        if inc is not None:
            inc.n += inck
        self.q[eng].append((builder, w, inc, inck))
        return None if inc is None else inc.n

    def emit(self, eng_name, eng):
        for builder, waits, inc, inck in self.q[eng_name]:
            for c, v in waits:
                eng.wait_ge(c.sem, v)
            inst = builder(eng)
            if inc is not None and inst is not None:
                inst.then_inc(inc.sem, inck)


def build_program(mk=None):
    factory = mk if mk is not None else bass.Bass
    nc = factory("TRN2", target_bir_lowering=False,
                 detect_race_conditions=False)

    x_h = nc.dram_tensor("x", [2304, 1024], F32, kind="ExternalInput")
    w_h = {n: nc.dram_tensor(n, [1024, 1024], F32, kind="ExternalInput")
           for n in ["wqt", "wkt", "wvt", "wot", "cbt"]}
    scb_h = nc.dram_tensor("scb", [128, 1024], F32, kind="ExternalInput")
    bias_h = nc.dram_tensor("bias", [128, 1024], F32, kind="ExternalInput")
    id_h = nc.dram_tensor("ident", [128, 128], BF16, kind="ExternalInput")
    mask_h = {n: nc.dram_tensor(n, [128, 512], BF16, kind="ExternalInput")
              for n in ["m_c0", "m_c2", "m_q0_c0", "m_q0_c1"]}
    out_h = nc.dram_tensor("out", [2048, 1024], F32, kind="ExternalOutput")
    dbg_h = {}
    if DBG_DUMP:
        for n, shp, dt in [("d_hln", [128, 2048], F32), ("d_hts", [128, 2048], F32),
                           ("d_qt", [128, 2048], F32), ("d_kt0", [128, 2048], F32),
                           ("d_vt0", [128, 2048], F32), ("d_kt2", [128, 2048], F32),
                           ("d_vt2", [128, 2048], F32),
                           ("d_pb0", [128, 2048], F32), ("d_pb1", [128, 2048], F32),
                           ("d_pb2", [128, 2048], F32), ("d_a0", [128, 1024], F32),
                           ("d_a1", [128, 1024], F32), ("d_at", [128, 2048], F32),
                           ("d_y1", [128, 2048], F32), ("d_g", [128, 2048], F32),
                           ("d_x2", [128, 2048], F32), ("d_sums", [128, 16], F32)]:
            dbg_h[n] = nc.dram_tensor(n, shp, dt, kind="ExternalOutput")

    ctx = ExitStack()
    with ctx:
        S = lambda *a: ctx.enter_context(nc.sbuf_tensor(*a))
        P = lambda *a: ctx.enter_context(nc.psum_tensor(*a))
        SEM = lambda n: Ctr(ctx.enter_context(nc.semaphore(n)))

        wsb = {n: S(n + "s", [128, 8, 1024], BF16) for n in w_h}
        xs = [S(f"xr{i}", [128, 2, 1024], F32) for i in range(3)]
        x2 = S("x2", [128, 2, 1024], F32)
        hln = S("hln", [128, 2, 1024], BF16)
        hts = S("hts", [128, 8, 256], BF16)
        hts2 = S("hts2", [128, 8, 256], BF16)
        qt_t = S("qt", [128, 8, 256], BF16)
        kts = [S(f"kt{i}", [128, 8, 256], BF16) for i in range(3)]
        vts = [S(f"vt{i}", [128, 2, 1024], BF16) for i in range(3)]
        probs = [S(f"pb{i}", [128, 16, 128], BF16) for i in range(3)]
        a_t = [S(f"a{i}", [128, 1024], BF16) for i in range(2)]
        at_t = S("at", [128, 8, 256], BF16)
        y1sb = S("y1sb", [128, 2, 1024], F32)
        gsb = S("gsb", [128, 2, 1024], F32)
        outt = gsb
        sqf = S("sqf", [128, 1024], BF16)
        sqf2 = S("sqf2", [128, 1024], BF16)
        scbrep = S("screp", [128, 1024], F32)
        biasrep = S("birep", [128, 1024], F32)
        idt = S("idt", [128, 128], BF16)
        msk = {n: S("s" + n, [128, 512], BF16) for n in mask_h}
        ones = S("ones", [128, 1], BF16)
        csb = S("csb", [128, 4], F32)
        recip = S("recip", [128, 16], F32)
        st_sum = [S(f"tsum{i}", [128, 2], F32) for i in range(2)]
        st_sq = [S(f"tsq{i}", [128, 2], F32) for i in range(2)]
        st_mean = [S(f"tmean{i}", [128, 2], F32) for i in range(2)]
        st_m2 = [S(f"tm2{i}", [128, 2], F32) for i in range(2)]
        st_ve = [S(f"tve{i}", [128, 2], F32) for i in range(2)]
        st_rstd = [S(f"trstd{i}", [128, 2], F32) for i in range(2)]

        nc.const_aps.register(F32, EPS, csb[:, 0:1])
        nc.const_aps.register(F32, -0.5, csb[:, 1:2])
        nc.const_aps.register(F32, 0.125, csb[:, 2:3])
        nc.const_aps.register(F32, ESHIFT, csb[:, 3:4])

        sc = [P(f"sc{i}", [128, 512], F32) for i in range(3)]
        pv = [P("pv0", [128, 512], F32), P("pv1", [128, 512], F32)]
        pj = [P("pj0", [128, 512], F32), P("pj1", [128, 512], F32),
              pv[0], pv[1]]
        pvl = P("pvl", [128, 16], F32)

        dx = [SEM(f"dx{i}") for i in range(3)]
        dw = {n: SEM("d" + n) for n in w_h}
        dsm = SEM("dsm")
        dO = SEM("dO")
        tS = SEM("tS")
        tP = SEM("tP")
        vS = SEM("vS")
        sS = SEM("sS")

        PH = 5 if DBG_PHASE in (51, 52) else DBG_PHASE
        sch = Sched()
        if USE_TNOP:
            sch.relay = (tP, tS)
        last_ve = {0: 0, 1: 0}
        sqf_free = [0]
        pj_gate = [[], [], [], []]
        sc_gate = [[], [], []]
        pvl_gate = []
        pj_i = 0
        first_tensor = [True]

        def _smalls(eng):
            dsm_sem = dsm.sem
            eng.dma_start(out=scbrep[:], in_=scb_h[:]).then_inc(dsm_sem, 16)
            eng.dma_start(out=biasrep[:], in_=bias_h[:]).then_inc(dsm_sem, 16)
            eng.dma_start(out=idt[:], in_=id_h[:]).then_inc(dsm_sem, 16)
            for n in mask_h:
                eng.dma_start(out=msk[n][:], in_=mask_h[n][:]).then_inc(dsm_sem, 16)
        sch.op("sync", _smalls)
        dsm.n = 16 * (3 + len(mask_h))
        SMALLS = dsm.n

        def _weights(eng):
            for n in ["wkt", "wvt", "wqt", "wot", "cbt"]:
                src = w_h[n][:].rearrange("(kc p) n -> p kc n", p=128)
                eng.dma_start(out=wsb[n][:], in_=src).then_inc(dw[n].sem, 16)
        sch.op("gpsimd", _weights)
        for n in dw:
            dw[n].n = 16


        def _vconsts(eng):
            eng.memset(csb[:, 0:1], EPS)
            eng.memset(csb[:, 1:2], -0.5)
            eng.memset(csb[:, 2:3], 0.125)
            eng.memset(csb[:, 3:4], ESHIFT)
            return eng.memset(ones[:], 1.0)
        sch.op("vector", _vconsts, inc=vS)
        V_CONST = vS.n
        sch.op("vector", lambda e: e.tensor_scalar(
            scbrep[:], scbrep[:], 1.0 / 127.0, None, ALU.mult),
            waits=[(dsm, SMALLS)], inc=vS)
        V_SCB = vS.n

        def ln_stats_apply(par, src, waits_v, waits_s, out_tile, out_is_bf,
                           out_guard=(), apply_vis=True):
            sch.op("scalar", lambda e: e.activation(
                sqf[:], src[:, 0, :], AF.Square),
                waits=list(waits_s) + [(vS, sqf_free[0])])
            sch.op("scalar", lambda e: e.activation(
                sqf2[:], src[:, 1, :], AF.Square))
            sch.op("scalar", lambda e: e.drain(), inc=sS, vis=False)
            sqs = sS.n
            sq1 = sqs
            for t in range(2):
                sch.op("vector", lambda e, t=t: e.reduce_sum(
                    st_sum[par][:, t:t + 1], src[:, t, :], AX.X),
                    waits=waits_v if t == 0 else ())
            sch.op("vector", lambda e, par=par: e.reduce_sum(
                st_sq[par][:, 0:1], sqf[:], AX.X),
                waits=[(sS, sqs), (vS, last_ve[par])])
            sch.op("vector", lambda e, par=par: e.reduce_sum(
                st_sq[par][:, 1:2], sqf2[:], AX.X), inc=vS, vis=False)
            sqf_free[0] = vS.n
            sch.op("vector", lambda e: e.drain())
            sch.op("vector", lambda e: e.tensor_scalar(
                st_mean[par][:], st_sum[par][:], 1.0 / 1024.0, None, ALU.mult))
            sch.op("vector", lambda e: e.drain())
            sch.op("vector", lambda e: e.tensor_mul(
                st_m2[par][:], st_mean[par][:], st_mean[par][:]))
            sch.op("vector", lambda e: e.drain())
            sch.op("vector", lambda e: e.tensor_scalar(
                st_ve[par][:], st_sq[par][:], 1.0 / 1024.0, None, ALU.mult))
            sch.op("vector", lambda e: e.drain())
            sch.op("vector", lambda e: e.tensor_sub(
                st_ve[par][:], st_ve[par][:], st_m2[par][:]), inc=vS)
            ve_done = vS.n
            last_ve[par] = ve_done
            sch.op("scalar", lambda e: e.activation(
                st_m2[par][:], st_ve[par][:], AF.Ln, bias=EPS),
                waits=[(vS, ve_done), (vS, V_CONST)])
            sch.op("scalar", lambda e: e.activation(
                st_rstd[par][:], st_m2[par][:], AF.Exp, scale=-0.5))
            sch.op("scalar", lambda e: e.drain(), inc=sS, vis=False)
            rstd_done = sS.n
            av = []
            ot = out_tile if callable(out_tile) else (
                lambda t, o=out_tile: o[:, t, :])
            for t in range(2):
                sch.op("vector", lambda e, t=t: e.tensor_scalar(
                    ot(t), src[:, t, :], st_mean[par][:, t:t + 1],
                    None, ALU.subtract),
                    waits=([(sS, rstd_done)] + list(out_guard))
                    if t == 0 else ())
                sch.op("vector", lambda e, t=t: e.tensor_scalar(
                    ot(t), ot(t),
                    st_rstd[par][:, t:t + 1], None, ALU.mult), inc=vS,
                    vis=apply_vis)
                av.append(vS.n)
            return av, sq1

        def transposes(src_fn, dst, dst_col, groups, waits0):
            nonlocal pj_i
            for (s, fcs) in groups:
                b = pj_i % 4
                pj_i += 1
                gate = list(pj_gate[b])

                def fill(e, s=s, fcs=fcs, b=b):
                    last = None
                    for i, fc in enumerate(fcs):
                        last = e.matmul(pj[b][:, i * 128:(i + 1) * 128],
                                        src_fn(s, fc), idt[:],
                                        start=(i == 0), stop=(i == len(fcs) - 1))
                    return last
                w = gate + (waits0(s) if callable(waits0) else list(waits0))
                if first_tensor[0]:
                    w = w + [(dsm, SMALLS)]
                    first_tensor[0] = False
                sch.op("tensor", fill, waits=w, inc=tS)
                fill_done = tS.n
                c0 = dst_col(s)
                sch.op("scalar", lambda e, fcs=fcs, b=b, c0=c0: e.copy(
                    dst[:, fcs[0]:fcs[0] + len(fcs), c0:c0 + 128],
                    pj[b][:].rearrange("p (a n) -> p a n", a=len(fcs))),
                    waits=[(tS, fill_done)], inc=sS,
                    vis=((s, fcs) == groups[-1]))
                pj_gate[b] = [(sS, sS.n)]
            return sS.n

        def proj_fill(wname, nc0, rhs_ap, extra_waits, out_dst, vis=False):
            nonlocal pj_i
            b = pj_i % 4
            pj_i += 1
            gate = list(pj_gate[b])

            def fill(e, b=b):
                last = None
                for ni in range(2):
                    for kc in range(8):
                        lhs = wsb[wname][:, kc, (nc0 + ni) * 128:
                                         (nc0 + ni + 1) * 128]
                        rhs = rhs_ap(kc)
                        o = pj[b][:, ni * 256:(ni + 1) * 256]
                        last = e.matmul(o, lhs, rhs,
                                        start=(ni == 0 and kc == 0),
                                        stop=(ni == 1 and kc == 7))
                return last
            sch.op("tensor", fill, waits=gate + extra_waits, inc=tS)
            fd = tS.n
            sch.op("vector", lambda e, b=b: out_dst(e, pj[b]),
                   waits=[(tS, fd)], inc=vS, vis=vis)
            pj_gate[b] = [(vS, vS.n)]
            return vS.n

        def mm_fill(fill_builder, evac_builder, extra_waits, evac_eng="vector",
                    evac_ctr=None, vis=False):
            nonlocal pj_i
            b = pj_i % 4
            pj_i += 1
            gate = list(pj_gate[b])
            sch.op("tensor", lambda e, b=b: fill_builder(e, pj[b]),
                   waits=gate + extra_waits, inc=tS)
            fd = tS.n
            ec = evac_ctr or (vS if evac_eng == "vector" else sS)
            sch.op(evac_eng, lambda e, b=b: evac_builder(e, pj[b]),
                   waits=[(tS, fd)], inc=ec, vis=vis)
            pj_gate[b] = [(ec, ec.n)]
            return ec.n

        x_sync_extra = {}
        resadd_v = {}
        outadd_v = {}
        sq_reads_x = {}

        def sched_x_dma(j):
            slot = (j + 1) % 3
            waits = []
            if j - 3 >= -1 and (j - 3) in resadd_v:
                waits = [(vS, resadd_v[j - 3]), (sS, sq_reads_x[j - 3])]
            row0 = (j + 1) * 256
            src = x_h[row0:row0 + 256, :].rearrange("(a p) n -> p a n", p=128)
            sch.op("sync", lambda e, slot=slot: e.dma_start(
                out=xs[slot][:], in_=src), waits=waits, inc=dx[slot], inck=16)
            return dx[slot].n

        def sched_out_dma(j):
            dst = out_h[j * 256:(j + 1) * 256, :].rearrange(
                "(a p) n -> p a n", p=128)
            sch.op("sync", lambda e: e.dma_start(out=dst, in_=outt[:]),
                   waits=[(vS, outadd_v[j])], inc=dO, inck=16)

        probs_gate = [[], [], []]
        kvslot_read = {0: [], 1: [], 2: []}
        a_read = [[], []]
        at_read = []
        hln_read = []
        y1_read = []
        exp_count = {}
        pv_done = {}
        applies1 = {}
        ht1_done = {}
        proj_done_v = {}
        a_v = {}
        y1_evac = {}
        applies3 = {}
        x2_sq = {}
        ln4_sq = {}

        xval = sched_x_dma(-1)
        sq_reads_x[-1] = None
        applies_h, _ = ln_stats_apply(1, xs[0], [(dx[0], xval)],
                                      [(dx[0], xval)], hln, True)
        sq_reads_x[-1] = sS.n
        resadd_v[-1] = applies_h[1]
        TGROUPS = [(s, [g * 4 + i for i in range(4)]) for s in range(2)
                   for g in range(2)]
        ht_halo = transposes(
            lambda s, fc: hln[:, s, fc * 128:(fc + 1) * 128], hts,
            lambda s: s * 128, TGROUPS,
            lambda s: [(vS, applies_h[s])])
        hln_read.append(tS.n)
        for nc0 in (0, 2, 4, 6):
            proj_fill("wkt", nc0, lambda kc: hts[:, kc, :],
                      [(sS, ht_halo), (dw["wkt"], 16)],
                      lambda e, p, nc0=nc0: e.tensor_copy(
                          kts[2][:, nc0:nc0 + 2, :],
                          p[:].rearrange("p (a n) -> p a n", a=2)))
        for s in range(2):
            for nh in range(2):
                def fill(e, p, s=s, nh=nh):
                    last = None
                    for kc in range(8):
                        last = e.matmul(
                            p[:], hts[:, kc, s * 128:(s + 1) * 128],
                            wsb["wvt"][:, kc, nh * 512:(nh + 1) * 512],
                            start=(kc == 0), stop=(kc == 7))
                    return last

                def evac(e, p, s=s, nh=nh):
                    return e.tensor_copy(
                        vts[2][:, s, nh * 512:(nh + 1) * 512], p[:])
                mm_fill(fill, evac, [(sS, ht_halo), (dw["wvt"], 16)])

        xv = {-1: xval}
        for j in range(min(3, NB)):
            xv[j] = sched_x_dma(j)

        def hln3_out(t):
            return at_t[:, 4 * t:4 * t + 4, :].rearrange("p a n -> p (a n)")

        def hln3_src(s, fc):
            o = s * 1024 + fc * 128
            return at_t[:, o // 256, o % 256:o % 256 + 128]

        def seg_A(j):
            xslot = (j + 1) % 3
            w = [(dx[xslot], xv[j])]
            hg = [(tS, hln_read[-1])] if hln_read else []
            av, _ = ln_stats_apply(j % 2, xs[xslot], list(w), list(w), hln,
                                   True, out_guard=hg)
            applies1[j] = av
            sq_reads_x[j] = sS.n

        def seg_B(j):
            ht1_done[j] = transposes(
                lambda s, fc: hln[:, s, fc * 128:(fc + 1) * 128], hts,
                lambda s: s * 128, TGROUPS,
                lambda s, j=j: [(vS, applies1[j][s])])
            hln_read.append(tS.n)

        def seg_C(j):
            kslot = j % 3
            kv_gate = list(kvslot_read[kslot])
            ht = ht1_done[j]
            for nc0 in (0, 2, 4, 6):
                proj_fill("wqt", nc0, lambda kc: hts[:, kc, :],
                          [(sS, ht)] + ([(dw["wqt"], 16)] if j == 0 else []),
                          lambda e, p, nc0=nc0: e.tensor_copy(
                              qt_t[:, nc0:nc0 + 2, :],
                              p[:].rearrange("p (a n) -> p a n", a=2)))
            for nc0 in (0, 2, 4, 6):
                proj_fill("wkt", nc0, lambda kc: hts[:, kc, :],
                          [(sS, ht)] + kv_gate +
                          ([(dw["wkt"], 16)] if j == 0 else []),
                          lambda e, p, nc0=nc0, kslot=kslot: e.tensor_copy(
                              kts[kslot][:, nc0:nc0 + 2, :],
                              p[:].rearrange("p (a n) -> p a n", a=2)))
            for s in range(2):
                for nh in range(2):
                    def fillv(e, p, s=s, nh=nh):
                        last = None
                        for kc in range(8):
                            last = e.matmul(
                                p[:], hts[:, kc, s * 128:(s + 1) * 128],
                                wsb["wvt"][:, kc, nh * 512:(nh + 1) * 512],
                                start=(kc == 0), stop=(kc == 7))
                        return last

                    def evacv(e, p, s=s, nh=nh, kslot=kslot):
                        return e.tensor_copy(
                            vts[kslot][:, s, nh * 512:(nh + 1) * 512], p[:])
                    mm_fill(fillv, evacv, [(sS, ht)] + kv_gate +
                            ([(dw["wvt"], 16)] if j == 0 else []),
                            vis=(s == 1 and nh == 1))
            kvslot_read[kslot] = []
            proj_done_v[j] = vS.n

        def kt_of(k):
            ob = (k >> 1) - 1
            return ob % 3, k % 2

        sc_i = [0]

        def seg_D(j):
            for qrel in range(2):
                qq = 2 * j + qrel
                for c in range(3):
                    k = qq + c
                    kslot_c, half = kt_of(k)
                    pslot = c
                    pgate = list(probs_gate[pslot])
                    for u4 in range(4):
                        g, par_u = u4 >> 1, u4 & 1
                        h0 = 8 * g + par_u
                        bi = sc_i[0] % 3
                        sc_i[0] += 1
                        gate = list(sc_gate[bi])
                        mname = None
                        if c == 0:
                            mname = ("m_q0_c0" if qq <= 1 else "m_c0")
                        elif c == 1 and qq == 0:
                            mname = "m_q0_c1"
                        elif c == 2:
                            mname = "m_c2"

                        def fill(e, h0=h0, par_u=par_u, kslot_c=kslot_c,
                                 half=half, qrel=qrel, bi=bi, mname=mname):
                            last = None
                            po = 64 * par_u
                            for i in range(4):
                                h = h0 + 2 * i
                                lhs = kts[kslot_c][po:po + 64, h // 2,
                                                   half * 128:half * 128 + 128]
                                rhs = qt_t[po:po + 64, h // 2,
                                           qrel * 128:qrel * 128 + 128]
                                last = e.matmul(
                                    sc[bi][:, i * 128:(i + 1) * 128],
                                    lhs, rhs, start=(i == 0),
                                    stop=(i == 3 and mname is None))
                            if mname is not None:
                                last = e.matmul(sc[bi][:], idt[:],
                                                msk[mname][:],
                                                start=False, stop=True)
                            return last
                        w_extra = ([(vS, proj_done_v[j])]
                                   if (qrel == 0 and c == 0 and u4 == 0)
                                   else [])
                        sch.op("tensor", fill, waits=gate + w_extra, inc=tS)
                        f_done = tS.n

                        def expf(e, bi=bi, pslot=pslot, h0=h0, g=g):
                            return e.activation(
                                probs[pslot][:, h0:8 * g + 8:2, :],
                                sc[bi][:].rearrange("p (a n) -> p a n", a=4),
                                AF.Exp, bias=ESHIFT, scale=0.125)
                        sch.op("scalar", expf,
                               waits=[(tS, f_done), (vS, V_CONST)] + pgate,
                               inc=sS, vis=(c == 2 and u4 == 3))
                        exp_count[(qq, c, u4)] = sS.n
                        sc_gate[bi] = [(sS, sS.n)]
                        pgate = []

                pv_w = (list(pj_gate[2]) + list(pj_gate[3]) +
                        list(pvl_gate) +
                        [(sS, exp_count[(qq, 2, 3)]), (vS, V_CONST)])

                def pvfill(e, qq=qq):
                    last = None
                    for c in range(3):
                        k = qq + c
                        kslot_c, half = kt_of(k)
                        for h in range(16):
                            lhs = probs[c][:, h, :]
                            rhs = vts[kslot_c][:, half, h * 64:(h + 1) * 64]
                            last = e.matmul(
                                pv[h // 8][:, (h % 8) * 64:(h % 8 + 1) * 64],
                                lhs, rhs, start=(c == 0 and h % 8 == 0),
                                stop=(c == 2 and (h % 8) == 7))
                            last = e.matmul(
                                pvl[:, h:h + 1], lhs, ones[:],
                                start=(c == 0 and h == 0),
                                stop=(c == 2 and h == 15))
                    return last
                sch.op("tensor", pvfill, waits=pv_w, inc=tS)
                pv_done[qq] = tS.n
                for c in range(3):
                    probs_gate[c] = [(tS, pv_done[qq])]
                for b in range(3):
                    kvslot_read[kt_of(qq + b)[0]] = [(tS, pv_done[qq])]

                agate = list(a_read[qrel])

                def norm(e, qrel=qrel):
                    e.reciprocal(recip[:], pvl[:])
                    e.drain()
                    last = None
                    for h in range(16):
                        last = e.tensor_scalar(
                            a_t[qrel][:, h * 64:(h + 1) * 64],
                            pv[h // 8][:, (h % 8) * 64:(h % 8 + 1) * 64],
                            recip[:, h:h + 1], None, ALU.mult)
                    return last
                sch.op("vector", norm, waits=[(tS, pv_done[qq])] + agate,
                       inc=vS)
                pj_gate[2] = [(vS, vS.n)]
                pj_gate[3] = [(vS, vS.n)]
                pvl_gate.clear()
                pvl_gate.append((vS, vS.n))
                a_read[qrel] = []
                a_v.setdefault(j, [None, None])[qrel] = vS.n

        def seg_E(j):
            atgate = [(tS, at_read[-1])] if at_read else []
            at_done = transposes(
                lambda s, fc: a_t[s][:, fc * 128:(fc + 1) * 128], at_t,
                lambda s: s * 128, TGROUPS,
                lambda s, j=j: [(vS, a_v[j][s])] + atgate)
            for s in range(2):
                a_read[s] = [(tS, tS.n)]
            y1gate = list(y1_read)
            del y1_read[:]
            for s in range(2):
                for nh in range(2):
                    def fillo(e, p, s=s, nh=nh):
                        last = None
                        for kc in range(8):
                            last = e.matmul(
                                p[:], at_t[:, kc, s * 128:(s + 1) * 128],
                                wsb["wot"][:, kc, nh * 512:(nh + 1) * 512],
                                start=(kc == 0), stop=(kc == 7))
                        return last

                    def evaco(e, p, s=s, nh=nh):
                        return e.copy(
                            y1sb[:, s, nh * 512:(nh + 1) * 512], p[:])
                    mm_fill(fillo, evaco,
                            [(sS, at_done)] + y1gate +
                            ([(dw["wot"], 16)] if j == 0 else []),
                            evac_eng="scalar", vis=(s == 1 and nh == 1))
                    y1gate = []
            at_read.append(tS.n)
            y1_evac[j] = sS.n

        def seg_F(j):
            par = j % 2
            g_guard = ([(sS, ln4_sq[j - 1]), (dO, 16 * j)]
                       if j > 0 else [])
            ln_stats_apply(par, y1sb, [(sS, y1_evac[j])], [], gsb, False,
                           out_guard=g_guard, apply_vis=False)
            xslot = (j + 1) % 3
            x_guard = [(sS, x2_sq[j - 1])] if j > 0 else []
            for t in range(2):
                sch.op("vector", lambda e, t=t, xslot=xslot: e.tensor_add(
                    x2[:, t, :], xs[xslot][:, t, :], gsb[:, t, :]),
                    waits=x_guard if t == 0 else (), inc=vS, vis=(t == 1))
            resadd_v[j] = vS.n
            if j + 3 < NB:
                xv[j + 3] = sched_x_dma(j + 3)
            av3, sq3 = ln_stats_apply(par, x2, [], [(vS, resadd_v[j])],
                                      hln3_out, True,
                                      out_guard=[(tS, at_read[-1])])
            applies3[j] = av3
            x2_sq[j] = sq3

        def seg_G(j):
            h2t_done = transposes(
                hln3_src, hts2, lambda s: s * 128, TGROUPS,
                lambda s, j=j: [(vS, applies3[j][s])])
            for s in range(2):
                for nh in range(2):
                    def fillf(e, p, s=s, nh=nh):
                        last = None
                        for kc in range(8):
                            last = e.matmul(
                                p[:], hts2[:, kc, s * 128:(s + 1) * 128],
                                wsb["cbt"][:, kc, nh * 512:(nh + 1) * 512],
                                start=(kc == 0), stop=(kc == 7))
                        return last

                    def evacf(e, p, s=s, nh=nh):
                        return e.tensor_mul(
                            gsb[:, s, nh * 512:(nh + 1) * 512], p[:],
                            scbrep[:, nh * 512:(nh + 1) * 512])
                    mm_fill(fillf, evacf,
                            [(sS, h2t_done), (dsm, SMALLS)] +
                            ([(dw["cbt"], 16)] if j == 0 else []))

        def seg_H(j):
            par = j % 2
            for t in range(2):
                sch.op("vector", lambda e, t=t: e.tensor_add(
                    gsb[:, t, :], gsb[:, t, :], biasrep[:]),
                    waits=[(dsm, SMALLS)] if (j == 0 and t == 0) else (),
                    inc=vS, vis=(t == 1))
            g_ready_v = vS.n
            _, sq4 = ln_stats_apply(par, gsb, [], [(vS, g_ready_v)], y1sb,
                                    False, apply_vis=False)
            ln4_sq[j] = sq4
            outw = ([(dO, 16 * j)] if j > 0 else []) + [(sS, ln4_sq[j])]
            for t in range(2):
                sch.op("vector", lambda e, t=t: e.tensor_add(
                    outt[:, t, :], x2[:, t, :], y1sb[:, t, :]),
                    waits=outw if t == 0 else (), inc=vS, vis=(t == 1))
            outadd_v[j] = vS.n
            y1_read.append((vS, vS.n))
            sched_out_dma(j)

        seg_A(0)
        seg_B(0)
        seg_C(0)
        for j in range(NB):
            seg_D(j)
            if j + 1 < NB:
                seg_A(j + 1)
            seg_E(j)
            seg_F(j)
            if j + 1 < NB:
                seg_B(j + 1)
                seg_C(j + 1)
            seg_G(j)
            seg_H(j)

        if DBG_PHASE < 99:
            tv, vv, sv2 = tS.n, vS.n, sS.n

            def _cleanup(e):
                for n in dw:
                    e.wait_ge(dw[n].sem, 16)
                e.wait_ge(dsm.sem, SMALLS)
                for i in range(3):
                    if dx[i].n:
                        e.wait_ge(dx[i].sem, dx[i].n)
                if tv:
                    e.wait_ge(tS.sem, tv)
                if vv:
                    e.wait_ge(vS.sem, vv)
                if sv2:
                    e.wait_ge(sS.sem, sv2)
                for jj in range(NB):
                    dst = out_h[jj * 256:(jj + 1) * 256, :].rearrange(
                        "(a p) n -> p a n", p=128)
                    e.dma_start(out=dst, in_=outt[:]).then_inc(dO.sem, 16)
            sch.op("sync", _cleanup)
            dO.n = 16 * NB

        if DBG_DUMP:
            dbg_src = {"d_hln": hln, "d_hts": hts, "d_qt": qt_t,
                       "d_kt0": kts[0], "d_vt0": vts[0], "d_kt2": kts[2],
                       "d_vt2": vts[2], "d_pb0": probs[0], "d_pb1": probs[1],
                       "d_pb2": probs[2], "d_a0": a_t[0], "d_a1": a_t[1],
                       "d_at": at_t, "d_y1": y1sb, "d_g": gsb, "d_x2": x2}
            tv, vv, sv2 = tS.n, vS.n, sS.n
            dov = dO.n
            dcvt = outt[:].rearrange("p a n -> p (a n)")

            def _dump(e):
                e.wait_ge(tS.sem, tv)
                e.wait_ge(vS.sem, vv)
                e.wait_ge(sS.sem, sv2)
                e.wait_ge(dO.sem, dov)
                e.tensor_copy(dcvt[:, 0:2], st_sum[0][:])
                e.tensor_copy(dcvt[:, 2:4], st_sq[0][:])
                e.tensor_copy(dcvt[:, 4:6], st_mean[0][:])
                e.tensor_copy(dcvt[:, 6:8], st_ve[0][:])
                e.tensor_copy(dcvt[:, 8:10], st_rstd[0][:])
                return e.engine_nop()
            sch.op("vector", _dump, inc=vS)
            sch.op("sync", lambda e, v=vS.n: e.wait_ge(vS.sem, v))
            sch.op("sync", lambda e: e.dma_start(
                out=dbg_h["d_sums"][:], in_=dcvt[:, 0:16]).then_inc(dO.sem, 16))
            dO.n += 16
            dcur0 = dO.n
            sch.op("vector", lambda e, d=dcur0: e.wait_ge(dO.sem, d))
            for n in ["d_hln", "d_hts", "d_qt", "d_kt0", "d_vt0", "d_kt2",
                      "d_vt2", "d_pb0", "d_pb1", "d_pb2", "d_a0", "d_a1",
                      "d_at", "d_y1", "d_g", "d_x2"]:
                srcap = dbg_src[n]
                nel = 1024 if n in ("d_a0", "d_a1") else 2048

                def _cvt(e, srcap=srcap, nel=nel):
                    flat = srcap[:].rearrange("p ... -> p (...)") \
                        if srcap[:].ndim > 2 else srcap[:]
                    return e.tensor_copy(dcvt[:, 0:nel], flat)
                sch.op("vector", _cvt, inc=vS)
                vcur = vS.n

                def _dma(e, n=n, nel=nel, vcur=vcur):
                    e.wait_ge(vS.sem, vcur)
                    return e.dma_start(out=dbg_h[n][:, 0:nel],
                                       in_=dcvt[:, 0:nel]).then_inc(dO.sem, 16)
                sch.op("sync", _dma)
                dO.n += 16
                dcur = dO.n
                sch.op("vector", lambda e, dcur=dcur: e.wait_ge(dO.sem, dcur))

        sch.op("sync", lambda e: e.wait_ge(dO.sem, dO.n if DBG_DUMP else 16 * NB))

        block = ctx.enter_context(nc.Block())

        @block.sync
        def _(eng):
            sch.emit("sync", eng)

        @block.gpsimd
        def _(eng):
            sch.emit("gpsimd", eng)

        @block.vector
        def _(eng):
            sch.emit("vector", eng)

        @block.scalar
        def _(eng):
            sch.emit("scalar", eng)

        @block.tensor
        def _(eng):
            sch.emit("tensor", eng)

    return nc


_NC_CACHE = {}


def _get_program():
    if "nc" not in _NC_CACHE:
        _NC_CACHE["nc"] = build_program()
    return _NC_CACHE["nc"]


def _masks_for_core(core):
    i = np.arange(128)
    m = np.arange(128)[:, None]
    c0 = np.where(m > i[None, :], 0.0, MASKVAL).astype(np.float32)
    c2 = np.where(m <= i[None, :], 0.0, MASKVAL).astype(np.float32)
    full = np.full((128, 128), MASKVAL, np.float32)
    zero = np.zeros((128, 128), np.float32)
    if core == 0:
        q0c0, q0c1, q1c0 = full, full, full
    else:
        q0c0, q0c1, q1c0 = c0, zero, c0
    tile = lambda t: np.tile(t, (1, 4)).astype(ml_dtypes.bfloat16)
    return {"m_c0": tile(c0), "m_c2": tile(c2), "m_q0_c0": tile(q0c0),
            "m_q0_c1": tile(q0c1)}


def kernel(L0_attn_x, L0_attn_wq, L0_attn_wk, L0_attn_wv, L0_attn_wo,
           L0_ffn_CB, L0_ffn_SCB, L0_ffn_bias):
    x = np.asarray(L0_attn_x, np.float32).reshape(16384, 1024)
    wts = {"wqt": L0_attn_wq, "wkt": L0_attn_wk, "wvt": L0_attn_wv,
           "wot": L0_attn_wo, "cbt": L0_ffn_CB}
    wts = {n: np.ascontiguousarray(np.asarray(w, np.float32).T)
           for n, w in wts.items()}
    scb = np.tile(np.asarray(L0_ffn_SCB, np.float32).reshape(1, 1024),
                  (128, 1))
    bias = np.tile(np.asarray(L0_ffn_bias, np.float32).reshape(1, 1024),
                   (128, 1))
    ident = np.eye(128, dtype=ml_dtypes.bfloat16)

    nc = _get_program()
    in_maps = []
    for c in range(8):
        lo = c * 2048
        xs = np.empty((2304, 1024), np.float32)
        if c == 0:
            xs[0:256] = 0.0
        else:
            xs[0:256] = x[lo - 256:lo]
        xs[256:] = x[lo:lo + 2048]
        m = {"x": xs, "scb": scb, "bias": bias, "ident": ident}
        m.update(wts)
        m.update(_masks_for_core(c))
        in_maps.append(m)

    res = run_bass_kernel_spmd(nc, in_maps, core_ids=list(range(8)))
    out = np.concatenate([res.results[c]["out"] for c in range(8)], axis=0)
    return out.reshape(1, 16384, 1024)


# version 19
# speedup vs baseline: 1.0813x; 1.0162x over previous
import numpy as np
import ml_dtypes
from contextlib import ExitStack

import concourse.bass as bass
import concourse.mybir as mybir
from concourse.bass_utils import run_bass_kernel_spmd

F32 = mybir.dt.float32
BF16 = mybir.dt.bfloat16
AF = mybir.ActivationFunctionType
ALU = mybir.AluOpType
AX = mybir.AxisListType

NB = 8
DBG_PHASE = 99
DBG_DUMP = False
USE_TNOP = True
USE_DRAIN = True
EPS = 1e-5
ESHIFT = -8.0
MASKVAL = -30000.0


class Ctr:
    def __init__(self, sem):
        self.sem = sem
        self.n = 0


class Sched:

    ENGINES = ("sync", "gpsimd", "vector", "scalar", "tensor")

    def __init__(self):
        self.q = {e: [] for e in self.ENGINES}
        self.relay = None

    def op(self, eng, builder, waits=(), inc=None, inck=1, vis=True):
        w = [(c, v) for (c, v) in waits if v > 0]
        if (eng == "tensor" and inc is not None and self.relay is not None
                and inc is self.relay[1]):
            priv, pub = self.relay
            priv.n += inck
            pv = priv.n
            self.q[eng].append((builder, w, priv, inck))
            pub.n += inck

            def relay_fn(e, pv=pv, inck=inck, pub=pub):
                e.wait_ge(priv.sem, pv)
                e.sem_inc(pub.sem, inck)
            self.q["gpsimd"].append((relay_fn, [], None, 0))
            return pub.n
        if inc is not None and vis and eng in ("vector", "scalar"):
            self.q[eng].append((builder, w, None, 0))
            inc.n += inck
            self.q[eng].append(((lambda e: e.drain()), [], inc, inck))
            return inc.n
        if inc is not None:
            inc.n += inck
        self.q[eng].append((builder, w, inc, inck))
        return None if inc is None else inc.n

    def emit(self, eng_name, eng):
        for builder, waits, inc, inck in self.q[eng_name]:
            for c, v in waits:
                eng.wait_ge(c.sem, v)
            inst = builder(eng)
            if inc is not None and inst is not None:
                inst.then_inc(inc.sem, inck)


def build_program(mk=None):
    factory = mk if mk is not None else bass.Bass
    nc = factory("TRN2", target_bir_lowering=False,
                 detect_race_conditions=False)

    x_h = nc.dram_tensor("x", [2304, 1024], F32, kind="ExternalInput")
    w_h = {n: nc.dram_tensor(n, [1024, 1024], F32, kind="ExternalInput")
           for n in ["wqt", "wkt", "wvt", "wot", "cbt"]}
    scb_h = nc.dram_tensor("scb", [128, 1024], F32, kind="ExternalInput")
    bias_h = nc.dram_tensor("bias", [128, 1024], F32, kind="ExternalInput")
    id_h = nc.dram_tensor("ident", [128, 128], BF16, kind="ExternalInput")
    mask_h = {n: nc.dram_tensor(n, [128, 512], BF16, kind="ExternalInput")
              for n in ["m_c0", "m_c2", "m_q0_c0", "m_q0_c1"]}
    out_h = nc.dram_tensor("out", [2048, 1024], F32, kind="ExternalOutput")
    dbg_h = {}
    if DBG_DUMP:
        for n, shp, dt in [("d_hln", [128, 2048], F32), ("d_hts", [128, 2048], F32),
                           ("d_qt", [128, 2048], F32), ("d_kt0", [128, 2048], F32),
                           ("d_vt0", [128, 2048], F32), ("d_kt2", [128, 2048], F32),
                           ("d_vt2", [128, 2048], F32),
                           ("d_pb0", [128, 2048], F32), ("d_pb1", [128, 2048], F32),
                           ("d_pb2", [128, 2048], F32), ("d_a0", [128, 1024], F32),
                           ("d_a1", [128, 1024], F32), ("d_at", [128, 2048], F32),
                           ("d_y1", [128, 2048], F32), ("d_g", [128, 2048], F32),
                           ("d_x2", [128, 2048], F32), ("d_sums", [128, 16], F32)]:
            dbg_h[n] = nc.dram_tensor(n, shp, dt, kind="ExternalOutput")

    ctx = ExitStack()
    with ctx:
        S = lambda *a: ctx.enter_context(nc.sbuf_tensor(*a))
        P = lambda *a: ctx.enter_context(nc.psum_tensor(*a))
        SEM = lambda n: Ctr(ctx.enter_context(nc.semaphore(n)))

        wsb = {n: S(n + "s", [128, 8, 1024], BF16) for n in w_h}
        xs = [S(f"xr{i}", [128, 2, 1024], F32) for i in range(3)]
        x2 = S("x2", [128, 2, 1024], F32)
        hln = S("hln", [128, 2, 1024], BF16)
        hts = S("hts", [128, 8, 256], BF16)
        hts2 = S("hts2", [128, 8, 256], BF16)
        qt_t = S("qt", [128, 8, 256], BF16)
        kts = [S(f"kt{i}", [128, 8, 256], BF16) for i in range(3)]
        vts = [S(f"vt{i}", [128, 2, 1024], BF16) for i in range(3)]
        probs = [S(f"pb{i}", [128, 16, 128], BF16) for i in range(3)]
        a_t = [S(f"a{i}", [128, 1024], BF16) for i in range(2)]
        at_t = S("at", [128, 8, 256], BF16)
        y1sb = S("y1sb", [128, 2, 1024], F32)
        gsb = S("gsb", [128, 2, 1024], F32)
        outt = gsb
        sqf = S("sqf", [128, 1024], BF16)
        sqf2 = S("sqf2", [128, 1024], BF16)
        scbrep = S("screp", [128, 1024], F32)
        biasrep = S("birep", [128, 1024], F32)
        idt = S("idt", [128, 128], BF16)
        msk = {n: S("s" + n, [128, 512], BF16) for n in mask_h}
        ones = S("ones", [128, 1], BF16)
        csb = S("csb", [128, 4], F32)
        recip = S("recip", [128, 16], F32)
        st_sum = [S(f"tsum{i}", [128, 2], F32) for i in range(2)]
        st_sq = [S(f"tsq{i}", [128, 2], F32) for i in range(2)]
        st_mean = [S(f"tmean{i}", [128, 2], F32) for i in range(2)]
        st_m2 = [S(f"tm2{i}", [128, 2], F32) for i in range(2)]
        st_ve = [S(f"tve{i}", [128, 2], F32) for i in range(2)]
        st_rstd = [S(f"trstd{i}", [128, 2], F32) for i in range(2)]

        nc.const_aps.register(F32, EPS, csb[:, 0:1])
        nc.const_aps.register(F32, -0.5, csb[:, 1:2])
        nc.const_aps.register(F32, 0.125, csb[:, 2:3])
        nc.const_aps.register(F32, ESHIFT, csb[:, 3:4])

        sc = [P(f"sc{i}", [128, 512], F32) for i in range(3)]
        pv = [P("pv0", [128, 512], F32), P("pv1", [128, 512], F32)]
        pj = [P("pj0", [128, 512], F32), P("pj1", [128, 512], F32),
              pv[0], pv[1]]
        pvl = P("pvl", [128, 16], F32)

        dx = [SEM(f"dx{i}") for i in range(3)]
        dw = {n: SEM("d" + n) for n in w_h}
        dsm = SEM("dsm")
        dO = SEM("dO")
        tS = SEM("tS")
        tP = SEM("tP")
        vS = SEM("vS")
        sS = SEM("sS")

        PH = 5 if DBG_PHASE in (51, 52) else DBG_PHASE
        sch = Sched()
        if USE_TNOP:
            sch.relay = (tP, tS)
        last_ve = {0: 0, 1: 0}
        sqf_free = [0]
        pj_gate = [[], [], [], []]
        sc_gate = [[], [], []]
        pvl_gate = []
        pj_i = 0
        first_tensor = [True]

        def _smalls(eng):
            dsm_sem = dsm.sem
            eng.dma_start(out=scbrep[:], in_=scb_h[:]).then_inc(dsm_sem, 16)
            eng.dma_start(out=biasrep[:], in_=bias_h[:]).then_inc(dsm_sem, 16)
            eng.dma_start(out=idt[:], in_=id_h[:]).then_inc(dsm_sem, 16)
            for n in mask_h:
                eng.dma_start(out=msk[n][:], in_=mask_h[n][:]).then_inc(dsm_sem, 16)
        sch.op("sync", _smalls)
        dsm.n = 16 * (3 + len(mask_h))
        SMALLS = dsm.n

        def _weights(eng):
            for n in ["wkt", "wvt", "wqt", "wot", "cbt"]:
                src = w_h[n][:].rearrange("(kc p) n -> p kc n", p=128)
                eng.dma_start(out=wsb[n][:], in_=src).then_inc(dw[n].sem, 16)
        sch.op("gpsimd", _weights)
        for n in dw:
            dw[n].n = 16


        def _vconsts(eng):
            eng.memset(csb[:, 0:1], EPS)
            eng.memset(csb[:, 1:2], -0.5)
            eng.memset(csb[:, 2:3], 0.125)
            eng.memset(csb[:, 3:4], ESHIFT)
            return eng.memset(ones[:], 1.0)
        sch.op("vector", _vconsts, inc=vS)
        V_CONST = vS.n
        sch.op("vector", lambda e: e.tensor_scalar(
            scbrep[:], scbrep[:], 1.0 / 127.0, None, ALU.mult),
            waits=[(dsm, SMALLS)], inc=vS)
        V_SCB = vS.n

        def ln_stats_apply(par, src, waits_v, waits_s, out_tile, out_is_bf,
                           out_guard=(), apply_vis=True):
            sch.op("scalar", lambda e: e.activation(
                sqf[:], src[:, 0, :], AF.Square),
                waits=list(waits_s) + [(vS, sqf_free[0])])
            sch.op("scalar", lambda e: e.activation(
                sqf2[:], src[:, 1, :], AF.Square))
            sch.op("scalar", lambda e: e.drain(), inc=sS, vis=False)
            sqs = sS.n
            sq1 = sqs
            for t in range(2):
                sch.op("vector", lambda e, t=t: e.reduce_sum(
                    st_sum[par][:, t:t + 1], src[:, t, :], AX.X),
                    waits=waits_v if t == 0 else ())
            sch.op("vector", lambda e, par=par: e.reduce_sum(
                st_sq[par][:, 0:1], sqf[:], AX.X),
                waits=[(sS, sqs), (vS, last_ve[par])])
            sch.op("vector", lambda e, par=par: e.reduce_sum(
                st_sq[par][:, 1:2], sqf2[:], AX.X), inc=vS, vis=False)
            sqf_free[0] = vS.n
            sch.op("vector", lambda e: e.drain())
            sch.op("vector", lambda e: e.tensor_scalar(
                st_ve[par][:], st_sq[par][:], 1.0 / 1024.0, None, ALU.mult))
            sch.op("vector", lambda e: e.tensor_scalar(
                st_mean[par][:], st_sum[par][:], 1.0 / 1024.0, None, ALU.mult))
            sch.op("vector", lambda e: e.drain())
            sch.op("vector", lambda e: e.tensor_mul(
                st_m2[par][:], st_mean[par][:], st_mean[par][:]))
            sch.op("vector", lambda e: e.drain())
            sch.op("vector", lambda e: e.tensor_sub(
                st_ve[par][:], st_ve[par][:], st_m2[par][:]), inc=vS)
            ve_done = vS.n
            last_ve[par] = ve_done
            sch.op("scalar", lambda e: e.activation(
                st_m2[par][:], st_ve[par][:], AF.Ln, bias=EPS),
                waits=[(vS, ve_done), (vS, V_CONST)])
            sch.op("scalar", lambda e: e.activation(
                st_rstd[par][:], st_m2[par][:], AF.Exp, scale=-0.5))
            sch.op("scalar", lambda e: e.drain(), inc=sS, vis=False)
            rstd_done = sS.n
            av = []
            ot = out_tile if callable(out_tile) else (
                lambda t, o=out_tile: o[:, t, :])
            for t in range(2):
                sch.op("vector", lambda e, t=t: e.tensor_scalar(
                    ot(t), src[:, t, :], st_mean[par][:, t:t + 1],
                    None, ALU.subtract),
                    waits=([(sS, rstd_done)] + list(out_guard))
                    if t == 0 else ())
                sch.op("vector", lambda e, t=t: e.tensor_scalar(
                    ot(t), ot(t),
                    st_rstd[par][:, t:t + 1], None, ALU.mult), inc=vS,
                    vis=apply_vis)
                av.append(vS.n)
            return av, sq1

        def transposes(src_fn, dst, dst_col, groups, waits0):
            nonlocal pj_i
            for (s, fcs) in groups:
                b = pj_i % 4
                pj_i += 1
                gate = list(pj_gate[b])

                def fill(e, s=s, fcs=fcs, b=b):
                    last = None
                    for i, fc in enumerate(fcs):
                        last = e.matmul(pj[b][:, i * 128:(i + 1) * 128],
                                        src_fn(s, fc), idt[:],
                                        start=(i == 0), stop=(i == len(fcs) - 1))
                    return last
                w = gate + (waits0(s) if callable(waits0) else list(waits0))
                if first_tensor[0]:
                    w = w + [(dsm, SMALLS)]
                    first_tensor[0] = False
                sch.op("tensor", fill, waits=w, inc=tS)
                fill_done = tS.n
                c0 = dst_col(s)
                sch.op("scalar", lambda e, fcs=fcs, b=b, c0=c0: e.copy(
                    dst[:, fcs[0]:fcs[0] + len(fcs), c0:c0 + 128],
                    pj[b][:].rearrange("p (a n) -> p a n", a=len(fcs))),
                    waits=[(tS, fill_done)], inc=sS,
                    vis=((s, fcs) == groups[-1]))
                pj_gate[b] = [(sS, sS.n)]
            return sS.n

        def proj_fill(wname, nc0, rhs_ap, extra_waits, out_dst, vis=False):
            nonlocal pj_i
            b = pj_i % 4
            pj_i += 1
            gate = list(pj_gate[b])

            def fill(e, b=b):
                last = None
                for ni in range(2):
                    for kc in range(8):
                        lhs = wsb[wname][:, kc, (nc0 + ni) * 128:
                                         (nc0 + ni + 1) * 128]
                        rhs = rhs_ap(kc)
                        o = pj[b][:, ni * 256:(ni + 1) * 256]
                        last = e.matmul(o, lhs, rhs,
                                        start=(ni == 0 and kc == 0),
                                        stop=(ni == 1 and kc == 7))
                return last
            sch.op("tensor", fill, waits=gate + extra_waits, inc=tS)
            fd = tS.n
            sch.op("vector", lambda e, b=b: out_dst(e, pj[b]),
                   waits=[(tS, fd)], inc=vS, vis=vis)
            pj_gate[b] = [(vS, vS.n)]
            return vS.n

        def mm_fill(fill_builder, evac_builder, extra_waits, evac_eng="vector",
                    evac_ctr=None, vis=False):
            nonlocal pj_i
            b = pj_i % 4
            pj_i += 1
            gate = list(pj_gate[b])
            sch.op("tensor", lambda e, b=b: fill_builder(e, pj[b]),
                   waits=gate + extra_waits, inc=tS)
            fd = tS.n
            ec = evac_ctr or (vS if evac_eng == "vector" else sS)
            sch.op(evac_eng, lambda e, b=b: evac_builder(e, pj[b]),
                   waits=[(tS, fd)], inc=ec, vis=vis)
            pj_gate[b] = [(ec, ec.n)]
            return ec.n

        x_sync_extra = {}
        resadd_v = {}
        outadd_v = {}
        sq_reads_x = {}

        def sched_x_dma(j):
            slot = (j + 1) % 3
            waits = []
            if j - 3 >= -1 and (j - 3) in resadd_v:
                waits = [(vS, resadd_v[j - 3]), (sS, sq_reads_x[j - 3])]
            row0 = (j + 1) * 256
            src = x_h[row0:row0 + 256, :].rearrange("(a p) n -> p a n", p=128)
            sch.op("sync", lambda e, slot=slot: e.dma_start(
                out=xs[slot][:], in_=src), waits=waits, inc=dx[slot], inck=16)
            return dx[slot].n

        def sched_out_dma(j):
            dst = out_h[j * 256:(j + 1) * 256, :].rearrange(
                "(a p) n -> p a n", p=128)
            sch.op("sync", lambda e: e.dma_start(out=dst, in_=outt[:]),
                   waits=[(vS, outadd_v[j])], inc=dO, inck=16)

        probs_gate = [[], [], []]
        kvslot_read = {0: [], 1: [], 2: []}
        a_read = [[], []]
        at_read = []
        hln_read = []
        y1_read = []
        exp_count = {}
        pv_done = {}
        applies1 = {}
        ht1_done = {}
        proj_done_v = {}
        a_v = {}
        y1_evac = {}
        applies3 = {}
        x2_sq = {}
        ln4_sq = {}

        xval = sched_x_dma(-1)
        sq_reads_x[-1] = None
        applies_h, _ = ln_stats_apply(1, xs[0], [(dx[0], xval)],
                                      [(dx[0], xval)], hln, True)
        sq_reads_x[-1] = sS.n
        resadd_v[-1] = applies_h[1]
        TGROUPS = [(s, [g * 4 + i for i in range(4)]) for s in range(2)
                   for g in range(2)]
        ht_halo = transposes(
            lambda s, fc: hln[:, s, fc * 128:(fc + 1) * 128], hts,
            lambda s: s * 128, TGROUPS,
            lambda s: [(vS, applies_h[s])])
        hln_read.append(tS.n)
        for nc0 in (0, 2, 4, 6):
            proj_fill("wkt", nc0, lambda kc: hts[:, kc, :],
                      [(sS, ht_halo), (dw["wkt"], 16)],
                      lambda e, p, nc0=nc0: e.tensor_copy(
                          kts[2][:, nc0:nc0 + 2, :],
                          p[:].rearrange("p (a n) -> p a n", a=2)))
        for s in range(2):
            for nh in range(2):
                def fill(e, p, s=s, nh=nh):
                    last = None
                    for kc in range(8):
                        last = e.matmul(
                            p[:], hts[:, kc, s * 128:(s + 1) * 128],
                            wsb["wvt"][:, kc, nh * 512:(nh + 1) * 512],
                            start=(kc == 0), stop=(kc == 7))
                    return last

                def evac(e, p, s=s, nh=nh):
                    return e.tensor_copy(
                        vts[2][:, s, nh * 512:(nh + 1) * 512], p[:])
                mm_fill(fill, evac, [(sS, ht_halo), (dw["wvt"], 16)])

        xv = {-1: xval}
        for j in range(min(3, NB)):
            xv[j] = sched_x_dma(j)

        def hln3_out(t):
            return at_t[:, 4 * t:4 * t + 4, :].rearrange("p a n -> p (a n)")

        def hln3_src(s, fc):
            o = s * 1024 + fc * 128
            return at_t[:, o // 256, o % 256:o % 256 + 128]

        def seg_A(j):
            xslot = (j + 1) % 3
            w = [(dx[xslot], xv[j])]
            hg = [(tS, hln_read[-1])] if hln_read else []
            av, _ = ln_stats_apply(j % 2, xs[xslot], list(w), list(w), hln,
                                   True, out_guard=hg)
            applies1[j] = av
            sq_reads_x[j] = sS.n

        def seg_B(j):
            ht1_done[j] = transposes(
                lambda s, fc: hln[:, s, fc * 128:(fc + 1) * 128], hts,
                lambda s: s * 128, TGROUPS,
                lambda s, j=j: [(vS, applies1[j][s])])
            hln_read.append(tS.n)

        def seg_C(j):
            kslot = j % 3
            kv_gate = list(kvslot_read[kslot])
            ht = ht1_done[j]
            for nc0 in (0, 2, 4, 6):
                proj_fill("wqt", nc0, lambda kc: hts[:, kc, :],
                          [(sS, ht)] + ([(dw["wqt"], 16)] if j == 0 else []),
                          lambda e, p, nc0=nc0: e.tensor_copy(
                              qt_t[:, nc0:nc0 + 2, :],
                              p[:].rearrange("p (a n) -> p a n", a=2)))
            for nc0 in (0, 2, 4, 6):
                proj_fill("wkt", nc0, lambda kc: hts[:, kc, :],
                          [(sS, ht)] + kv_gate +
                          ([(dw["wkt"], 16)] if j == 0 else []),
                          lambda e, p, nc0=nc0, kslot=kslot: e.tensor_copy(
                              kts[kslot][:, nc0:nc0 + 2, :],
                              p[:].rearrange("p (a n) -> p a n", a=2)))
            for s in range(2):
                for nh in range(2):
                    def fillv(e, p, s=s, nh=nh):
                        last = None
                        for kc in range(8):
                            last = e.matmul(
                                p[:], hts[:, kc, s * 128:(s + 1) * 128],
                                wsb["wvt"][:, kc, nh * 512:(nh + 1) * 512],
                                start=(kc == 0), stop=(kc == 7))
                        return last

                    def evacv(e, p, s=s, nh=nh, kslot=kslot):
                        return e.tensor_copy(
                            vts[kslot][:, s, nh * 512:(nh + 1) * 512], p[:])
                    mm_fill(fillv, evacv, [(sS, ht)] + kv_gate +
                            ([(dw["wvt"], 16)] if j == 0 else []),
                            vis=(s == 1 and nh == 1))
            kvslot_read[kslot] = []
            proj_done_v[j] = vS.n

        def kt_of(k):
            ob = (k >> 1) - 1
            return ob % 3, k % 2

        sc_i = [0]

        def seg_D(j):
            for qrel in range(2):
                qq = 2 * j + qrel
                for c in range(3):
                    k = qq + c
                    kslot_c, half = kt_of(k)
                    pslot = c
                    pgate = list(probs_gate[pslot])
                    for u4 in range(4):
                        g, par_u = u4 >> 1, u4 & 1
                        h0 = 8 * g + par_u
                        bi = sc_i[0] % 3
                        sc_i[0] += 1
                        gate = list(sc_gate[bi])
                        mname = None
                        if c == 0:
                            mname = ("m_q0_c0" if qq <= 1 else "m_c0")
                        elif c == 1 and qq == 0:
                            mname = "m_q0_c1"
                        elif c == 2:
                            mname = "m_c2"

                        def fill(e, h0=h0, par_u=par_u, kslot_c=kslot_c,
                                 half=half, qrel=qrel, bi=bi, mname=mname):
                            last = None
                            po = 64 * par_u
                            for i in range(4):
                                h = h0 + 2 * i
                                lhs = kts[kslot_c][po:po + 64, h // 2,
                                                   half * 128:half * 128 + 128]
                                rhs = qt_t[po:po + 64, h // 2,
                                           qrel * 128:qrel * 128 + 128]
                                last = e.matmul(
                                    sc[bi][:, i * 128:(i + 1) * 128],
                                    lhs, rhs, start=(i == 0),
                                    stop=(i == 3 and mname is None))
                            if mname is not None:
                                last = e.matmul(sc[bi][:], idt[:],
                                                msk[mname][:],
                                                start=False, stop=True)
                            return last
                        w_extra = ([(vS, proj_done_v[j])]
                                   if (qrel == 0 and c == 0 and u4 == 0)
                                   else [])
                        sch.op("tensor", fill, waits=gate + w_extra, inc=tS)
                        f_done = tS.n

                        def expf(e, bi=bi, pslot=pslot, h0=h0, g=g):
                            return e.activation(
                                probs[pslot][:, h0:8 * g + 8:2, :],
                                sc[bi][:].rearrange("p (a n) -> p a n", a=4),
                                AF.Exp, bias=ESHIFT, scale=0.125)
                        sch.op("scalar", expf,
                               waits=[(tS, f_done), (vS, V_CONST)] + pgate,
                               inc=sS, vis=(c == 2 and u4 == 3))
                        exp_count[(qq, c, u4)] = sS.n
                        sc_gate[bi] = [(sS, sS.n)]
                        pgate = []

                pv_w = (list(pj_gate[2]) + list(pj_gate[3]) +
                        list(pvl_gate) +
                        [(sS, exp_count[(qq, 2, 3)]), (vS, V_CONST)])

                def pvfill(e, qq=qq):
                    last = None
                    for c in range(3):
                        k = qq + c
                        kslot_c, half = kt_of(k)
                        for h in range(16):
                            lhs = probs[c][:, h, :]
                            rhs = vts[kslot_c][:, half, h * 64:(h + 1) * 64]
                            last = e.matmul(
                                pv[h // 8][:, (h % 8) * 64:(h % 8 + 1) * 64],
                                lhs, rhs, start=(c == 0 and h % 8 == 0),
                                stop=(c == 2 and (h % 8) == 7))
                            last = e.matmul(
                                pvl[:, h:h + 1], lhs, ones[:],
                                start=(c == 0 and h == 0),
                                stop=(c == 2 and h == 15))
                    return last
                sch.op("tensor", pvfill, waits=pv_w, inc=tS)
                pv_done[qq] = tS.n
                for c in range(3):
                    probs_gate[c] = [(tS, pv_done[qq])]
                for b in range(3):
                    kvslot_read[kt_of(qq + b)[0]] = [(tS, pv_done[qq])]

                agate = list(a_read[qrel])

                def norm(e, qrel=qrel):
                    e.reciprocal(recip[:], pvl[:])
                    e.drain()
                    last = None
                    for h in range(16):
                        last = e.tensor_scalar(
                            a_t[qrel][:, h * 64:(h + 1) * 64],
                            pv[h // 8][:, (h % 8) * 64:(h % 8 + 1) * 64],
                            recip[:, h:h + 1], None, ALU.mult)
                    return last
                sch.op("vector", norm, waits=[(tS, pv_done[qq])] + agate,
                       inc=vS)
                pj_gate[2] = [(vS, vS.n)]
                pj_gate[3] = [(vS, vS.n)]
                pvl_gate.clear()
                pvl_gate.append((vS, vS.n))
                a_read[qrel] = []
                a_v.setdefault(j, [None, None])[qrel] = vS.n

        def seg_E(j):
            atgate = [(tS, at_read[-1])] if at_read else []
            at_done = transposes(
                lambda s, fc: a_t[s][:, fc * 128:(fc + 1) * 128], at_t,
                lambda s: s * 128, TGROUPS,
                lambda s, j=j: [(vS, a_v[j][s])] + atgate)
            for s in range(2):
                a_read[s] = [(tS, tS.n)]
            y1gate = list(y1_read)
            del y1_read[:]
            for s in range(2):
                for nh in range(2):
                    def fillo(e, p, s=s, nh=nh):
                        last = None
                        for kc in range(8):
                            last = e.matmul(
                                p[:], at_t[:, kc, s * 128:(s + 1) * 128],
                                wsb["wot"][:, kc, nh * 512:(nh + 1) * 512],
                                start=(kc == 0), stop=(kc == 7))
                        return last

                    def evaco(e, p, s=s, nh=nh):
                        return e.copy(
                            y1sb[:, s, nh * 512:(nh + 1) * 512], p[:])
                    mm_fill(fillo, evaco,
                            [(sS, at_done)] + y1gate +
                            ([(dw["wot"], 16)] if j == 0 else []),
                            evac_eng="scalar", vis=(s == 1 and nh == 1))
                    y1gate = []
            at_read.append(tS.n)
            y1_evac[j] = sS.n

        def seg_F(j):
            par = j % 2
            g_guard = ([(sS, ln4_sq[j - 1]), (dO, 16 * j)]
                       if j > 0 else [])
            ln_stats_apply(par, y1sb, [(sS, y1_evac[j])], [], gsb, False,
                           out_guard=g_guard, apply_vis=False)
            xslot = (j + 1) % 3
            x_guard = [(sS, x2_sq[j - 1])] if j > 0 else []
            for t in range(2):
                sch.op("vector", lambda e, t=t, xslot=xslot: e.tensor_add(
                    x2[:, t, :], xs[xslot][:, t, :], gsb[:, t, :]),
                    waits=x_guard if t == 0 else (), inc=vS, vis=(t == 1))
            resadd_v[j] = vS.n
            if j + 3 < NB:
                xv[j + 3] = sched_x_dma(j + 3)
            av3, sq3 = ln_stats_apply(par, x2, [], [(vS, resadd_v[j])],
                                      hln3_out, True,
                                      out_guard=[(tS, at_read[-1])])
            applies3[j] = av3
            x2_sq[j] = sq3

        def seg_G(j):
            h2t_done = transposes(
                hln3_src, hts2, lambda s: s * 128, TGROUPS,
                lambda s, j=j: [(vS, applies3[j][s])])
            for s in range(2):
                for nh in range(2):
                    def fillf(e, p, s=s, nh=nh):
                        last = None
                        for kc in range(8):
                            last = e.matmul(
                                p[:], hts2[:, kc, s * 128:(s + 1) * 128],
                                wsb["cbt"][:, kc, nh * 512:(nh + 1) * 512],
                                start=(kc == 0), stop=(kc == 7))
                        return last

                    def evacf(e, p, s=s, nh=nh):
                        return e.tensor_mul(
                            gsb[:, s, nh * 512:(nh + 1) * 512], p[:],
                            scbrep[:, nh * 512:(nh + 1) * 512])
                    mm_fill(fillf, evacf,
                            [(sS, h2t_done), (dsm, SMALLS)] +
                            ([(dw["cbt"], 16)] if j == 0 else []))

        def seg_H(j):
            par = j % 2
            for t in range(2):
                sch.op("vector", lambda e, t=t: e.tensor_add(
                    gsb[:, t, :], gsb[:, t, :], biasrep[:]),
                    waits=[(dsm, SMALLS)] if (j == 0 and t == 0) else (),
                    inc=vS, vis=(t == 1))
            g_ready_v = vS.n
            _, sq4 = ln_stats_apply(par, gsb, [], [(vS, g_ready_v)], y1sb,
                                    False, apply_vis=False)
            ln4_sq[j] = sq4
            outw = ([(dO, 16 * j)] if j > 0 else []) + [(sS, ln4_sq[j])]
            for t in range(2):
                sch.op("vector", lambda e, t=t: e.tensor_add(
                    outt[:, t, :], x2[:, t, :], y1sb[:, t, :]),
                    waits=outw if t == 0 else (), inc=vS, vis=(t == 1))
            outadd_v[j] = vS.n
            y1_read.append((vS, vS.n))
            sched_out_dma(j)

        seg_A(0)
        seg_B(0)
        seg_C(0)
        for j in range(NB):
            seg_D(j)
            if j + 1 < NB:
                seg_A(j + 1)
            seg_E(j)
            seg_F(j)
            if j + 1 < NB:
                seg_B(j + 1)
                seg_C(j + 1)
            seg_G(j)
            seg_H(j)

        if DBG_PHASE < 99:
            tv, vv, sv2 = tS.n, vS.n, sS.n

            def _cleanup(e):
                for n in dw:
                    e.wait_ge(dw[n].sem, 16)
                e.wait_ge(dsm.sem, SMALLS)
                for i in range(3):
                    if dx[i].n:
                        e.wait_ge(dx[i].sem, dx[i].n)
                if tv:
                    e.wait_ge(tS.sem, tv)
                if vv:
                    e.wait_ge(vS.sem, vv)
                if sv2:
                    e.wait_ge(sS.sem, sv2)
                for jj in range(NB):
                    dst = out_h[jj * 256:(jj + 1) * 256, :].rearrange(
                        "(a p) n -> p a n", p=128)
                    e.dma_start(out=dst, in_=outt[:]).then_inc(dO.sem, 16)
            sch.op("sync", _cleanup)
            dO.n = 16 * NB

        if DBG_DUMP:
            dbg_src = {"d_hln": hln, "d_hts": hts, "d_qt": qt_t,
                       "d_kt0": kts[0], "d_vt0": vts[0], "d_kt2": kts[2],
                       "d_vt2": vts[2], "d_pb0": probs[0], "d_pb1": probs[1],
                       "d_pb2": probs[2], "d_a0": a_t[0], "d_a1": a_t[1],
                       "d_at": at_t, "d_y1": y1sb, "d_g": gsb, "d_x2": x2}
            tv, vv, sv2 = tS.n, vS.n, sS.n
            dov = dO.n
            dcvt = outt[:].rearrange("p a n -> p (a n)")

            def _dump(e):
                e.wait_ge(tS.sem, tv)
                e.wait_ge(vS.sem, vv)
                e.wait_ge(sS.sem, sv2)
                e.wait_ge(dO.sem, dov)
                e.tensor_copy(dcvt[:, 0:2], st_sum[0][:])
                e.tensor_copy(dcvt[:, 2:4], st_sq[0][:])
                e.tensor_copy(dcvt[:, 4:6], st_mean[0][:])
                e.tensor_copy(dcvt[:, 6:8], st_ve[0][:])
                e.tensor_copy(dcvt[:, 8:10], st_rstd[0][:])
                return e.engine_nop()
            sch.op("vector", _dump, inc=vS)
            sch.op("sync", lambda e, v=vS.n: e.wait_ge(vS.sem, v))
            sch.op("sync", lambda e: e.dma_start(
                out=dbg_h["d_sums"][:], in_=dcvt[:, 0:16]).then_inc(dO.sem, 16))
            dO.n += 16
            dcur0 = dO.n
            sch.op("vector", lambda e, d=dcur0: e.wait_ge(dO.sem, d))
            for n in ["d_hln", "d_hts", "d_qt", "d_kt0", "d_vt0", "d_kt2",
                      "d_vt2", "d_pb0", "d_pb1", "d_pb2", "d_a0", "d_a1",
                      "d_at", "d_y1", "d_g", "d_x2"]:
                srcap = dbg_src[n]
                nel = 1024 if n in ("d_a0", "d_a1") else 2048

                def _cvt(e, srcap=srcap, nel=nel):
                    flat = srcap[:].rearrange("p ... -> p (...)") \
                        if srcap[:].ndim > 2 else srcap[:]
                    return e.tensor_copy(dcvt[:, 0:nel], flat)
                sch.op("vector", _cvt, inc=vS)
                vcur = vS.n

                def _dma(e, n=n, nel=nel, vcur=vcur):
                    e.wait_ge(vS.sem, vcur)
                    return e.dma_start(out=dbg_h[n][:, 0:nel],
                                       in_=dcvt[:, 0:nel]).then_inc(dO.sem, 16)
                sch.op("sync", _dma)
                dO.n += 16
                dcur = dO.n
                sch.op("vector", lambda e, dcur=dcur: e.wait_ge(dO.sem, dcur))

        sch.op("sync", lambda e: e.wait_ge(dO.sem, dO.n if DBG_DUMP else 16 * NB))

        block = ctx.enter_context(nc.Block())

        @block.sync
        def _(eng):
            sch.emit("sync", eng)

        @block.gpsimd
        def _(eng):
            sch.emit("gpsimd", eng)

        @block.vector
        def _(eng):
            sch.emit("vector", eng)

        @block.scalar
        def _(eng):
            sch.emit("scalar", eng)

        @block.tensor
        def _(eng):
            sch.emit("tensor", eng)

    return nc


_NC_CACHE = {}


def _get_program():
    if "nc" not in _NC_CACHE:
        _NC_CACHE["nc"] = build_program()
    return _NC_CACHE["nc"]


def _masks_for_core(core):
    i = np.arange(128)
    m = np.arange(128)[:, None]
    c0 = np.where(m > i[None, :], 0.0, MASKVAL).astype(np.float32)
    c2 = np.where(m <= i[None, :], 0.0, MASKVAL).astype(np.float32)
    full = np.full((128, 128), MASKVAL, np.float32)
    zero = np.zeros((128, 128), np.float32)
    if core == 0:
        q0c0, q0c1, q1c0 = full, full, full
    else:
        q0c0, q0c1, q1c0 = c0, zero, c0
    tile = lambda t: np.tile(t, (1, 4)).astype(ml_dtypes.bfloat16)
    return {"m_c0": tile(c0), "m_c2": tile(c2), "m_q0_c0": tile(q0c0),
            "m_q0_c1": tile(q0c1)}


def kernel(L0_attn_x, L0_attn_wq, L0_attn_wk, L0_attn_wv, L0_attn_wo,
           L0_ffn_CB, L0_ffn_SCB, L0_ffn_bias):
    x = np.asarray(L0_attn_x, np.float32).reshape(16384, 1024)
    wts = {"wqt": L0_attn_wq, "wkt": L0_attn_wk, "wvt": L0_attn_wv,
           "wot": L0_attn_wo, "cbt": L0_ffn_CB}
    wts = {n: np.ascontiguousarray(np.asarray(w, np.float32).T)
           for n, w in wts.items()}
    scb = np.tile(np.asarray(L0_ffn_SCB, np.float32).reshape(1, 1024),
                  (128, 1))
    bias = np.tile(np.asarray(L0_ffn_bias, np.float32).reshape(1, 1024),
                   (128, 1))
    ident = np.eye(128, dtype=ml_dtypes.bfloat16)

    nc = _get_program()
    in_maps = []
    for c in range(8):
        lo = c * 2048
        xs = np.empty((2304, 1024), np.float32)
        if c == 0:
            xs[0:256] = 0.0
        else:
            xs[0:256] = x[lo - 256:lo]
        xs[256:] = x[lo:lo + 2048]
        m = {"x": xs, "scb": scb, "bias": bias, "ident": ident}
        m.update(wts)
        m.update(_masks_for_core(c))
        in_maps.append(m)

    res = run_bass_kernel_spmd(nc, in_maps, core_ids=list(range(8)))
    out = np.concatenate([res.results[c]["out"] for c in range(8)], axis=0)
    return out.reshape(1, 16384, 1024)


# version 20 (final)
# speedup vs baseline: 1.0891x; 1.0072x over previous
import numpy as np
import ml_dtypes
from contextlib import ExitStack

import concourse.bass as bass
import concourse.mybir as mybir
from concourse.bass_utils import run_bass_kernel_spmd

F32 = mybir.dt.float32
BF16 = mybir.dt.bfloat16
AF = mybir.ActivationFunctionType
ALU = mybir.AluOpType
AX = mybir.AxisListType

NB = 8
DBG_PHASE = 99
DBG_DUMP = False
USE_TNOP = True
USE_DRAIN = True
EPS = 1e-5
ESHIFT = -8.0
MASKVAL = -30000.0


class Ctr:
    def __init__(self, sem):
        self.sem = sem
        self.n = 0


class Sched:

    ENGINES = ("sync", "gpsimd", "vector", "scalar", "tensor")

    def __init__(self):
        self.q = {e: [] for e in self.ENGINES}
        self.relay = None

    def op(self, eng, builder, waits=(), inc=None, inck=1, vis=True):
        w = [(c, v) for (c, v) in waits if v > 0]
        if (eng == "tensor" and inc is not None and self.relay is not None
                and inc is self.relay[1]):
            priv, pub = self.relay
            priv.n += inck
            pv = priv.n
            self.q[eng].append((builder, w, priv, inck))
            pub.n += inck

            def relay_fn(e, pv=pv, inck=inck, pub=pub):
                e.wait_ge(priv.sem, pv)
                e.sem_inc(pub.sem, inck)
            self.q["gpsimd"].append((relay_fn, [], None, 0))
            return pub.n
        if inc is not None and vis and eng in ("vector", "scalar"):
            self.q[eng].append((builder, w, None, 0))
            inc.n += inck
            self.q[eng].append(((lambda e: e.drain()), [], inc, inck))
            return inc.n
        if inc is not None:
            inc.n += inck
        self.q[eng].append((builder, w, inc, inck))
        return None if inc is None else inc.n

    def emit(self, eng_name, eng):
        for builder, waits, inc, inck in self.q[eng_name]:
            for c, v in waits:
                eng.wait_ge(c.sem, v)
            inst = builder(eng)
            if inc is not None and inst is not None:
                inst.then_inc(inc.sem, inck)


def build_program(mk=None):
    factory = mk if mk is not None else bass.Bass
    nc = factory("TRN2", target_bir_lowering=False,
                 detect_race_conditions=False)

    x_h = nc.dram_tensor("x", [2304, 1024], F32, kind="ExternalInput")
    w_h = {n: nc.dram_tensor(n, [1024, 1024], F32, kind="ExternalInput")
           for n in ["wqt", "wkt", "wvt", "wot", "cbt"]}
    scb_h = nc.dram_tensor("scb", [128, 1024], F32, kind="ExternalInput")
    bias_h = nc.dram_tensor("bias", [128, 1024], F32, kind="ExternalInput")
    id_h = nc.dram_tensor("ident", [128, 128], BF16, kind="ExternalInput")
    mask_h = {n: nc.dram_tensor(n, [128, 512], BF16, kind="ExternalInput")
              for n in ["m_c0", "m_c2", "m_q0_c0", "m_q0_c1"]}
    out_h = nc.dram_tensor("out", [2048, 1024], F32, kind="ExternalOutput")
    dbg_h = {}
    if DBG_DUMP:
        for n, shp, dt in [("d_hln", [128, 2048], F32), ("d_hts", [128, 2048], F32),
                           ("d_qt", [128, 2048], F32), ("d_kt0", [128, 2048], F32),
                           ("d_vt0", [128, 2048], F32), ("d_kt2", [128, 2048], F32),
                           ("d_vt2", [128, 2048], F32),
                           ("d_pb0", [128, 2048], F32), ("d_pb1", [128, 2048], F32),
                           ("d_pb2", [128, 2048], F32), ("d_a0", [128, 1024], F32),
                           ("d_a1", [128, 1024], F32), ("d_at", [128, 2048], F32),
                           ("d_y1", [128, 2048], F32), ("d_g", [128, 2048], F32),
                           ("d_x2", [128, 2048], F32), ("d_sums", [128, 16], F32)]:
            dbg_h[n] = nc.dram_tensor(n, shp, dt, kind="ExternalOutput")

    ctx = ExitStack()
    with ctx:
        S = lambda *a: ctx.enter_context(nc.sbuf_tensor(*a))
        P = lambda *a: ctx.enter_context(nc.psum_tensor(*a))
        SEM = lambda n: Ctr(ctx.enter_context(nc.semaphore(n)))

        wsb = {n: S(n + "s", [128, 8, 1024], BF16) for n in w_h}
        xs = [S(f"xr{i}", [128, 2, 1024], F32) for i in range(3)]
        x2 = S("x2", [128, 2, 1024], F32)
        hln = S("hln", [128, 2, 1024], BF16)
        hts = S("hts", [128, 8, 256], BF16)
        hts2 = S("hts2", [128, 8, 256], BF16)
        qt_t = S("qt", [128, 8, 256], BF16)
        kts = [S(f"kt{i}", [128, 8, 256], BF16) for i in range(3)]
        vts = [S(f"vt{i}", [128, 2, 1024], BF16) for i in range(3)]
        probs = [S(f"pb{i}", [128, 16, 128], BF16) for i in range(3)]
        a_t = [S(f"a{i}", [128, 1024], BF16) for i in range(2)]
        at_t = S("at", [128, 8, 256], BF16)
        y1sb = S("y1sb", [128, 2, 1024], F32)
        gsb = S("gsb", [128, 2, 1024], F32)
        outt = gsb
        sqf = S("sqf", [128, 1024], BF16)
        sqf2 = S("sqf2", [128, 1024], BF16)
        scbrep = S("screp", [128, 1024], F32)
        biasrep = S("birep", [128, 1024], F32)
        idt = S("idt", [128, 128], BF16)
        msk = {n: S("s" + n, [128, 512], BF16) for n in mask_h}
        ones = S("ones", [128, 1], BF16)
        csb = S("csb", [128, 4], F32)
        recip = S("recip", [128, 16], F32)
        st_sum = [S(f"tsum{i}", [128, 2], F32) for i in range(2)]
        st_sq = [S(f"tsq{i}", [128, 2], F32) for i in range(2)]
        st_mean = [S(f"tmean{i}", [128, 2], F32) for i in range(2)]
        st_m2 = [S(f"tm2{i}", [128, 2], F32) for i in range(2)]
        st_ve = [S(f"tve{i}", [128, 2], F32) for i in range(2)]
        st_rstd = [S(f"trstd{i}", [128, 2], F32) for i in range(2)]

        nc.const_aps.register(F32, EPS, csb[:, 0:1])
        nc.const_aps.register(F32, -0.5, csb[:, 1:2])
        nc.const_aps.register(F32, 0.125, csb[:, 2:3])
        nc.const_aps.register(F32, ESHIFT, csb[:, 3:4])

        sc = [P(f"sc{i}", [128, 512], F32) for i in range(3)]
        pv = [P("pv0", [128, 512], F32), P("pv1", [128, 512], F32)]
        pj = [P("pj0", [128, 512], F32), P("pj1", [128, 512], F32),
              pv[0], pv[1]]
        pvl = P("pvl", [128, 16], F32)

        dx = [SEM(f"dx{i}") for i in range(3)]
        dw = {n: SEM("d" + n) for n in w_h}
        dsm = SEM("dsm")
        dO = SEM("dO")
        tS = SEM("tS")
        tP = SEM("tP")
        vS = SEM("vS")
        sS = SEM("sS")

        PH = 5 if DBG_PHASE in (51, 52) else DBG_PHASE
        sch = Sched()
        if USE_TNOP:
            sch.relay = (tP, tS)
        last_ve = {0: 0, 1: 0}
        sqf_free = [0]
        pj_gate = [[], [], [], []]
        sc_gate = [[], [], []]
        pvl_gate = []
        pj_i = 0
        first_tensor = [True]

        def _smalls(eng):
            dsm_sem = dsm.sem
            eng.dma_start(out=scbrep[:], in_=scb_h[:]).then_inc(dsm_sem, 16)
            eng.dma_start(out=biasrep[:], in_=bias_h[:]).then_inc(dsm_sem, 16)
            eng.dma_start(out=idt[:], in_=id_h[:]).then_inc(dsm_sem, 16)
            for n in mask_h:
                eng.dma_start(out=msk[n][:], in_=mask_h[n][:]).then_inc(dsm_sem, 16)
        sch.op("sync", _smalls)
        dsm.n = 16 * (3 + len(mask_h))
        SMALLS = dsm.n

        def _weights(eng):
            for n in ["wkt", "wvt", "wqt", "wot", "cbt"]:
                src = w_h[n][:].rearrange("(kc p) n -> p kc n", p=128)
                eng.dma_start(out=wsb[n][:], in_=src).then_inc(dw[n].sem, 16)
        sch.op("gpsimd", _weights)
        for n in dw:
            dw[n].n = 16


        def _vconsts(eng):
            eng.memset(csb[:, 0:1], EPS)
            eng.memset(csb[:, 1:2], -0.5)
            eng.memset(csb[:, 2:3], 0.125)
            eng.memset(csb[:, 3:4], ESHIFT)
            return eng.memset(ones[:], 1.0)
        sch.op("vector", _vconsts, inc=vS)
        V_CONST = vS.n
        sch.op("vector", lambda e: e.tensor_scalar(
            scbrep[:], scbrep[:], 1.0 / 127.0, None, ALU.mult),
            waits=[(dsm, SMALLS)], inc=vS)
        V_SCB = vS.n

        def ln_stats_apply(par, src, waits_v, waits_s, out_tile, out_is_bf,
                           out_guard=(), apply_vis=True):
            sch.op("scalar", lambda e: e.activation(
                sqf[:], src[:, 0, :], AF.Square),
                waits=list(waits_s) + [(vS, sqf_free[0])])
            sch.op("scalar", lambda e: e.activation(
                sqf2[:], src[:, 1, :], AF.Square))
            sch.op("scalar", lambda e: e.drain(), inc=sS, vis=False)
            sqs = sS.n
            sq1 = sqs
            for t in range(2):
                sch.op("vector", lambda e, t=t: e.reduce_sum(
                    st_sum[par][:, t:t + 1], src[:, t, :], AX.X),
                    waits=waits_v if t == 0 else ())
            sch.op("vector", lambda e, par=par: e.reduce_sum(
                st_sq[par][:, 0:1], sqf[:], AX.X),
                waits=[(sS, sqs), (vS, last_ve[par])])
            sch.op("vector", lambda e, par=par: e.reduce_sum(
                st_sq[par][:, 1:2], sqf2[:], AX.X), inc=vS, vis=False)
            sqf_free[0] = vS.n
            sch.op("vector", lambda e: e.drain())
            sch.op("vector", lambda e: e.tensor_scalar(
                st_ve[par][:], st_sq[par][:], 1.0 / 1024.0, None, ALU.mult))
            sch.op("vector", lambda e: e.tensor_scalar(
                st_mean[par][:], st_sum[par][:], 1.0 / 1024.0, None, ALU.mult))
            sch.op("vector", lambda e: e.drain())
            sch.op("vector", lambda e: e.tensor_mul(
                st_m2[par][:], st_mean[par][:], st_mean[par][:]))
            sch.op("vector", lambda e: e.drain())
            sch.op("vector", lambda e: e.tensor_sub(
                st_ve[par][:], st_ve[par][:], st_m2[par][:]), inc=vS)
            ve_done = vS.n
            last_ve[par] = ve_done
            sch.op("scalar", lambda e: e.activation(
                st_m2[par][:], st_ve[par][:], AF.Ln, bias=EPS),
                waits=[(vS, ve_done), (vS, V_CONST)])
            sch.op("scalar", lambda e: e.activation(
                st_rstd[par][:], st_m2[par][:], AF.Exp, scale=-0.5))
            sch.op("scalar", lambda e: e.drain(), inc=sS, vis=False)
            rstd_done = sS.n
            av = []
            ot = out_tile if callable(out_tile) else (
                lambda t, o=out_tile: o[:, t, :])
            for t in range(2):
                sch.op("vector", lambda e, t=t: e.tensor_scalar(
                    ot(t), src[:, t, :], st_mean[par][:, t:t + 1],
                    None, ALU.subtract),
                    waits=list(out_guard) if t == 0 else ())
            for t in range(2):
                sch.op("vector", lambda e, t=t: e.tensor_scalar(
                    ot(t), ot(t),
                    st_rstd[par][:, t:t + 1], None, ALU.mult),
                    waits=[(sS, rstd_done)] if t == 0 else (), inc=vS,
                    vis=apply_vis)
                av.append(vS.n)
            return av, sq1

        def transposes(src_fn, dst, dst_col, groups, waits0):
            nonlocal pj_i
            for (s, fcs) in groups:
                b = pj_i % 4
                pj_i += 1
                gate = list(pj_gate[b])

                def fill(e, s=s, fcs=fcs, b=b):
                    last = None
                    for i, fc in enumerate(fcs):
                        last = e.matmul(pj[b][:, i * 128:(i + 1) * 128],
                                        src_fn(s, fc), idt[:],
                                        start=(i == 0), stop=(i == len(fcs) - 1))
                    return last
                w = gate + (waits0(s) if callable(waits0) else list(waits0))
                if first_tensor[0]:
                    w = w + [(dsm, SMALLS)]
                    first_tensor[0] = False
                sch.op("tensor", fill, waits=w, inc=tS)
                fill_done = tS.n
                c0 = dst_col(s)
                sch.op("scalar", lambda e, fcs=fcs, b=b, c0=c0: e.copy(
                    dst[:, fcs[0]:fcs[0] + len(fcs), c0:c0 + 128],
                    pj[b][:].rearrange("p (a n) -> p a n", a=len(fcs))),
                    waits=[(tS, fill_done)], inc=sS,
                    vis=((s, fcs) == groups[-1]))
                pj_gate[b] = [(sS, sS.n)]
            return sS.n

        def proj_fill(wname, nc0, rhs_ap, extra_waits, out_dst, vis=False):
            nonlocal pj_i
            b = pj_i % 4
            pj_i += 1
            gate = list(pj_gate[b])

            def fill(e, b=b):
                last = None
                for ni in range(2):
                    for kc in range(8):
                        lhs = wsb[wname][:, kc, (nc0 + ni) * 128:
                                         (nc0 + ni + 1) * 128]
                        rhs = rhs_ap(kc)
                        o = pj[b][:, ni * 256:(ni + 1) * 256]
                        last = e.matmul(o, lhs, rhs,
                                        start=(ni == 0 and kc == 0),
                                        stop=(ni == 1 and kc == 7))
                return last
            sch.op("tensor", fill, waits=gate + extra_waits, inc=tS)
            fd = tS.n
            sch.op("vector", lambda e, b=b: out_dst(e, pj[b]),
                   waits=[(tS, fd)], inc=vS, vis=vis)
            pj_gate[b] = [(vS, vS.n)]
            return vS.n

        def mm_fill(fill_builder, evac_builder, extra_waits, evac_eng="vector",
                    evac_ctr=None, vis=False):
            nonlocal pj_i
            b = pj_i % 4
            pj_i += 1
            gate = list(pj_gate[b])
            sch.op("tensor", lambda e, b=b: fill_builder(e, pj[b]),
                   waits=gate + extra_waits, inc=tS)
            fd = tS.n
            ec = evac_ctr or (vS if evac_eng == "vector" else sS)
            sch.op(evac_eng, lambda e, b=b: evac_builder(e, pj[b]),
                   waits=[(tS, fd)], inc=ec, vis=vis)
            pj_gate[b] = [(ec, ec.n)]
            return ec.n

        x_sync_extra = {}
        resadd_v = {}
        outadd_v = {}
        sq_reads_x = {}

        def sched_x_dma(j):
            slot = (j + 1) % 3
            waits = []
            if j - 3 >= -1 and (j - 3) in resadd_v:
                waits = [(vS, resadd_v[j - 3]), (sS, sq_reads_x[j - 3])]
            row0 = (j + 1) * 256
            src = x_h[row0:row0 + 256, :].rearrange("(a p) n -> p a n", p=128)
            sch.op("sync", lambda e, slot=slot: e.dma_start(
                out=xs[slot][:], in_=src), waits=waits, inc=dx[slot], inck=16)
            return dx[slot].n

        def sched_out_dma(j):
            dst = out_h[j * 256:(j + 1) * 256, :].rearrange(
                "(a p) n -> p a n", p=128)
            sch.op("sync", lambda e: e.dma_start(out=dst, in_=outt[:]),
                   waits=[(vS, outadd_v[j])], inc=dO, inck=16)

        probs_gate = [[], [], []]
        kvslot_read = {0: [], 1: [], 2: []}
        a_read = [[], []]
        at_read = []
        hln_read = []
        y1_read = []
        exp_count = {}
        pv_done = {}
        applies1 = {}
        ht1_done = {}
        proj_done_v = {}
        a_v = {}
        y1_evac = {}
        applies3 = {}
        x2_sq = {}
        ln4_sq = {}

        xval = sched_x_dma(-1)
        sq_reads_x[-1] = None
        applies_h, _ = ln_stats_apply(1, xs[0], [(dx[0], xval)],
                                      [(dx[0], xval)], hln, True)
        sq_reads_x[-1] = sS.n
        resadd_v[-1] = applies_h[1]
        TGROUPS = [(s, [g * 4 + i for i in range(4)]) for s in range(2)
                   for g in range(2)]
        ht_halo = transposes(
            lambda s, fc: hln[:, s, fc * 128:(fc + 1) * 128], hts,
            lambda s: s * 128, TGROUPS,
            lambda s: [(vS, applies_h[s])])
        hln_read.append(tS.n)
        for nc0 in (0, 2, 4, 6):
            proj_fill("wkt", nc0, lambda kc: hts[:, kc, :],
                      [(sS, ht_halo), (dw["wkt"], 16)],
                      lambda e, p, nc0=nc0: e.tensor_copy(
                          kts[2][:, nc0:nc0 + 2, :],
                          p[:].rearrange("p (a n) -> p a n", a=2)))
        for s in range(2):
            for nh in range(2):
                def fill(e, p, s=s, nh=nh):
                    last = None
                    for kc in range(8):
                        last = e.matmul(
                            p[:], hts[:, kc, s * 128:(s + 1) * 128],
                            wsb["wvt"][:, kc, nh * 512:(nh + 1) * 512],
                            start=(kc == 0), stop=(kc == 7))
                    return last

                def evac(e, p, s=s, nh=nh):
                    return e.tensor_copy(
                        vts[2][:, s, nh * 512:(nh + 1) * 512], p[:])
                mm_fill(fill, evac, [(sS, ht_halo), (dw["wvt"], 16)])

        xv = {-1: xval}
        for j in range(min(3, NB)):
            xv[j] = sched_x_dma(j)

        def hln3_out(t):
            return at_t[:, 4 * t:4 * t + 4, :].rearrange("p a n -> p (a n)")

        def hln3_src(s, fc):
            o = s * 1024 + fc * 128
            return at_t[:, o // 256, o % 256:o % 256 + 128]

        def seg_A(j):
            xslot = (j + 1) % 3
            w = [(dx[xslot], xv[j])]
            hg = [(tS, hln_read[-1])] if hln_read else []
            av, _ = ln_stats_apply(j % 2, xs[xslot], list(w), list(w), hln,
                                   True, out_guard=hg)
            applies1[j] = av
            sq_reads_x[j] = sS.n

        def seg_B(j):
            ht1_done[j] = transposes(
                lambda s, fc: hln[:, s, fc * 128:(fc + 1) * 128], hts,
                lambda s: s * 128, TGROUPS,
                lambda s, j=j: [(vS, applies1[j][s])])
            hln_read.append(tS.n)

        def seg_C(j):
            kslot = j % 3
            kv_gate = list(kvslot_read[kslot])
            ht = ht1_done[j]
            for nc0 in (0, 2, 4, 6):
                proj_fill("wqt", nc0, lambda kc: hts[:, kc, :],
                          [(sS, ht)] + ([(dw["wqt"], 16)] if j == 0 else []),
                          lambda e, p, nc0=nc0: e.tensor_copy(
                              qt_t[:, nc0:nc0 + 2, :],
                              p[:].rearrange("p (a n) -> p a n", a=2)))
            for nc0 in (0, 2, 4, 6):
                proj_fill("wkt", nc0, lambda kc: hts[:, kc, :],
                          [(sS, ht)] + kv_gate +
                          ([(dw["wkt"], 16)] if j == 0 else []),
                          lambda e, p, nc0=nc0, kslot=kslot: e.tensor_copy(
                              kts[kslot][:, nc0:nc0 + 2, :],
                              p[:].rearrange("p (a n) -> p a n", a=2)))
            for s in range(2):
                for nh in range(2):
                    def fillv(e, p, s=s, nh=nh):
                        last = None
                        for kc in range(8):
                            last = e.matmul(
                                p[:], hts[:, kc, s * 128:(s + 1) * 128],
                                wsb["wvt"][:, kc, nh * 512:(nh + 1) * 512],
                                start=(kc == 0), stop=(kc == 7))
                        return last

                    def evacv(e, p, s=s, nh=nh, kslot=kslot):
                        return e.tensor_copy(
                            vts[kslot][:, s, nh * 512:(nh + 1) * 512], p[:])
                    mm_fill(fillv, evacv, [(sS, ht)] + kv_gate +
                            ([(dw["wvt"], 16)] if j == 0 else []),
                            vis=(s == 1 and nh == 1))
            kvslot_read[kslot] = []
            proj_done_v[j] = vS.n

        def kt_of(k):
            ob = (k >> 1) - 1
            return ob % 3, k % 2

        sc_i = [0]

        def seg_D(j):
            for qrel in range(2):
                qq = 2 * j + qrel
                for c in range(3):
                    k = qq + c
                    kslot_c, half = kt_of(k)
                    pslot = c
                    pgate = list(probs_gate[pslot])
                    for u4 in range(4):
                        g, par_u = u4 >> 1, u4 & 1
                        h0 = 8 * g + par_u
                        bi = sc_i[0] % 3
                        sc_i[0] += 1
                        gate = list(sc_gate[bi])
                        mname = None
                        if c == 0:
                            mname = ("m_q0_c0" if qq <= 1 else "m_c0")
                        elif c == 1 and qq == 0:
                            mname = "m_q0_c1"
                        elif c == 2:
                            mname = "m_c2"

                        def fill(e, h0=h0, par_u=par_u, kslot_c=kslot_c,
                                 half=half, qrel=qrel, bi=bi, mname=mname):
                            last = None
                            po = 64 * par_u
                            for i in range(4):
                                h = h0 + 2 * i
                                lhs = kts[kslot_c][po:po + 64, h // 2,
                                                   half * 128:half * 128 + 128]
                                rhs = qt_t[po:po + 64, h // 2,
                                           qrel * 128:qrel * 128 + 128]
                                last = e.matmul(
                                    sc[bi][:, i * 128:(i + 1) * 128],
                                    lhs, rhs, start=(i == 0),
                                    stop=(i == 3 and mname is None))
                            if mname is not None:
                                last = e.matmul(sc[bi][:], idt[:],
                                                msk[mname][:],
                                                start=False, stop=True)
                            return last
                        w_extra = ([(vS, proj_done_v[j])]
                                   if (qrel == 0 and c == 0 and u4 == 0)
                                   else [])
                        sch.op("tensor", fill, waits=gate + w_extra, inc=tS)
                        f_done = tS.n

                        def expf(e, bi=bi, pslot=pslot, h0=h0, g=g):
                            return e.activation(
                                probs[pslot][:, h0:8 * g + 8:2, :],
                                sc[bi][:].rearrange("p (a n) -> p a n", a=4),
                                AF.Exp, bias=ESHIFT, scale=0.125)
                        sch.op("scalar", expf,
                               waits=[(tS, f_done), (vS, V_CONST)] + pgate,
                               inc=sS, vis=(c == 2 and u4 == 3))
                        exp_count[(qq, c, u4)] = sS.n
                        sc_gate[bi] = [(sS, sS.n)]
                        pgate = []

                pv_w = (list(pj_gate[2]) + list(pj_gate[3]) +
                        list(pvl_gate) +
                        [(sS, exp_count[(qq, 2, 3)]), (vS, V_CONST)])

                def pvfill(e, qq=qq):
                    last = None
                    for c in range(3):
                        k = qq + c
                        kslot_c, half = kt_of(k)
                        for h in range(16):
                            lhs = probs[c][:, h, :]
                            rhs = vts[kslot_c][:, half, h * 64:(h + 1) * 64]
                            last = e.matmul(
                                pv[h // 8][:, (h % 8) * 64:(h % 8 + 1) * 64],
                                lhs, rhs, start=(c == 0 and h % 8 == 0),
                                stop=(c == 2 and (h % 8) == 7))
                            last = e.matmul(
                                pvl[:, h:h + 1], lhs, ones[:],
                                start=(c == 0 and h == 0),
                                stop=(c == 2 and h == 15))
                    return last
                sch.op("tensor", pvfill, waits=pv_w, inc=tS)
                pv_done[qq] = tS.n
                for c in range(3):
                    probs_gate[c] = [(tS, pv_done[qq])]
                for b in range(3):
                    kvslot_read[kt_of(qq + b)[0]] = [(tS, pv_done[qq])]

                agate = list(a_read[qrel])

                def norm(e, qrel=qrel):
                    e.reciprocal(recip[:], pvl[:])
                    e.drain()
                    last = None
                    for h in range(16):
                        last = e.tensor_scalar(
                            a_t[qrel][:, h * 64:(h + 1) * 64],
                            pv[h // 8][:, (h % 8) * 64:(h % 8 + 1) * 64],
                            recip[:, h:h + 1], None, ALU.mult)
                    return last
                sch.op("vector", norm, waits=[(tS, pv_done[qq])] + agate,
                       inc=vS)
                pj_gate[2] = [(vS, vS.n)]
                pj_gate[3] = [(vS, vS.n)]
                pvl_gate.clear()
                pvl_gate.append((vS, vS.n))
                a_read[qrel] = []
                a_v.setdefault(j, [None, None])[qrel] = vS.n

        def seg_E(j):
            atgate = [(tS, at_read[-1])] if at_read else []
            at_done = transposes(
                lambda s, fc: a_t[s][:, fc * 128:(fc + 1) * 128], at_t,
                lambda s: s * 128, TGROUPS,
                lambda s, j=j: [(vS, a_v[j][s])] + atgate)
            for s in range(2):
                a_read[s] = [(tS, tS.n)]
            y1gate = list(y1_read)
            del y1_read[:]
            for s in range(2):
                for nh in range(2):
                    def fillo(e, p, s=s, nh=nh):
                        last = None
                        for kc in range(8):
                            last = e.matmul(
                                p[:], at_t[:, kc, s * 128:(s + 1) * 128],
                                wsb["wot"][:, kc, nh * 512:(nh + 1) * 512],
                                start=(kc == 0), stop=(kc == 7))
                        return last

                    def evaco(e, p, s=s, nh=nh):
                        return e.copy(
                            y1sb[:, s, nh * 512:(nh + 1) * 512], p[:])
                    mm_fill(fillo, evaco,
                            [(sS, at_done)] + y1gate +
                            ([(dw["wot"], 16)] if j == 0 else []),
                            evac_eng="scalar", vis=(s == 1 and nh == 1))
                    y1gate = []
            at_read.append(tS.n)
            y1_evac[j] = sS.n

        def seg_F(j):
            par = j % 2
            g_guard = ([(sS, ln4_sq[j - 1]), (dO, 16 * j)]
                       if j > 0 else [])
            ln_stats_apply(par, y1sb, [(sS, y1_evac[j])], [], gsb, False,
                           out_guard=g_guard, apply_vis=False)
            xslot = (j + 1) % 3
            x_guard = [(sS, x2_sq[j - 1])] if j > 0 else []
            for t in range(2):
                sch.op("vector", lambda e, t=t, xslot=xslot: e.tensor_add(
                    x2[:, t, :], xs[xslot][:, t, :], gsb[:, t, :]),
                    waits=x_guard if t == 0 else (), inc=vS, vis=(t == 1))
            resadd_v[j] = vS.n
            if j + 3 < NB:
                xv[j + 3] = sched_x_dma(j + 3)
            av3, sq3 = ln_stats_apply(par, x2, [], [(vS, resadd_v[j])],
                                      hln3_out, True,
                                      out_guard=[(tS, at_read[-1])])
            applies3[j] = av3
            x2_sq[j] = sq3

        def seg_G(j):
            h2t_done = transposes(
                hln3_src, hts2, lambda s: s * 128, TGROUPS,
                lambda s, j=j: [(vS, applies3[j][s])])
            for s in range(2):
                for nh in range(2):
                    def fillf(e, p, s=s, nh=nh):
                        last = None
                        for kc in range(8):
                            last = e.matmul(
                                p[:], hts2[:, kc, s * 128:(s + 1) * 128],
                                wsb["cbt"][:, kc, nh * 512:(nh + 1) * 512],
                                start=(kc == 0), stop=(kc == 7))
                        return last

                    def evacf(e, p, s=s, nh=nh):
                        return e.tensor_mul(
                            gsb[:, s, nh * 512:(nh + 1) * 512], p[:],
                            scbrep[:, nh * 512:(nh + 1) * 512])
                    mm_fill(fillf, evacf,
                            [(sS, h2t_done), (dsm, SMALLS)] +
                            ([(dw["cbt"], 16)] if j == 0 else []))

        def seg_H(j):
            par = j % 2
            for t in range(2):
                sch.op("vector", lambda e, t=t: e.tensor_add(
                    gsb[:, t, :], gsb[:, t, :], biasrep[:]),
                    waits=[(dsm, SMALLS)] if (j == 0 and t == 0) else (),
                    inc=vS, vis=(t == 1))
            g_ready_v = vS.n
            _, sq4 = ln_stats_apply(par, gsb, [], [(vS, g_ready_v)], y1sb,
                                    False, apply_vis=False)
            ln4_sq[j] = sq4
            outw = ([(dO, 16 * j)] if j > 0 else []) + [(sS, ln4_sq[j])]
            for t in range(2):
                sch.op("vector", lambda e, t=t: e.tensor_add(
                    outt[:, t, :], x2[:, t, :], y1sb[:, t, :]),
                    waits=outw if t == 0 else (), inc=vS, vis=(t == 1))
            outadd_v[j] = vS.n
            y1_read.append((vS, vS.n))
            sched_out_dma(j)

        seg_A(0)
        seg_B(0)
        seg_C(0)
        for j in range(NB):
            seg_D(j)
            if j + 1 < NB:
                seg_A(j + 1)
            seg_E(j)
            seg_F(j)
            if j + 1 < NB:
                seg_B(j + 1)
                seg_C(j + 1)
            seg_G(j)
            seg_H(j)

        if DBG_PHASE < 99:
            tv, vv, sv2 = tS.n, vS.n, sS.n

            def _cleanup(e):
                for n in dw:
                    e.wait_ge(dw[n].sem, 16)
                e.wait_ge(dsm.sem, SMALLS)
                for i in range(3):
                    if dx[i].n:
                        e.wait_ge(dx[i].sem, dx[i].n)
                if tv:
                    e.wait_ge(tS.sem, tv)
                if vv:
                    e.wait_ge(vS.sem, vv)
                if sv2:
                    e.wait_ge(sS.sem, sv2)
                for jj in range(NB):
                    dst = out_h[jj * 256:(jj + 1) * 256, :].rearrange(
                        "(a p) n -> p a n", p=128)
                    e.dma_start(out=dst, in_=outt[:]).then_inc(dO.sem, 16)
            sch.op("sync", _cleanup)
            dO.n = 16 * NB

        if DBG_DUMP:
            dbg_src = {"d_hln": hln, "d_hts": hts, "d_qt": qt_t,
                       "d_kt0": kts[0], "d_vt0": vts[0], "d_kt2": kts[2],
                       "d_vt2": vts[2], "d_pb0": probs[0], "d_pb1": probs[1],
                       "d_pb2": probs[2], "d_a0": a_t[0], "d_a1": a_t[1],
                       "d_at": at_t, "d_y1": y1sb, "d_g": gsb, "d_x2": x2}
            tv, vv, sv2 = tS.n, vS.n, sS.n
            dov = dO.n
            dcvt = outt[:].rearrange("p a n -> p (a n)")

            def _dump(e):
                e.wait_ge(tS.sem, tv)
                e.wait_ge(vS.sem, vv)
                e.wait_ge(sS.sem, sv2)
                e.wait_ge(dO.sem, dov)
                e.tensor_copy(dcvt[:, 0:2], st_sum[0][:])
                e.tensor_copy(dcvt[:, 2:4], st_sq[0][:])
                e.tensor_copy(dcvt[:, 4:6], st_mean[0][:])
                e.tensor_copy(dcvt[:, 6:8], st_ve[0][:])
                e.tensor_copy(dcvt[:, 8:10], st_rstd[0][:])
                return e.engine_nop()
            sch.op("vector", _dump, inc=vS)
            sch.op("sync", lambda e, v=vS.n: e.wait_ge(vS.sem, v))
            sch.op("sync", lambda e: e.dma_start(
                out=dbg_h["d_sums"][:], in_=dcvt[:, 0:16]).then_inc(dO.sem, 16))
            dO.n += 16
            dcur0 = dO.n
            sch.op("vector", lambda e, d=dcur0: e.wait_ge(dO.sem, d))
            for n in ["d_hln", "d_hts", "d_qt", "d_kt0", "d_vt0", "d_kt2",
                      "d_vt2", "d_pb0", "d_pb1", "d_pb2", "d_a0", "d_a1",
                      "d_at", "d_y1", "d_g", "d_x2"]:
                srcap = dbg_src[n]
                nel = 1024 if n in ("d_a0", "d_a1") else 2048

                def _cvt(e, srcap=srcap, nel=nel):
                    flat = srcap[:].rearrange("p ... -> p (...)") \
                        if srcap[:].ndim > 2 else srcap[:]
                    return e.tensor_copy(dcvt[:, 0:nel], flat)
                sch.op("vector", _cvt, inc=vS)
                vcur = vS.n

                def _dma(e, n=n, nel=nel, vcur=vcur):
                    e.wait_ge(vS.sem, vcur)
                    return e.dma_start(out=dbg_h[n][:, 0:nel],
                                       in_=dcvt[:, 0:nel]).then_inc(dO.sem, 16)
                sch.op("sync", _dma)
                dO.n += 16
                dcur = dO.n
                sch.op("vector", lambda e, dcur=dcur: e.wait_ge(dO.sem, dcur))

        sch.op("sync", lambda e: e.wait_ge(dO.sem, dO.n if DBG_DUMP else 16 * NB))

        block = ctx.enter_context(nc.Block())

        @block.sync
        def _(eng):
            sch.emit("sync", eng)

        @block.gpsimd
        def _(eng):
            sch.emit("gpsimd", eng)

        @block.vector
        def _(eng):
            sch.emit("vector", eng)

        @block.scalar
        def _(eng):
            sch.emit("scalar", eng)

        @block.tensor
        def _(eng):
            sch.emit("tensor", eng)

    return nc


_NC_CACHE = {}


def _get_program():
    if "nc" not in _NC_CACHE:
        _NC_CACHE["nc"] = build_program()
    return _NC_CACHE["nc"]


def _masks_for_core(core):
    i = np.arange(128)
    m = np.arange(128)[:, None]
    c0 = np.where(m > i[None, :], 0.0, MASKVAL).astype(np.float32)
    c2 = np.where(m <= i[None, :], 0.0, MASKVAL).astype(np.float32)
    full = np.full((128, 128), MASKVAL, np.float32)
    zero = np.zeros((128, 128), np.float32)
    if core == 0:
        q0c0, q0c1, q1c0 = full, full, full
    else:
        q0c0, q0c1, q1c0 = c0, zero, c0
    tile = lambda t: np.tile(t, (1, 4)).astype(ml_dtypes.bfloat16)
    return {"m_c0": tile(c0), "m_c2": tile(c2), "m_q0_c0": tile(q0c0),
            "m_q0_c1": tile(q0c1)}


def kernel(L0_attn_x, L0_attn_wq, L0_attn_wk, L0_attn_wv, L0_attn_wo,
           L0_ffn_CB, L0_ffn_SCB, L0_ffn_bias):
    x = np.asarray(L0_attn_x, np.float32).reshape(16384, 1024)
    wts = {"wqt": L0_attn_wq, "wkt": L0_attn_wk, "wvt": L0_attn_wv,
           "wot": L0_attn_wo, "cbt": L0_ffn_CB}
    wts = {n: np.ascontiguousarray(np.asarray(w, np.float32).T)
           for n, w in wts.items()}
    scb = np.tile(np.asarray(L0_ffn_SCB, np.float32).reshape(1, 1024),
                  (128, 1))
    bias = np.tile(np.asarray(L0_ffn_bias, np.float32).reshape(1, 1024),
                   (128, 1))
    ident = np.eye(128, dtype=ml_dtypes.bfloat16)

    nc = _get_program()
    in_maps = []
    for c in range(8):
        lo = c * 2048
        xs = np.empty((2304, 1024), np.float32)
        if c == 0:
            xs[0:256] = 0.0
        else:
            xs[0:256] = x[lo - 256:lo]
        xs[256:] = x[lo:lo + 2048]
        m = {"x": xs, "scb": scb, "bias": bias, "ident": ident}
        m.update(wts)
        m.update(_masks_for_core(c))
        in_maps.append(m)

    res = run_bass_kernel_spmd(nc, in_maps, core_ids=list(range(8)))
    out = np.concatenate([res.results[c]["out"] for c in range(8)], axis=0)
    return out.reshape(1, 16384, 1024)
